# version 48
# baseline (speedup 1.0000x reference)
from contextlib import ExitStack

import numpy as np

import concourse.bass as bass
import concourse.mybir as mybir
from concourse.bass_utils import run_bass_kernel_spmd

F32 = mybir.dt.float32
BF16 = mybir.dt.bfloat16
I32 = mybir.dt.int32
AF = mybir.ActivationFunctionType
OP = mybir.AluOpType

FULL_CFG = dict(L=8192, D=1024, DI=2048, NS=16, DR=64, KC=4, H=3072,
                CTA=512, CTB=1024, CTC=512, EPS=1e-5, QSCALE=1.0 / 127.0)


def _dv(cfg):
    d = dict(cfg)
    d["HDD"] = cfg["DI"] // 2
    d["NDT"] = d["HDD"] // 128
    d["NXT"] = cfg["D"] // 128
    d["XD"] = cfg["DR"] + 2 * cfg["NS"]
    d["MYT"] = cfg["L"] // 2
    d["NCHA"] = cfg["L"] // cfg["CTA"]
    d["NCHB"] = cfg["L"] // cfg["CTB"]
    d["NCHC"] = d["MYT"] // cfg["CTC"]
    d["NH"] = cfg["H"] // 128
    d["NDI"] = cfg["DI"] // 128
    d["TSB"] = min(512, cfg["CTB"])
    d["NTB"] = cfg["CTB"] // d["TSB"]
    assert d["XD"] <= 128 and cfg["DR"] <= 128
    assert cfg["CTA"] <= 512 and cfg["CTC"] <= 512
    assert d["NCHB"] % 2 == 0
    return d


class K:

    ENGS = ("sync", "gpsimd", "vector", "scalar", "tensor")
    OWN = {"vector": "cV", "scalar": "cS", "gpsimd": "cG"}

    def __init__(self, nc):
        self.nc = nc
        self.ops = {e: [] for e in self.ENGS}
        self.stack = ExitStack()
        self.sem = {}
        self.cnt = {}
        self.waited = {e: {} for e in self.ENGS}
        self.psum_free = {}

    def mksem(self, *names):
        for n in names:
            self.sem[n] = self.stack.enter_context(self.nc.semaphore(n))
            self.cnt[n] = 0

    def wait(self, eng, semname, val):
        if val <= 0:
            return
        if self.waited[eng].get(semname, -1) >= val:
            return
        self.waited[eng][semname] = val
        s = self.sem[semname]
        self.ops[eng].append(lambda e, s=s, v=val: e.wait_ge(s, v))

    def op(self, eng, semname, build, inc=1, nb=False):
        own = self.OWN.get(eng)
        if not nb and own is not None and eng in ("vector", "scalar", "gpsimd"):
            self.wait(eng, own, self.cnt[own])
        self.cnt[semname] += inc
        s = self.sem[semname]
        self.ops[eng].append(lambda e, b=build, s=s, i=inc: b(e).then_inc(s, i))
        return self.cnt[semname]

    def psum_guard(self, idx):
        for sname, v in self.psum_free.get(idx, ()):
            self.wait("tensor", sname, v)

    def psum_set_free(self, idx, *pairs):
        self.psum_free[idx] = pairs

    def barrier(self, exclude=("cc",)):
        snap = dict(self.cnt)
        for e in self.ENGS:
            for sname, v in snap.items():
                if sname in exclude:
                    continue
                self.wait(e, sname, v)

    def play(self):
        nc = self.nc
        with nc.Block() as block:
            @block.sync
            def _(e):
                for f in self.ops["sync"]:
                    f(e)

            @block.gpsimd
            def _(e):
                for f in self.ops["gpsimd"]:
                    f(e)

            @block.vector
            def _(e):
                for f in self.ops["vector"]:
                    f(e)

            @block.scalar
            def _(e):
                for f in self.ops["scalar"]:
                    f(e)

            @block.tensor
            def _(e):
                for f in self.ops["tensor"]:
                    f(e)


class Arena:

    def __init__(self, k, name, dtype, n):
        self.t = k.stack.enter_context(k.nc.sbuf_tensor(name, [128, n], dtype))
        self.n = n
        self.off = 0

    def alloc(self, shape):
        p = shape[0]
        cols = 1
        for s in shape[1:]:
            cols *= s
        off = (self.off + 31) & ~31
        ap = self.t[0:p, off:off + cols]
        self.off = off + cols
        assert self.off <= self.n, (self.off, self.n, shape)
        if len(shape) > 2:
            names = " ".join(f"d{i}" for i in range(1, len(shape)))
            kw = {f"d{i}": shape[i] for i in range(1, len(shape))}
            ap = ap.rearrange(f"p ({names}) -> p {names}", **kw)
        return ap

    def reset(self):
        self.off = 0


def build_nc(cfg):
    g = _dv(cfg)
    L, D, DI, NS, DR, KC, H = (g[k] for k in ("L", "D", "DI", "NS", "DR", "KC", "H"))
    CTA, CTB, CTC, EPS, QS = (g[k] for k in ("CTA", "CTB", "CTC", "EPS", "QSCALE"))
    HDD, NDT, NXT, XD, MYT = (g[k] for k in ("HDD", "NDT", "NXT", "XD", "MYT"))
    NCHA, NCHB, NCHC, NH, NDI = (g[k] for k in ("NCHA", "NCHB", "NCHC", "NH", "NDI"))
    TSB, NTB = g["TSB"], g["NTB"]

    nc = bass.Bass("TRN2", target_bir_lowering=False)
    k = K(nc)
    k.mksem("qS", "qG", "cV", "cS", "cT", "cG", "cc", "q0", "q1", "g0", "g1", "g2", "g3", "q2", "q3", "qW", "qL", "qY", "qYs", "qU", "qX", "qA", "cx0", "cx1", "cy0", "cy1", "qi0", "qi1", "qf0", "qf1", "qM", "cy2", "cy3")

    din = lambda n, s, t: nc.dram_tensor(n, list(s), t, kind="ExternalInput").ap()
    dint = lambda n, s, t: nc.dram_tensor(n, list(s), t, kind="Internal").ap()

    xT = din("xT", (D, L), F32)
    xTmine = din("xTmine", (D, MYT), F32)
    w_xin = din("w_xin", (D, HDD), F32)
    w_res = din("w_res", (D, DI), F32)
    w_x = din("w_x", (128, NDT, XD), F32)
    w_dt = din("w_dt", (DR, HDD), F32)
    b_dt = din("b_dt", (128, NDT), F32)
    a_log = din("a_log", (128, NDT, NS), F32)
    d_vecI = din("d_vec", (128, NDT), F32)
    conv_wI = din("conv_w", (128, NDT, KC), F32)
    conv_bI = din("conv_b", (128, NDT), F32)
    w_out = din("w_out", (DI, D), F32)
    m_sel = din("m_sel", (128, 2), F32)
    identI = din("ident", (128, 128), F32)
    wg_t = din("wg_t", (NH, 128, NXT, 128), I32)
    wu_t = din("wu_t", (NH, 128, NXT, 128), I32)
    wd_t = din("wd_t", (H, D), I32)
    outT = nc.dram_tensor("outT", [D, MYT], F32, kind="ExternalOutput").ap()

    d_u = dint("d_u", (HDD, L), BF16)
    NARP = g["NCHB"]
    d_xdblp_p = [dint(f"d_xdblp{p}", (XD, cfg["CTB"]), F32) for p in range(NARP)]
    d_xdbl_p = [dint(f"d_xdbl{p}", (XD, cfg["CTB"]), F32) for p in range(NARP)]
    d_bc = dint("d_bc", (2 * NS, L), BF16)
    NAGP = g["NCHB"] // 2
    d_y_p = [dint(f"d_y{p}", (2, HDD, cfg["CTB"]), BF16) for p in range(NAGP)]
    d_yg_p = [dint(f"d_yg{p}", (2, 2, HDD, cfg["CTB"]), BF16) for p in range(NAGP)]
    d_wres = dint("d_wres", (D, DI), BF16)
    d_wout = dint("d_wout", (DI, D), BF16)
    d_wd = dint("d_wd", (H, D), BF16)
    d_ln = dint("d_ln", (2, CTC), BF16)
    d_x1 = dint("d_x1", (D, MYT), BF16)
    d_h = dint("d_h", (cfg["H"] // 128, 128, MYT), BF16)

    RG = [[0, 1], [2, 3], [4, 5], [6, 7]]

    psum = [k.stack.enter_context(nc.psum_tensor(f"pb{i}", [128, 512], F32))
            for i in range(8)]

    sb = k.stack.enter_context

    s_wx = sb(nc.sbuf_tensor("s_wx", [128, NDT, XD], BF16))
    s_wdt = sb(nc.sbuf_tensor("s_wdt", [DR, HDD], BF16))
    s_bdt = sb(nc.sbuf_tensor("s_bdt", [128, NDT], F32))
    s_A = sb(nc.sbuf_tensor("s_A", [128, NDT, NS], F32))
    s_D = sb(nc.sbuf_tensor("s_D", [128, NDT], F32))
    s_cw = sb(nc.sbuf_tensor("s_cw", [128, NDT, KC], F32))
    s_cb = sb(nc.sbuf_tensor("s_cb", [128, NDT], F32))
    s_carry = sb(nc.sbuf_tensor("s_carry", [128, NDT, NS], F32))
    s_ones = sb(nc.sbuf_tensor("s_ones", [128, 1], BF16))
    s_wxin = sb(nc.sbuf_tensor("s_wxin", [128, NXT, HDD], BF16))
    s_msel = sb(nc.sbuf_tensor("s_msel", [128, 2], F32))
    s_eps = sb(nc.sbuf_tensor("s_eps", [128, 1], F32))
    s_id = sb(nc.sbuf_tensor("s_id", [128, 128], BF16))

    WSLA = min(6, max(NDI, NXT)) * 128
    WTW = max(HDD, D, NXT * 128)
    f32_sz = max(2 * WTW + NDT * NS
                 + NXT * CTA + CTA + 7 * CTB,
                 2 * NXT * CTC + 4 * CTC) + 64 * 14
    b16_sz = max(2 * WTW
                 + (NXT + NDT + 2) * CTA + NDT * (KC - 1 + CTA) + (11 + 4 * NDT) * CTB,
                 (NXT + 5) * CTC
                 + max(4 * WSLA + (3 * NXT + 2 * NDI + 4 * NDT) * CTC,
                       (NXT + 2) * MYT + 2 * CTC + 8 * NXT * 128,
                       4 * WSLA + (3 * NXT + 2 * NH) * CTC)) + 64 * 24
    AF32 = Arena(k, "ar_f32", F32, f32_sz)
    AB16 = Arena(k, "ar_b16", BF16, b16_sz)
    AI32 = Arena(k, "ar_i32", I32, 2 * 1024 + 64)

    r128 = lambda ap: ap.rearrange("(t p) x -> p t x", p=128)

    if True:
        w_tmpf = AF32.alloc([128, 2, WTW])
        w_tmpi = AI32.alloc([128, 2, 1024])
        w_tmpb = AB16.alloc([128, 2, WTW])
        s_alog = AF32.alloc([128, NDT, NS])
        s_wxf = w_tmpf[:, 0, 0:NDT * XD].rearrange("p (t x) -> p t x", t=NDT)
        s_wdtf = w_tmpf[0:DR, 1, 0:HDD]

        def ld(dst_ap, src_ap):
            k.op("sync", "qS", lambda e, d=dst_ap, s=src_ap: e.dma_start(out=d, in_=s), 16)

        ld(s_alog, a_log)
        ld(s_bdt[:], b_dt)
        ld(s_D[:], d_vecI)
        ld(s_cw[:], conv_wI)
        ld(s_cb[:], conv_bI)
        ld(s_wxf, w_x.rearrange("p t x -> p (t x)").rearrange(
            "p (t x) -> p t x", t=NDT))
        ld(s_wdtf, w_dt)
        ld(s_msel[:], m_sel)
        vq = k.cnt["qS"]
        k.wait("vector", "qS", vq)
        k.wait("scalar", "qS", vq)
        k.op("scalar", "cS", lambda e: e.activation(out=s_A[:], in_=s_alog[:], func=AF.Exp))
        k.wait("vector", "cS", k.cnt["cS"])
        k.op("vector", "cV", lambda e: e.tensor_scalar_mul(s_A[:], s_A[:], -1.0))
        k.op("vector", "cV", lambda e: e.tensor_copy(s_wx[:], s_wxf))
        k.op("vector", "cV", lambda e: e.tensor_copy(s_wdt[:], s_wdtf))
        k.op("vector", "cV", lambda e: e.memset(s_carry[:], 0.0))
        k.op("vector", "cV", lambda e: e.memset(s_ones[:], 1.0))
        k.op("vector", "cV", lambda e: e.memset(s_eps[:], EPS))
        wvc = k.cnt["cV"]

        k.wait("sync", "cV", wvc)
        k.op("sync", "qS", lambda e: e.dma_start(
            out=w_tmpf[:, 0, 0:128], in_=identI), 16)
        k.wait("vector", "qS", k.cnt["qS"])
        k.op("vector", "cV", lambda e: e.tensor_copy(s_id[:], w_tmpf[:, 0, 0:128]))

        wxq = {0: k.cnt["cV"], 1: wvc}
        for i in range(D // 128):
            par = i % 2
            tin = w_tmpf[:, par, 0:HDD]
            qn = f"q{2 + par}"
            k.wait("sync", "cV", wxq[par])
            k.op("sync", qn, lambda e, t=tin, i=i: e.dma_start(
                out=t, in_=w_xin[i * 128:(i + 1) * 128, :]), 16)
            k.wait("vector", qn, k.cnt[qn])
            k.op("vector", "cV", lambda e, i=i, t=tin: e.tensor_copy(
                s_wxin[:, i, :], t))
            wxq[par] = k.cnt["cV"]

        w_tiles = []
        for hh in range(2):
            for i in range(D // 128):
                w_tiles.append((w_res[i * 128:(i + 1) * 128,
                                      hh * (DI // 2):(hh + 1) * (DI // 2)],
                                d_wres[i * 128:(i + 1) * 128,
                                       hh * (DI // 2):(hh + 1) * (DI // 2)],
                                DI // 2, False))
        for i in range(DI // 128):
            w_tiles.append((w_out[i * 128:(i + 1) * 128, :],
                            d_wout[i * 128:(i + 1) * 128, :], D, False))
        for i in range(H // 128):
            w_tiles.append((wd_t[i * 128:(i + 1) * 128, :],
                            d_wd[i * 128:(i + 1) * 128, :], D, True))
        wst = dict(t=0, iss=0, q={}, store={0: 0, 1: 0}, slot={})

        def issue_w_load(t):
            if t >= len(w_tiles):
                return
            src, dst, ncols, is_int = w_tiles[t]
            par = t % 2
            tin = (w_tmpi if is_int else w_tmpf)[:, par, 0:ncols]
            qn = f"qi{par}" if is_int else f"qf{par}"
            if not is_int and not wst.get(("fgate", par)):
                k.wait("scalar", "cV", wxq[par])
                wst[("fgate", par)] = True
            k.wait("scalar", "cS", wst["slot"].get((is_int, par), 0))
            k.op("scalar", qn, lambda e, t_=tin, s=src: e.dma_start(
                out=t_, in_=s), 16, nb=True)
            wst["q"][t] = (qn, k.cnt[qn])

        def emit_w_tiles(n):
            while wst["iss"] < min(wst["t"] + 2, len(w_tiles)):
                issue_w_load(wst["iss"])
                wst["iss"] += 1
            for _ in range(n):
                if wst["t"] >= len(w_tiles):
                    return
                t = wst["t"]
                wst["t"] += 1
                src, dst, ncols, is_int = w_tiles[t]
                par = t % 2
                tin = (w_tmpi if is_int else w_tmpf)[:, par, 0:ncols]
                tout = w_tmpb[:, par, 0:ncols]
                gn = f"g{par}"
                qn, qv = wst["q"][t]
                k.wait("scalar", qn, qv)
                k.wait("scalar", gn, wst["store"][par])
                k.op("scalar", "cS", lambda e, o=tout, t_=tin: e.activation(
                    out=o, in_=t_, func=AF.Copy), nb=True)
                wst["slot"][(is_int, par)] = k.cnt["cS"]
                k.wait("gpsimd", "cS", k.cnt["cS"])
                k.op("gpsimd", gn, lambda e, d=dst, t_=tout: e.dma_start(
                    out=d, in_=t_), 16)
                wst["store"][par] = k.cnt[gn]
                if wst["iss"] < len(w_tiles):
                    issue_w_load(wst["iss"])
                    wst["iss"] += 1

        NWT = len(w_tiles)

    k.barrier()

    if True:
        a_x = AF32.alloc([128, NXT, CTA])
        a_xb = AB16.alloc([128, NXT, CTA])
        a_xin = AB16.alloc([128, NDT, KC - 1 + CTA])
        a_conv = AB16.alloc([128, 2, CTA])
        a_u = AB16.alloc([128, NDT, CTA])
        a_xdbl = AF32.alloc([XD, CTA])
        b_xdbl = AF32.alloc([XD, CTB])
        b_dtb = AB16.alloc([DR, CTB])
        b_r = AB16.alloc([128, CTB])
        b_a = AF32.alloc([128, CTB])
        b_l = AF32.alloc([128, CTB])
        b_dbf = AB16.alloc([128, NDT, CTB])
        b_du = AB16.alloc([128, NDT, CTB])
        b_u = AB16.alloc([128, NDT, CTB])
        b_bcb = AB16.alloc([2 * NS, CTB])
        b_brep = AB16.alloc([128, 2, CTB])
        b_crep = AB16.alloc([128, 2, CTB])
        b_dA = AF32.alloc([128, 4, CTB])
        b_h = AB16.alloc([128, 2, CTB])
        b_tmp = AB16.alloc([128, 2, CTB])
        b_y = AB16.alloc([128, NDT, CTB])
        xT_t = xT.rearrange("(t p) l -> p t l", p=128)
        d_u_t = d_u.rearrange("(t p) l -> p t l", p=128)

        ast = dict(xb_v=0, evac_s=0, conv_v=0, wx_t=0, ustore=[], xstore=[])

        def emit_A(c):
            lo = c * CTA
            evac_m = {}
            k.wait("sync", "cV", ast["xb_v"])
            k.op("sync", "qA",
                 lambda e, lo=lo: e.dma_start(out=a_x, in_=xT_t[:, :, lo:lo + CTA]), 16)
            k.wait("vector", "qA", k.cnt["qA"])
            k.wait("vector", "cS", ast["evac_s"])
            if c == 0:
                k.op("vector", "cV", lambda e: e.memset(a_xin[:, :, 0:KC - 1], 0.0))
            else:
                k.op("vector", "cV", lambda e: e.tensor_copy(
                    a_xin[:, :, 0:KC - 1], a_xin[:, :, CTA:CTA + KC - 1]))
            halo_v = k.cnt["cV"]
            k.op("vector", "cV", lambda e: e.tensor_copy(a_xb, a_x))
            ast["xb_v"] = k.cnt["cV"]

            k.wait("tensor", "cV", ast["xb_v"])
            for m in range(NDT):
                pi = m % 2
                k.psum_guard(pi)
                for kk in range(NXT):
                    k.op("tensor", "cT", lambda e, pi=pi, kk=kk, m=m: e.matmul(
                        psum[pi][:, 0:CTA],
                        s_wxin[:, kk, m * 128:(m + 1) * 128],
                        a_xb[:, kk, :],
                        start=(kk == 0), stop=(kk == NXT - 1)))
                mm_t = k.cnt["cT"]
                k.wait("scalar", "cT", mm_t)
                k.wait("scalar", "cV", halo_v)
                k.op("scalar", "cS", lambda e, m=m, pi=pi: e.activation(
                    out=a_xin[:, m, KC - 1:KC - 1 + CTA], in_=psum[pi][:, 0:CTA],
                    func=AF.Copy))
                evac_m[m] = k.cnt["cS"]
                k.psum_set_free(pi, ("cS", k.cnt["cS"]))
            ast["evac_s"] = k.cnt["cS"]

            silu_done = {}
            for m in range(NDT):
                cp = m % 2
                k.wait("vector", "cS", evac_m[m])
                if m >= 2:
                    k.wait("vector", "cS", silu_done[m - 2])
                k.op("vector", "cV", lambda e, m=m, cp=cp: e.tensor_scalar(
                    out=a_conv[:, cp, :], in0=a_xin[:, m, 0:CTA],
                    scalar1=s_cw[:, m, 0:1], scalar2=s_cb[:, m:m + 1],
                    op0=OP.mult, op1=OP.add))
                for t in range(1, KC):
                    k.op("vector", "cV", lambda e, m=m, t=t, cp=cp: e.scalar_tensor_tensor(
                        out=a_conv[:, cp, :], in0=a_xin[:, m, t:t + CTA],
                        scalar=s_cw[:, m, t:t + 1], in1=a_conv[:, cp, :],
                        op0=OP.mult, op1=OP.add))
                k.wait("scalar", "cV", k.cnt["cV"])
                k.wait("scalar", "cT", ast["wx_t"])
                if ast["ustore"]:
                    k.wait("scalar", "qU", ast["ustore"][-1])
                k.op("scalar", "cS", lambda e, m=m, cp=cp: e.activation(
                    out=a_u[:, m, :], in_=a_conv[:, cp, :], func=AF.Silu), nb=True)
                silu_done[m] = k.cnt["cS"]
            silu_s = k.cnt["cS"]

            k.wait("tensor", "cS", silu_s)
            k.psum_guard(2)
            for kk in range(NDT):
                k.op("tensor", "cT", lambda e, kk=kk: e.matmul(
                    psum[2][0:XD, 0:CTA], s_wx[:, kk, :], a_u[:, kk, :],
                    start=(kk == 0), stop=(kk == NDT - 1)))
            ast["wx_t"] = k.cnt["cT"]
            k.wait("scalar", "cT", ast["wx_t"])
            if ast["xstore"]:
                k.wait("scalar", "qX", ast["xstore"][-1])
            k.op("scalar", "cS", lambda e: e.activation(
                out=a_xdbl[:], in_=psum[2][0:XD, 0:CTA], func=AF.Copy))
            k.psum_set_free(2, ("cS", k.cnt["cS"]))

            k.wait("gpsimd", "cS", k.cnt["cS"])
            k.op("gpsimd", "qU",
                 lambda e, lo=lo: e.dma_start(out=d_u_t[:, :, lo:lo + CTA], in_=a_u), 16)
            ast["ustore"].append(k.cnt["qU"])
            pp = c // (CTB // CTA)
            off = (c % (CTB // CTA)) * CTA
            k.op("gpsimd", "qX",
                 lambda e, pp=pp, off=off: e.dma_start(
                     out=d_xdblp_p[pp][:, off:off + CTA], in_=a_xdbl), 16)
            ast["xstore"].append(k.cnt["qX"])

        ar_cc = {}
        ag_cc = {}

        def emit_AR(p):
            k.wait("gpsimd", "qU", k.cnt["qU"])
            k.wait("gpsimd", "qX", k.cnt["qX"])
            ar_cc[p] = k.op("gpsimd", "cc", lambda e, p=p: e.collective_compute(
                "AllReduce", OP.add, replica_groups=RG,
                ins=[d_xdblp_p[p]], outs=[d_xdbl_p[p]]))

        scan_done = {}
        add_done = {}
        bst = dict(prev_yinit=0, prev_store=0)
        HTB = CTB // 2
        assert HTB <= 512
        OFFM = dict(zip(list(range(NDT))[-2:], ((5, 6), (7, 3))))
        idmm = {}

        def emit_B(c):
            lo = c * CTB
            k.wait("sync", "cc", ar_cc[c])
            k.wait("sync", "cV", bst["prev_yinit"])
            k.op("sync", "qS", lambda e, c=c: e.dma_start(
                out=b_xdbl, in_=d_xdbl_p[c][:]), 16)
            k.wait("sync", "qU", ast["ustore"][2 * c + 1])
            k.op("sync", "qS", lambda e, lo=lo: e.dma_start(
                out=b_u, in_=d_u_t[:, :, lo:lo + CTB]), 16)
            k.wait("vector", "qS", k.cnt["qS"])
            k.op("vector", "cV", lambda e: e.tensor_copy(b_dtb, b_xdbl[0:DR, :]))
            k.op("vector", "cV", lambda e: e.tensor_copy(b_bcb, b_xdbl[DR:XD, :]))
            cast_v = k.cnt["cV"]
            k.wait("gpsimd", "cV", cast_v)
            k.op("gpsimd", "qG", lambda e, lo=lo: e.dma_start(
                out=d_bc[:, lo:lo + CTB], in_=b_bcb), 16)
            bc_q = k.cnt["qG"]

            k.wait("tensor", "cV", cast_v)
            for m in range(NDT):
                def emit_dbf(m, sl, pln):
                    k.wait("vector", "cS", pln)
                    k.op("vector", "cV", lambda e, m=m, sl=sl: e.tensor_tensor(
                        out=b_dbf[:, m, sl], in0=b_r[:, sl], in1=b_l[:, sl],
                        op=OP.add))
                    k.op("vector", "cV", lambda e, m=m, sl=sl: e.tensor_tensor(
                        out=b_du[:, m, sl], in0=b_dbf[:, m, sl],
                        in1=b_u[:, m, sl], op=OP.mult))

                hp = None
                for ts in range(NTB):
                    pi = 3 + (ts % 2)
                    sl = slice(ts * TSB, (ts + 1) * TSB)
                    k.psum_guard(pi)
                    k.op("tensor", "cT", lambda e, m=m, sl=sl, pi=pi: e.matmul(
                        psum[pi][:, 0:TSB], s_wdt[:, m * 128:(m + 1) * 128],
                        b_dtb[:, sl], start=True, stop=True))
                    mm_t = k.cnt["cT"]
                    k.wait("vector", "cT", mm_t)
                    k.op("vector", "cV", lambda e, pi=pi, sl=sl, m=m: e.tensor_scalar(
                        out=b_r[:, sl], in0=psum[pi][:, 0:TSB],
                        scalar1=s_bdt[:, m:m + 1], scalar2=0.0,
                        op0=OP.add, op1=OP.max))
                    k.op("vector", "cV", lambda e, pi=pi, sl=sl, m=m: e.tensor_scalar(
                        out=b_a[:, sl], in0=psum[pi][:, 0:TSB],
                        scalar1=s_bdt[:, m:m + 1], scalar2=None, op0=OP.add))
                    k.psum_set_free(pi, ("cV", k.cnt["cV"]))
                    k.op("vector", "cV", lambda e, sl=sl: e.scalar_tensor_tensor(
                        out=b_a[:, sl], in0=b_r[:, sl], scalar=-2.0,
                        in1=b_a[:, sl], op0=OP.mult, op1=OP.add))
                    stt_v = k.cnt["cV"]
                    k.wait("scalar", "cV", stt_v)
                    k.op("scalar", "cS", lambda e, sl=sl: e.activation(
                        out=b_l[:, sl], in_=b_a[:, sl], func=AF.Exp), nb=True)
                    k.op("scalar", "cS", lambda e, sl=sl: e.activation(
                        out=b_l[:, sl], in_=b_l[:, sl], func=AF.Ln, bias=1.0))
                    if hp is not None:
                        emit_dbf(m, hp[0], hp[1])
                    hp = (sl, k.cnt["cS"])
                emit_dbf(m, hp[0], hp[1])
            du_v = k.cnt["cV"]

            k.wait("vector", "qYs", bst["prev_store"])
            for m in range(NDT):
                k.op("vector", "cV", lambda e, m=m: e.tensor_scalar(
                    out=b_y[:, m, :], in0=b_u[:, m, :], scalar1=s_D[:, m:m + 1],
                    scalar2=None, op0=OP.mult))
            bst["prev_yinit"] = k.cnt["cV"]

            k.wait("sync", "qG", bc_q)
            for n in range(NS):
                par = n % 2
                qn = f"q{par}"
                gn = c * NS + n
                if gn >= 2:
                    k.wait("sync", "cV", scan_done[(gn - 2, "mul", NDT - 1)])
                k.op("sync", qn, lambda e, n=n, lo=lo, par=par: e.dma_start(
                    out=b_brep[:, par, :],
                    in_=d_bc[n:n + 1, lo:lo + CTB].broadcast_to([128, CTB])), 16)
                k.op("sync", qn, lambda e, n=n, lo=lo, par=par: e.dma_start(
                    out=b_crep[:, par, :],
                    in_=d_bc[NS + n:NS + n + 1, lo:lo + CTB].broadcast_to([128, CTB])), 16)
                rep_q = k.cnt[qn]
                for m in range(NDT):
                    u = gn * NDT + m
                    up = u % 4
                    if u >= 4:
                        k.wait("scalar", "cV", scan_done[((u - 4) // NDT, "scan", (u - 4) % NDT)])
                    k.wait("scalar", "cV", du_v)
                    k.op("scalar", "cS", lambda e, m=m, n=n, up=up: e.activation(
                        out=b_dA[:, up, :], in_=b_dbf[:, m, :], func=AF.Exp,
                        scale=s_A[:, m, n:n + 1]), nb=True)
                    dA_s = k.cnt["cS"]
                    k.wait("vector", qn, rep_q)
                    u2 = u % 2
                    if (u - 2) in idmm:
                        k.wait("vector", "cT", idmm[u - 2])
                    k.op("vector", "cV", lambda e, m=m, u2=u2, par=par: e.tensor_tensor(
                        out=b_tmp[:, u2, :], in0=b_du[:, m, :], in1=b_brep[:, par, :],
                        op=OP.mult))
                    k.wait("vector", "cS", dA_s)
                    k.op("vector", "cV", lambda e, m=m, n=n, up=up, u2=u2: e.tensor_tensor_scan(
                        out=b_h[:, u2, :], data0=b_dA[:, up, :], data1=b_tmp[:, u2, :],
                        initial=s_carry[:, m, n:n + 1],
                        op0=OP.mult, op1=OP.add))
                    scan_done[(gn, "scan", m)] = k.cnt["cV"]
                    k.op("vector", "cV", lambda e, m=m, n=n, u2=u2: e.tensor_copy(
                        s_carry[:, m, n:n + 1], b_h[:, u2, CTB - 1:CTB]))
                    k.op("vector", "cV", lambda e, u2=u2, par=par: e.tensor_tensor(
                        out=b_tmp[:, u2, :], in0=b_h[:, u2, :], in1=b_crep[:, par, :],
                        op=OP.mult))
                    scan_done[(gn, "mul", m)] = k.cnt["cV"]
                    if m in OFFM:
                        blo, bhi = OFFM[m]
                        if n == 0:
                            k.psum_guard(blo)
                            k.psum_guard(bhi)
                        k.wait("tensor", "cV", scan_done[(gn, "mul", m)])
                        k.op("tensor", "cT", lambda e, u2=u2, blo=blo, n=n: e.matmul(
                            psum[blo][:, 0:HTB], s_id[:], b_tmp[:, u2, 0:HTB],
                            start=(n == 0), stop=(n == NS - 1)))
                        k.op("tensor", "cT", lambda e, u2=u2, bhi=bhi, n=n: e.matmul(
                            psum[bhi][:, 0:HTB], s_id[:], b_tmp[:, u2, HTB:CTB],
                            start=(n == 0), stop=(n == NS - 1)))
                        idmm[u] = k.cnt["cT"]
                    else:
                        k.op("vector", "cV", lambda e, m=m, u2=u2: e.tensor_tensor(
                            out=b_y[:, m, :], in0=b_y[:, m, :], in1=b_tmp[:, u2, :],
                            op=OP.add))
                        add_done[u] = k.cnt["cV"]

            for m, (blo, bhi) in OFFM.items():
                k.wait("vector", "cT", idmm[(c * NS + NS - 1) * NDT + m])
                k.op("vector", "cV", lambda e, m=m, blo=blo: e.tensor_tensor(
                    out=b_y[:, m, 0:HTB], in0=b_y[:, m, 0:HTB],
                    in1=psum[blo][:, 0:HTB], op=OP.add))
                k.op("vector", "cV", lambda e, m=m, bhi=bhi: e.tensor_tensor(
                    out=b_y[:, m, HTB:CTB], in0=b_y[:, m, HTB:CTB],
                    in1=psum[bhi][:, 0:HTB], op=OP.add))
                k.psum_set_free(blo, ("cV", k.cnt["cV"]))
                k.psum_set_free(bhi, ("cV", k.cnt["cV"]))

            s = c // (NCHB // 2)
            p = c % (NCHB // 2)
            k.wait("gpsimd", "cV", k.cnt["cV"])
            k.op("gpsimd", "qYs", lambda e, s=s, p=p: e.dma_start(
                out=d_y_p[p][s].rearrange("(t p) l -> p t l", p=128),
                in_=b_y), 16)
            bst["prev_store"] = k.cnt["qYs"]
            if s == 1:
                k.wait("gpsimd", "qYs", k.cnt["qYs"])
                ag_cc[p] = k.op("gpsimd", "cc", lambda e, p=p: e.collective_compute(
                    "AllGather", OP.bypass, replica_groups=RG,
                    ins=[d_y_p[p]], outs=[d_yg_p[p]]))

        RAT = CTB // CTA
        WPC = -(-NWT // NCHB)
        for pp in range(min(2, NCHB)):
            emit_A(RAT * pp)
            emit_A(RAT * pp + 1)
            emit_AR(pp)
        for p in range(NCHB):
            if p + 2 < NCHB:
                emit_A(RAT * (p + 2))
                emit_A(RAT * (p + 2) + 1)
                emit_AR(p + 2)
            emit_B(p)
            emit_w_tiles(WPC)
        emit_w_tiles(NWT)

    k.barrier()
    AF32.reset(); AB16.reset(); AI32.reset()

    if True:
        c_sq = AB16.alloc([128, CTC])
        c_mean = AF32.alloc([1, CTC])
        c_msq = AF32.alloc([1, CTC])
        c_var = AF32.alloc([1, CTC])
        c_mr = AB16.alloc([128, CTC])
        c_rr = AB16.alloc([128, CTC])
        c_lnr0 = AB16.alloc([1, CTC])
        c_lnr1 = AB16.alloc([1, CTC])
        c_out = AF32.alloc([128, CTC])
        c_z = AB16.alloc([128, NXT, CTC])
        c2_f32_mark = AF32.off
        c2_b16_mark = AB16.off
        WSL1 = min(6, max(NDI, NXT)) * 128
        c_wsl = AB16.alloc([128, 4, WSL1])
        c_x = AF32.alloc([128, 2, NXT, CTC])
        c_xb = AB16.alloc([128, 2, NXT, CTC])
        c_y = AB16.alloc([128, NDI, CTC])
        c_yc = AB16.alloc([128, 4, NDT, CTC])
        c_res = AB16.alloc([128, NDI, CTC])
        c_x1 = AB16.alloc([128, NXT, CTC])
        xm_t = xTmine.rearrange("(t p) l -> p t l", p=128)
        outT_t = outT.rearrange("(t p) l -> p t l", p=128)
        dx1_t = d_x1.rearrange("(t p) l -> p t l", p=128)

        def grps(n):
            out, o = [], 0
            while o < n:
                sz = min(6, n - o)
                out.append((o, sz))
                o += sz
            return out

        def layernorm(src3, dst3, out_tile=None, mid=None, dst_wait=None,
                      ln_sync=False):
            k.psum_guard(6)
            k.psum_guard(7)
            k.wait("scalar", "cV", k.cnt["cV"])
            for i in range(NXT):
                k.wait("scalar", "cT", k.cnt["cT"])
                k.op("scalar", "cS", lambda e, i=i: e.activation(
                    out=c_sq, in_=src3[:, i, :], func=AF.Square))
                sq_s = k.cnt["cS"]
                k.wait("tensor", "cS", sq_s)
                k.op("tensor", "cT", lambda e, i=i: e.matmul(
                    psum[6][0:1, 0:CTC], s_ones[:], src3[:, i, :],
                    start=(i == 0), stop=(i == NXT - 1)))
                k.op("tensor", "cT", lambda e, i=i: e.matmul(
                    psum[7][0:1, 0:CTC], s_ones[:], c_sq[:],
                    start=(i == 0), stop=(i == NXT - 1)))
            st_t = k.cnt["cT"]
            k.wait("vector", "cT", st_t)
            k.op("vector", "cV", lambda e: e.tensor_scalar_mul(
                c_mean, psum[6][0:1, 0:CTC], 1.0 / D))
            k.op("vector", "cV", lambda e: e.tensor_scalar_mul(
                c_msq, psum[7][0:1, 0:CTC], 1.0 / D))
            k.psum_set_free(6, ("cV", k.cnt["cV"]))
            k.psum_set_free(7, ("cV", k.cnt["cV"]))
            k.op("vector", "cV", lambda e: e.tensor_tensor(
                out=c_var, in0=c_mean, in1=c_mean, op=OP.mult))
            k.op("vector", "cV", lambda e: e.tensor_tensor(
                out=c_var, in0=c_msq, in1=c_var, op=OP.subtract))
            var_v = k.cnt["cV"]
            k.wait("scalar", "cV", var_v)
            k.op("scalar", "cS", lambda e: e.activation(
                out=c_var, in_=c_var, func=AF.Ln, bias=s_eps[0:1, 0:1]))
            k.op("scalar", "cS", lambda e: e.activation(
                out=c_msq, in_=c_var, func=AF.Exp, scale=-0.5))
            rows_s = k.cnt["cS"]
            k.wait("vector", "cS", rows_s)
            k.op("vector", "cV", lambda e: e.tensor_copy(c_lnr0, c_mean))
            k.op("vector", "cV", lambda e: e.tensor_copy(c_lnr1, c_msq))
            rows_v = k.cnt["cV"]
            if ln_sync:
                k.wait("sync", "cV", rows_v)
                k.wait("sync", "qL", k.cnt["qL"])
                k.op("sync", "qL", lambda e: e.dma_start(
                    out=d_ln[0:1, :], in_=c_lnr0), 16)
                k.op("sync", "qL", lambda e: e.dma_start(
                    out=d_ln[1:2, :], in_=c_lnr1), 16)
                k.wait("sync", "qL", k.cnt["qL"])
            else:
                k.wait("gpsimd", "cV", rows_v)
                k.wait("gpsimd", "qM", k.cnt["qM"])
                k.op("gpsimd", "qG", lambda e: e.dma_start(
                    out=d_ln[0:1, :], in_=c_lnr0), 16)
                k.op("gpsimd", "qG", lambda e: e.dma_start(
                    out=d_ln[1:2, :], in_=c_lnr1), 16)
                k.wait("scalar", "qG", k.cnt["qG"])
            leng = "sync" if ln_sync else "scalar"
            lsem = "qL" if ln_sync else "qM"
            k.op(leng, lsem, lambda e: e.dma_start(
                out=c_mr, in_=d_ln[0:1, :].broadcast_to([128, CTC])), 16, nb=True)
            k.op(leng, lsem, lambda e: e.dma_start(
                out=c_rr, in_=d_ln[1:2, :].broadcast_to([128, CTC])), 16, nb=True)
            if mid is not None:
                mid()
            k.wait("vector", lsem, k.cnt[lsem])
            if dst_wait is not None:
                k.wait("vector", dst_wait[0], dst_wait[1])
            for i in range(NXT):
                k.op("vector", "cV", lambda e, i=i: e.tensor_tensor(
                    out=dst3[:, i, :], in0=src3[:, i, :], in1=c_mr,
                    op=OP.subtract))
                if out_tile is None:
                    k.op("vector", "cV", lambda e, i=i: e.tensor_tensor(
                        out=dst3[:, i, :], in0=dst3[:, i, :], in1=c_rr,
                        op=OP.mult))
                else:
                    k.wait("vector", "qY", k.cnt["qY"])
                    k.op("vector", "cV", lambda e, i=i: e.tensor_tensor(
                        out=c_out[:], in0=dst3[:, i, :], in1=c_rr,
                        op=OP.mult))
                    ov = k.cnt["cV"]
                    k.wait("sync", "cV", ov)
                    k.op("sync", "qY",
                         lambda e, i=i: e.dma_start(out=out_tile(i), in_=c_out), 16)

        NPC = 4
        pieces = [(c, j, s) for c in range(NCHC) for j in range(2)
                  for s in range(2)]
        st = dict(zadd={-1: 0, -2: 0}, wout={-1: 0}, res={-1: 0, -2: 0},
                  selv={}, ycq={}, cast={}, xq={}, x1st={-1: 0}, silu_gm={},
                  g_gm={})
        slab = dict(i=0, done={})

        def slab_load(src, W):
            i = slab["i"]
            par4 = i % 4
            qn = f"q{par4}"
            if i >= 4:
                k.wait("sync", "cT", slab["done"][i - 4])
            k.op("sync", qn, lambda e, s=src, par4=par4, W=W: e.dma_start(
                out=c_wsl[:, par4, 0:W], in_=s), 16)
            k.wait("tensor", qn, k.cnt[qn])
            return par4

        def slab_mark():
            slab["done"][slab["i"]] = k.cnt["cT"]
            slab["i"] += 1

        def x_load(c):
            if c >= NCHC:
                return
            par = c % 2
            k.wait("scalar", "cV", st["zadd"][c - 2])
            k.op("scalar", f"cx{par}", lambda e, lo=c * CTC, par=par: e.dma_start(
                out=c_x[:, par], in_=xm_t[:, :, lo:lo + CTC]), 16, nb=True)
            st["xq"][c] = k.cnt[f"cx{par}"]

        def x_cast(c):
            if c >= NCHC:
                return
            par = c % 2
            k.wait("vector", f"cx{par}", st["xq"][c])
            k.wait("vector", "cT", st["res"][c - 2])
            k.op("vector", "cV", lambda e, par=par: e.tensor_copy(
                c_xb[:, par], c_x[:, par]))
            st["cast"][c] = k.cnt["cV"]

        def piece_load(gi):
            if gi >= len(pieces):
                return
            c, j, s = pieces[gi]
            lo = c * CTC
            pidx = lo // CTB
            poff = lo % CTB
            slot = gi % 4
            sn = f"cy{slot}"
            k.wait("scalar", "cV", st["selv"].get(gi - 4, 0))
            k.wait("scalar", "cc", ag_cc[pidx])
            k.op("scalar", sn, lambda e, j=j, s=s, pidx=pidx, poff=poff, slot=slot:
                 e.dma_start(out=c_yc[:, slot],
                             in_=d_yg_p[pidx][j, s, :, poff:poff + CTC]
                             .rearrange("(t p) l -> p t l", p=128)), 16, nb=True)
            st["ycq"][gi] = (sn, k.cnt[sn])

        def emit_select(c):
            for i in range(NPC):
                gi = NPC * c + i
                _, j, s = pieces[gi]
                slot = gi % 4
                sn, sv = st["ycq"][gi]
                k.wait("vector", sn, sv)
                k.wait("vector", "cT", st["wout"][c - 1])
                for m in range(NDT):
                    if s == 0:
                        k.op("vector", "cV", lambda e, j=j, m=m, slot=slot:
                             e.tensor_scalar(
                                 out=c_y[:, j * NDT + m, :],
                                 in0=c_yc[:, slot, m, :],
                                 scalar1=s_msel[:, 0:1], scalar2=None,
                                 op0=OP.mult))
                    else:
                        k.op("vector", "cV", lambda e, j=j, m=m, slot=slot:
                             e.scalar_tensor_tensor(
                                 out=c_y[:, j * NDT + m, :],
                                 in0=c_yc[:, slot, m, :],
                                 scalar=s_msel[:, 1:2],
                                 in1=c_y[:, j * NDT + m, :],
                                 op0=OP.mult, op1=OP.add))
                st["selv"][gi] = k.cnt["cV"]
                piece_load(gi + 4)

        def emit_res(c):
            par = c % 2
            k.wait("tensor", "cV", st["cast"][c])
            for (gof, gsz) in grps(NDI):
                for pi in range(gsz):
                    k.psum_guard(pi)
                GW = gsz * 128
                for kk in range(NXT):
                    par4 = slab_load(
                        d_wres[kk * 128:(kk + 1) * 128,
                               gof * 128:gof * 128 + GW], GW)
                    for m in range(gsz):
                        k.op("tensor", "cT", lambda e, m=m, kk=kk, par4=par4,
                             par=par: e.matmul(
                                 psum[m][:, 0:CTC],
                                 c_wsl[:, par4, m * 128:(m + 1) * 128],
                                 c_xb[:, par, kk, :],
                                 start=(kk == 0), stop=(kk == NXT - 1)))
                    slab_mark()
                last_mm = k.cnt["cT"]
                for m in range(gsz):
                    gm = gof + m
                    k.wait("scalar", "cT", last_mm)
                    k.op("scalar", "cS", lambda e, m=m, gm=gm: e.activation(
                        out=c_res[:, gm, :], in_=psum[m][:, 0:CTC],
                        func=AF.Silu))
                    st["silu_gm"][gm] = k.cnt["cS"]
                    k.psum_set_free(m, ("cS", k.cnt["cS"]))
            st["res"][c] = k.cnt["cT"]

        def emit_gw(c):
            par = c % 2
            for gm in range(NDI):
                k.wait("vector", "cS", st["silu_gm"][gm])
                k.op("vector", "cV", lambda e, gm=gm: e.tensor_tensor(
                    out=c_y[:, gm, :], in0=c_y[:, gm, :], in1=c_res[:, gm, :],
                    op=OP.mult))
                st["g_gm"][gm] = k.cnt["cV"]
            for (mof, msz) in grps(NXT):
                for pi in range(msz):
                    k.psum_guard(pi)
                MW = msz * 128
                for kk in range(NDI):
                    par4 = slab_load(
                        d_wout[kk * 128:(kk + 1) * 128,
                               mof * 128:mof * 128 + MW], MW)
                    k.wait("tensor", "cV", st["g_gm"][kk])
                    for m in range(msz):
                        k.op("tensor", "cT", lambda e, m=m, kk=kk, par4=par4:
                             e.matmul(
                                 psum[m][:, 0:CTC],
                                 c_wsl[:, par4, m * 128:(m + 1) * 128],
                                 c_y[:, kk, :],
                                 start=(kk == 0), stop=(kk == NDI - 1)))
                    slab_mark()
                k.wait("vector", "cT", k.cnt["cT"])
                k.wait("vector", f"cx{par}", st["xq"][c])
                for m in range(msz):
                    k.op("vector", "cV", lambda e, m=m, mof=mof, par=par:
                         e.tensor_tensor(
                             out=c_z[:, mof + m, :],
                             in0=c_x[:, par, mof + m, :],
                             in1=psum[m][:, 0:CTC], op=OP.add))
                    k.psum_set_free(m, ("cV", k.cnt["cV"]))
            st["wout"][c] = k.cnt["cT"]
            st["zadd"][c] = k.cnt["cV"]

        for gi0 in range(4):
            piece_load(gi0)
        x_load(0)
        for c in range(NCHC):
            x_load(c + 1)
            if c == 0:
                x_cast(0)
                emit_select(0)
            x_cast(c + 1)
            emit_res(c)
            emit_gw(c)
            layernorm(c_z, c_x1,
                      mid=(lambda c=c: emit_select(c + 1))
                      if c + 1 < NCHC else None,
                      dst_wait=("qYs", st["x1st"][c - 1]))
            k.wait("gpsimd", "cV", k.cnt["cV"])
            k.op("gpsimd", "qYs", lambda e, lo=c * CTC: e.dma_start(
                out=dx1_t[:, :, lo:lo + CTC], in_=c_x1), 16)
            st["x1st"][c] = k.cnt["qYs"]

        k.barrier()
        AF32.off = c2_f32_mark
        AB16.off = c2_b16_mark
        x1a = AB16.alloc([128, NXT, MYT])
        hmb = AB16.alloc([128, 2, MYT])
        sgp = AB16.alloc([128, 2, CTC])
        c_wg = AB16.alloc([128, 4, NXT, 128])
        c_wu = AB16.alloc([128, 4, NXT, 128])
        c_wi = AI32.alloc([128, 2, NXT * 128])
        k.op("sync", "qW", lambda e: e.dma_start(out=x1a, in_=dx1_t), 16)
        k.wait("tensor", "qW", k.cnt["qW"])
        wgi = wg_t.rearrange("m p t c -> m p (t c)")
        wui = wu_t.rearrange("m p t c -> m p (t c)")
        NTC2 = MYT // CTC
        ffn_mm = {}
        hst = {}
        wc = {0: 0, 1: 0}
        wgu_done = {}

        def emit_wgu_cast(m):
            if m >= NH:
                return
            par = m % 4
            for wh in range(2):
                pari = (2 * m + wh) % 2
                qn = f"q{pari}"
                k.wait("sync", "cS", wc[pari])
                k.op("sync", qn, lambda e, m=m, pari=pari, wh=wh: e.dma_start(
                    out=c_wi[:, pari], in_=(wgi if wh == 0 else wui)[m]), 16)
                k.wait("scalar", qn, k.cnt[qn])
                k.wait("scalar", "cT", ffn_mm.get(m - 4, 0))
                dst = (c_wg if wh == 0 else c_wu)[:, par].rearrange(
                    "p t c -> p (t c)")
                k.op("scalar", "cS", lambda e, dst=dst, pari=pari: e.activation(
                    out=dst, in_=c_wi[:, pari], func=AF.Copy), nb=True)
                wc[pari] = k.cnt["cS"]
            wgu_done[m] = k.cnt["cS"]

        emit_wgu_cast(0)
        emit_wgu_cast(1)
        for m in range(NH):
            par = m % 4
            emit_wgu_cast(m + 2)
            k.wait("tensor", "cS", wgu_done[m])
            hb = m % 2
            if m >= 2:
                k.wait("vector", hst[m - 2][0], hst[m - 2][1])
            for tc in range(NTC2):
                sl = slice(tc * CTC, (tc + 1) * CTC)
                pg = 4 + 2 * (tc % 2)
                pu = pg + 1
                k.psum_guard(pg)
                k.psum_guard(pu)
                for kk in range(NXT):
                    k.op("tensor", "cT", lambda e, kk=kk, par=par, sl=sl, pg=pg: e.matmul(
                        psum[pg][:, 0:CTC], c_wg[:, par, kk, :], x1a[:, kk, sl],
                        start=(kk == 0), stop=(kk == NXT - 1)))
                for kk in range(NXT):
                    k.op("tensor", "cT", lambda e, kk=kk, par=par, sl=sl, pu=pu: e.matmul(
                        psum[pu][:, 0:CTC], c_wu[:, par, kk, :], x1a[:, kk, sl],
                        start=(kk == 0), stop=(kk == NXT - 1)))
                mm_t = k.cnt["cT"]
                k.wait("scalar", "cT", mm_t)
                sp = tc % 2
                k.op("scalar", "cS", lambda e, sp=sp, pg=pg: e.activation(
                    out=sgp[:, sp, :], in_=psum[pg][:, 0:CTC], func=AF.Silu, scale=QS))
                sg_s = k.cnt["cS"]
                k.psum_set_free(pg, ("cS", sg_s))
                k.wait("vector", "cS", sg_s)
                k.op("vector", "cV", lambda e, hb=hb, sl=sl, sp=sp, pu=pu: e.scalar_tensor_tensor(
                    out=hmb[:, hb, sl], in0=psum[pu][:, 0:CTC], scalar=QS,
                    in1=sgp[:, sp, :], op0=OP.mult, op1=OP.mult))
                k.psum_set_free(pu, ("cV", k.cnt["cV"]))
            ffn_mm[m] = k.cnt["cT"]
            k.wait("gpsimd", "cV", k.cnt["cV"])
            hq = "qU" if hb == 0 else "qX"
            k.op("gpsimd", hq, lambda e, m=m, hb=hb: e.dma_start(
                out=d_h[m], in_=hmb[:, hb, :]), 16)
            hst[m] = (hq, k.cnt[hq])

        k.barrier()
        AF32.off = c2_f32_mark
        AB16.off = c2_b16_mark
        c_z2 = AB16.alloc([128, NXT, CTC])
        c_wd = AB16.alloc([128, 4, WSL1])
        htc = AB16.alloc([128, 2, NH, CTC])
        x1c = AB16.alloc([128, 2, NXT, CTC])
        d_h_t = d_h.rearrange("m p l -> p m l")
        st2 = dict(dn={-1: 0, -2: 0}, z2={-1: 0, -2: 0})
        slab2 = dict(i=0, done={})

        def slab2_load(src, W):
            i = slab2["i"]
            par4 = i % 4
            gn = f"g{par4}"
            if i >= 4:
                k.wait("gpsimd", "cT", slab2["done"][i - 4])
            dst = c_wd[:, par4, 0:W]
            if len(src.shape) == 3:
                dst = dst.rearrange("p (a c) -> p a c", a=src.shape[1])
            k.op("gpsimd", gn, lambda e, s=src, d=dst: e.dma_start(
                out=d, in_=s), 16)
            k.wait("tensor", gn, k.cnt[gn])
            return par4

        def slab2_mark():
            slab2["done"][slab2["i"]] = k.cnt["cT"]
            slab2["i"] += 1

        for tc in range(NTC2):
            lo = tc * CTC
            hb = tc % 2
            hq = "qU" if hb == 0 else "qX"
            k.wait("gpsimd", "cT", st2["dn"][tc - 2])
            k.wait("gpsimd", "cV", st2["z2"][tc - 2])
            k.op("gpsimd", hq, lambda e, lo=lo, hb=hb: e.dma_start(
                out=htc[:, hb], in_=d_h_t[:, :, lo:lo + CTC]), 16)
            k.op("gpsimd", hq, lambda e, lo=lo, hb=hb: e.dma_start(
                out=x1c[:, hb], in_=dx1_t[:, :, lo:lo + CTC]), 16)
            lq = k.cnt[hq]
            k.wait("tensor", hq, lq)
            for (mof, msz) in grps(NXT):
                for pi in range(msz):
                    k.psum_guard(pi)
                MW = msz * 128
                KG = 2 if (2 * MW <= WSL1 and NH % 2 == 0) else 1
                for kk0 in range(0, NH, KG):
                    par4 = slab2_load(
                        d_wd[kk0 * 128:(kk0 + KG) * 128,
                             mof * 128:mof * 128 + MW]
                        .rearrange("(a p) c -> p a c", p=128), KG * MW)
                    for kl in range(KG):
                        kk = kk0 + kl
                        for m in range(msz):
                            k.op("tensor", "cT", lambda e, m=m, kk=kk, kl=kl,
                                 par4=par4, hb=hb, MW=MW: e.matmul(
                                     psum[m][:, 0:CTC],
                                     c_wd[:, par4,
                                          kl * MW + m * 128:kl * MW + (m + 1) * 128],
                                     htc[:, hb, kk, :],
                                     start=(kk == 0), stop=(kk == NH - 1)))
                    slab2_mark()
                k.wait("vector", "cT", k.cnt["cT"])
                k.wait("vector", hq, lq)
                for m in range(msz):
                    k.op("vector", "cV", lambda e, m=m, mof=mof, hb=hb:
                         e.scalar_tensor_tensor(
                             out=c_z2[:, mof + m, :], in0=psum[m][:, 0:CTC],
                             scalar=QS, in1=x1c[:, hb, mof + m, :],
                             op0=OP.mult, op1=OP.add))
                    k.psum_set_free(m, ("cV", k.cnt["cV"]))
            st2["dn"][tc] = k.cnt["cT"]
            st2["z2"][tc] = k.cnt["cV"]
            layernorm(c_z2, c_z,
                      out_tile=lambda i, lo=lo: outT_t[:, i, lo:lo + CTC],
                      ln_sync=True)

    for q in ("qG", "qS", "q0", "q1", "g0", "g1", "g2", "g3", "q2", "q3", "qW", "qL", "qY", "qYs", "qU", "qX", "qA", "cx0", "cx1", "cy0", "cy1", "qi0", "qi1", "qf0", "qf1", "qM", "cy2", "cy3"):
        k.wait("gpsimd", q, k.cnt[q])
    k.play()
    k.stack.close()
    return nc


_NC_CACHE = {}


def _get_nc(key, cfg):
    if key not in _NC_CACHE:
        _NC_CACHE[key] = build_nc(cfg)
    return _NC_CACHE[key]


def make_inputs(cfg, x, W_in, conv_w, conv_b, W_x, W_dt, b_dt, A_log, Dv, W_out,
                wg, wu, wd):
    g = _dv(cfg)
    DI, NS, DR, H = (g[k] for k in ("DI", "NS", "DR", "H"))
    HDD, MYT, NH, NXT = g["HDD"], g["MYT"], g["NH"], g["NXT"]
    NDT, XD, KC = g["NDT"], g["XD"], cfg["KC"]
    C = np.ascontiguousarray
    wgT = wg.T
    wuT = wu.T
    wdT = C(wd.T)
    wg_t = C(wgT.reshape(NXT, 128, NH, 128).transpose(2, 1, 0, 3))
    wu_t = C(wuT.reshape(NXT, 128, NH, 128).transpose(2, 1, 0, 3))
    w_res_c = C(W_in[:, DI:])
    w_out_c = C(W_out)
    ins = []
    for c in range(8):
        b, h = c >> 1, c & 1
        dd = slice(h * HDD, (h + 1) * HDD)
        ins.append({
            "xT": C(x[b].T),
            "xTmine": C(x[b, h * MYT:(h + 1) * MYT, :].T),
            "w_xin": C(W_in[:, dd]),
            "w_res": w_res_c,
            "w_x": C(W_x[dd, :].reshape(NDT, 128, XD).transpose(1, 0, 2)),
            "w_dt": C(W_dt[:, dd]),
            "b_dt": C(b_dt[dd].reshape(NDT, 128).T),
            "a_log": C(A_log[dd, :].reshape(NDT, 128, NS).transpose(1, 0, 2)),
            "d_vec": C(Dv[dd].reshape(NDT, 128).T),
            "conv_w": C(conv_w[dd, :].reshape(NDT, 128, KC).transpose(1, 0, 2)),
            "conv_b": C(conv_b[dd].reshape(NDT, 128).T),
            "w_out": w_out_c,
            "m_sel": C(np.tile(np.array([[1.0 - h, float(h)]], np.float32), (128, 1))),
            "ident": np.eye(128, dtype=np.float32),
            "wg_t": wg_t,
            "wu_t": wu_t,
            "wd_t": wdT,
        })
    return ins


def run(cfg, key, x, W_in, conv_w, conv_b, W_x, W_dt, b_dt, A_log, Dv, W_out,
        wg, wu, wd, runner=None):
    g = _dv(cfg)
    nc = _get_nc(key, cfg)
    ins = make_inputs(cfg, x, W_in, conv_w, conv_b, W_x, W_dt, b_dt, A_log,
                      Dv, W_out, wg, wu, wd)
    if runner is None:
        res = run_bass_kernel_spmd(nc, ins, core_ids=list(range(8)))
        outs = [np.asarray(r["outT"]) for r in res.results]
    else:
        outs = runner(nc, ins)
    B, L, D, MYT = x.shape[0], cfg["L"], cfg["D"], g["MYT"]
    out = np.empty((B, L, D), np.float32)
    for c in range(8):
        b, h = c >> 1, c & 1
        out[b, h * MYT:(h + 1) * MYT, :] = outs[c].T
    return out


def kernel(L0_attn_x, L0_attn_W_in, L0_attn_conv_w, L0_attn_conv_b, L0_attn_W_x,
           L0_attn_W_dt, L0_attn_b_dt, L0_attn_A_log, L0_attn_D, L0_attn_W_out,
           L0_ffn_w_gate, L0_ffn_w_up, L0_ffn_w_down):
    return run(FULL_CFG, "full", L0_attn_x, L0_attn_W_in, L0_attn_conv_w,
               L0_attn_conv_b, L0_attn_W_x, L0_attn_W_dt, L0_attn_b_dt,
               L0_attn_A_log, L0_attn_D, L0_attn_W_out,
               L0_ffn_w_gate, L0_ffn_w_up, L0_ffn_w_down)


# version 49
# speedup vs baseline: 1.0273x; 1.0273x over previous
from contextlib import ExitStack

import numpy as np

import concourse.bass as bass
import concourse.mybir as mybir
from concourse.bass_utils import run_bass_kernel_spmd

F32 = mybir.dt.float32
BF16 = mybir.dt.bfloat16
I32 = mybir.dt.int32
AF = mybir.ActivationFunctionType
OP = mybir.AluOpType

FULL_CFG = dict(L=8192, D=1024, DI=2048, NS=16, DR=64, KC=4, H=3072,
                CTA=512, CTB=1024, CTC=512, EPS=1e-5, QSCALE=1.0 / 127.0)


def _dv(cfg):
    d = dict(cfg)
    d["HDD"] = cfg["DI"] // 2
    d["NDT"] = d["HDD"] // 128
    d["NXT"] = cfg["D"] // 128
    d["XD"] = cfg["DR"] + 2 * cfg["NS"]
    d["MYT"] = cfg["L"] // 2
    d["NCHA"] = cfg["L"] // cfg["CTA"]
    d["NCHB"] = cfg["L"] // cfg["CTB"]
    d["NCHC"] = d["MYT"] // cfg["CTC"]
    d["NH"] = cfg["H"] // 128
    d["NDI"] = cfg["DI"] // 128
    d["TSB"] = min(512, cfg["CTB"])
    d["NTB"] = cfg["CTB"] // d["TSB"]
    assert d["XD"] <= 128 and cfg["DR"] <= 128
    assert cfg["CTA"] <= 512 and cfg["CTC"] <= 512
    assert d["NCHB"] % 2 == 0
    return d


class K:

    ENGS = ("sync", "gpsimd", "vector", "scalar", "tensor")
    OWN = {"vector": "cV", "scalar": "cS", "gpsimd": "cG"}

    def __init__(self, nc):
        self.nc = nc
        self.ops = {e: [] for e in self.ENGS}
        self.stack = ExitStack()
        self.sem = {}
        self.cnt = {}
        self.waited = {e: {} for e in self.ENGS}
        self.psum_free = {}

    def mksem(self, *names):
        for n in names:
            self.sem[n] = self.stack.enter_context(self.nc.semaphore(n))
            self.cnt[n] = 0

    def wait(self, eng, semname, val):
        if val <= 0:
            return
        if self.waited[eng].get(semname, -1) >= val:
            return
        self.waited[eng][semname] = val
        s = self.sem[semname]
        self.ops[eng].append(lambda e, s=s, v=val: e.wait_ge(s, v))

    def op(self, eng, semname, build, inc=1, nb=False):
        own = self.OWN.get(eng)
        if not nb and own is not None and eng in ("vector", "scalar", "gpsimd"):
            self.wait(eng, own, self.cnt[own])
        self.cnt[semname] += inc
        s = self.sem[semname]
        self.ops[eng].append(lambda e, b=build, s=s, i=inc: b(e).then_inc(s, i))
        return self.cnt[semname]

    def psum_guard(self, idx):
        for sname, v in self.psum_free.get(idx, ()):
            self.wait("tensor", sname, v)

    def psum_set_free(self, idx, *pairs):
        self.psum_free[idx] = pairs

    def barrier(self, exclude=("cc",)):
        snap = dict(self.cnt)
        for e in self.ENGS:
            for sname, v in snap.items():
                if sname in exclude:
                    continue
                self.wait(e, sname, v)

    def play(self):
        nc = self.nc
        with nc.Block() as block:
            @block.sync
            def _(e):
                for f in self.ops["sync"]:
                    f(e)

            @block.gpsimd
            def _(e):
                for f in self.ops["gpsimd"]:
                    f(e)

            @block.vector
            def _(e):
                for f in self.ops["vector"]:
                    f(e)

            @block.scalar
            def _(e):
                for f in self.ops["scalar"]:
                    f(e)

            @block.tensor
            def _(e):
                for f in self.ops["tensor"]:
                    f(e)


class Arena:

    def __init__(self, k, name, dtype, n):
        self.t = k.stack.enter_context(k.nc.sbuf_tensor(name, [128, n], dtype))
        self.n = n
        self.off = 0

    def alloc(self, shape):
        p = shape[0]
        cols = 1
        for s in shape[1:]:
            cols *= s
        off = (self.off + 31) & ~31
        ap = self.t[0:p, off:off + cols]
        self.off = off + cols
        assert self.off <= self.n, (self.off, self.n, shape)
        if len(shape) > 2:
            names = " ".join(f"d{i}" for i in range(1, len(shape)))
            kw = {f"d{i}": shape[i] for i in range(1, len(shape))}
            ap = ap.rearrange(f"p ({names}) -> p {names}", **kw)
        return ap

    def reset(self):
        self.off = 0


def build_nc(cfg):
    g = _dv(cfg)
    L, D, DI, NS, DR, KC, H = (g[k] for k in ("L", "D", "DI", "NS", "DR", "KC", "H"))
    CTA, CTB, CTC, EPS, QS = (g[k] for k in ("CTA", "CTB", "CTC", "EPS", "QSCALE"))
    HDD, NDT, NXT, XD, MYT = (g[k] for k in ("HDD", "NDT", "NXT", "XD", "MYT"))
    NCHA, NCHB, NCHC, NH, NDI = (g[k] for k in ("NCHA", "NCHB", "NCHC", "NH", "NDI"))
    TSB, NTB = g["TSB"], g["NTB"]

    nc = bass.Bass("TRN2", target_bir_lowering=False)
    k = K(nc)
    k.mksem("qS", "qG", "cV", "cS", "cT", "cG", "cc", "q0", "q1", "g0", "g1", "g2", "g3", "q2", "q3", "qW", "qL", "qY", "qYs", "qU", "qX", "qA", "cx0", "cx1", "cy0", "cy1", "qi0", "qi1", "qf0", "qf1", "qM", "cy2", "cy3")

    din = lambda n, s, t: nc.dram_tensor(n, list(s), t, kind="ExternalInput").ap()
    dint = lambda n, s, t: nc.dram_tensor(n, list(s), t, kind="Internal").ap()

    xT = din("xT", (D, L), F32)
    xTmine = din("xTmine", (D, MYT), F32)
    w_xin = din("w_xin", (D, HDD), F32)
    w_res = din("w_res", (D, DI), F32)
    w_x = din("w_x", (128, NDT, XD), F32)
    w_dt = din("w_dt", (DR, HDD), F32)
    b_dt = din("b_dt", (128, NDT), F32)
    a_log = din("a_log", (128, NDT, NS), F32)
    d_vecI = din("d_vec", (128, NDT), F32)
    conv_wI = din("conv_w", (128, NDT, KC), F32)
    conv_bI = din("conv_b", (128, NDT), F32)
    w_out = din("w_out", (DI, D), F32)
    m_sel = din("m_sel", (128, 2), F32)
    identI = din("ident", (128, 128), F32)
    wg_t = din("wg_t", (NH, 128, NXT, 128), I32)
    wu_t = din("wu_t", (NH, 128, NXT, 128), I32)
    wd_t = din("wd_t", (H, D), I32)
    outT = nc.dram_tensor("outT", [D, MYT], F32, kind="ExternalOutput").ap()

    d_u = dint("d_u", (HDD, L), BF16)
    NARP = g["NCHB"]
    d_xdblp_p = [dint(f"d_xdblp{p}", (XD, cfg["CTB"]), F32) for p in range(NARP)]
    d_xdbl_p = [dint(f"d_xdbl{p}", (XD, cfg["CTB"]), F32) for p in range(NARP)]
    d_bc = dint("d_bc", (2 * NS, L), BF16)
    NAGP = g["NCHB"] // 2
    d_y_p = [dint(f"d_y{p}", (2, HDD, cfg["CTB"]), BF16) for p in range(NAGP)]
    d_yg_p = [dint(f"d_yg{p}", (2, 2, HDD, cfg["CTB"]), BF16) for p in range(NAGP)]
    d_wres = dint("d_wres", (D, DI), BF16)
    d_wout = dint("d_wout", (DI, D), BF16)
    d_wd = dint("d_wd", (H, D), BF16)
    d_ln = dint("d_ln", (2, CTC), BF16)
    d_x1 = dint("d_x1", (D, MYT), BF16)
    d_h = dint("d_h", (cfg["H"] // 128, 128, MYT), BF16)

    RG = [[0, 1], [2, 3], [4, 5], [6, 7]]

    psum = [k.stack.enter_context(nc.psum_tensor(f"pb{i}", [128, 512], F32))
            for i in range(8)]

    sb = k.stack.enter_context

    s_wx = sb(nc.sbuf_tensor("s_wx", [128, NDT, XD], BF16))
    s_wdt = sb(nc.sbuf_tensor("s_wdt", [DR, HDD], BF16))
    s_bdt = sb(nc.sbuf_tensor("s_bdt", [128, NDT], F32))
    s_A = sb(nc.sbuf_tensor("s_A", [128, NDT, NS], F32))
    s_D = sb(nc.sbuf_tensor("s_D", [128, NDT], F32))
    s_cw = sb(nc.sbuf_tensor("s_cw", [128, NDT, KC], F32))
    s_cb = sb(nc.sbuf_tensor("s_cb", [128, NDT], F32))
    s_carry = sb(nc.sbuf_tensor("s_carry", [128, NDT, NS], F32))
    s_ones = sb(nc.sbuf_tensor("s_ones", [128, 1], BF16))
    s_wxin = sb(nc.sbuf_tensor("s_wxin", [128, NXT, HDD], BF16))
    s_msel = sb(nc.sbuf_tensor("s_msel", [128, 2], F32))
    s_eps = sb(nc.sbuf_tensor("s_eps", [128, 1], F32))
    s_id = sb(nc.sbuf_tensor("s_id", [128, 128], BF16))

    WSLA = min(6, max(NDI, NXT)) * 128
    WTW = max(HDD, D, NXT * 128)
    f32_sz = max(2 * WTW + NDT * NS
                 + NXT * CTA + CTA + 7 * CTB,
                 2 * NXT * CTC + 4 * CTC) + 64 * 14
    b16_sz = max(2 * WTW
                 + (NXT + NDT + 2) * CTA + NDT * (KC - 1 + CTA) + (11 + 4 * NDT) * CTB,
                 (NXT + 5) * CTC
                 + max(4 * WSLA + (3 * NXT + 2 * NDI + 4 * NDT) * CTC,
                       (NXT + 2) * MYT + 2 * CTC + 8 * NXT * 128,
                       4 * WSLA + (3 * NXT + 2 * NH) * CTC)) + 64 * 24
    AF32 = Arena(k, "ar_f32", F32, f32_sz)
    AB16 = Arena(k, "ar_b16", BF16, b16_sz)
    AI32 = Arena(k, "ar_i32", I32, 2 * 1024 + 64)

    r128 = lambda ap: ap.rearrange("(t p) x -> p t x", p=128)

    if True:
        w_tmpf = AF32.alloc([128, 2, WTW])
        w_tmpi = AI32.alloc([128, 2, 1024])
        w_tmpb = AB16.alloc([128, 2, WTW])
        s_alog = AF32.alloc([128, NDT, NS])
        s_wxf = w_tmpf[:, 0, 0:NDT * XD].rearrange("p (t x) -> p t x", t=NDT)
        s_wdtf = w_tmpf[0:DR, 1, 0:HDD]

        def ld(dst_ap, src_ap):
            k.op("sync", "qS", lambda e, d=dst_ap, s=src_ap: e.dma_start(out=d, in_=s), 16)

        ld(s_alog, a_log)
        ld(s_bdt[:], b_dt)
        ld(s_D[:], d_vecI)
        ld(s_cw[:], conv_wI)
        ld(s_cb[:], conv_bI)
        ld(s_wxf, w_x.rearrange("p t x -> p (t x)").rearrange(
            "p (t x) -> p t x", t=NDT))
        ld(s_wdtf, w_dt)
        ld(s_msel[:], m_sel)
        vq = k.cnt["qS"]
        k.wait("vector", "qS", vq)
        k.wait("scalar", "qS", vq)
        k.op("scalar", "cS", lambda e: e.activation(out=s_A[:], in_=s_alog[:], func=AF.Exp))
        k.wait("vector", "cS", k.cnt["cS"])
        k.op("vector", "cV", lambda e: e.tensor_scalar_mul(s_A[:], s_A[:], -1.0))
        k.op("vector", "cV", lambda e: e.tensor_copy(s_wx[:], s_wxf))
        k.op("vector", "cV", lambda e: e.tensor_copy(s_wdt[:], s_wdtf))
        k.op("vector", "cV", lambda e: e.memset(s_carry[:], 0.0))
        k.op("vector", "cV", lambda e: e.memset(s_ones[:], 1.0))
        k.op("vector", "cV", lambda e: e.memset(s_eps[:], EPS))
        wvc = k.cnt["cV"]

        k.wait("sync", "cV", wvc)
        k.op("sync", "qS", lambda e: e.dma_start(
            out=w_tmpf[:, 0, 0:128], in_=identI), 16)
        k.wait("vector", "qS", k.cnt["qS"])
        k.op("vector", "cV", lambda e: e.tensor_copy(s_id[:], w_tmpf[:, 0, 0:128]))

        wxq = {0: k.cnt["cV"], 1: wvc}
        for i in range(D // 128):
            par = i % 2
            tin = w_tmpf[:, par, 0:HDD]
            qn = f"q{2 + par}"
            k.wait("sync", "cV", wxq[par])
            k.op("sync", qn, lambda e, t=tin, i=i: e.dma_start(
                out=t, in_=w_xin[i * 128:(i + 1) * 128, :]), 16)
            k.wait("vector", qn, k.cnt[qn])
            k.op("vector", "cV", lambda e, i=i, t=tin: e.tensor_copy(
                s_wxin[:, i, :], t))
            wxq[par] = k.cnt["cV"]

        w_tiles = []
        for hh in range(2):
            for i in range(D // 128):
                w_tiles.append((w_res[i * 128:(i + 1) * 128,
                                      hh * (DI // 2):(hh + 1) * (DI // 2)],
                                d_wres[i * 128:(i + 1) * 128,
                                       hh * (DI // 2):(hh + 1) * (DI // 2)],
                                DI // 2, False))
        for i in range(DI // 128):
            w_tiles.append((w_out[i * 128:(i + 1) * 128, :],
                            d_wout[i * 128:(i + 1) * 128, :], D, False))
        for i in range(H // 128):
            w_tiles.append((wd_t[i * 128:(i + 1) * 128, :],
                            d_wd[i * 128:(i + 1) * 128, :], D, True))
        wst = dict(t=0, iss=0, q={}, store={0: 0, 1: 0}, slot={})

        def issue_w_load(t):
            if t >= len(w_tiles):
                return
            src, dst, ncols, is_int = w_tiles[t]
            par = t % 2
            tin = (w_tmpi if is_int else w_tmpf)[:, par, 0:ncols]
            qn = f"qi{par}" if is_int else f"qf{par}"
            if not is_int and not wst.get(("fgate", par)):
                k.wait("scalar", "cV", wxq[par])
                wst[("fgate", par)] = True
            k.wait("scalar", "cS", wst["slot"].get((is_int, par), 0))
            k.op("scalar", qn, lambda e, t_=tin, s=src: e.dma_start(
                out=t_, in_=s), 16, nb=True)
            wst["q"][t] = (qn, k.cnt[qn])

        def emit_w_tiles(n):
            while wst["iss"] < min(wst["t"] + 2, len(w_tiles)):
                issue_w_load(wst["iss"])
                wst["iss"] += 1
            for _ in range(n):
                if wst["t"] >= len(w_tiles):
                    return
                t = wst["t"]
                wst["t"] += 1
                src, dst, ncols, is_int = w_tiles[t]
                par = t % 2
                tin = (w_tmpi if is_int else w_tmpf)[:, par, 0:ncols]
                tout = w_tmpb[:, par, 0:ncols]
                gn = f"g{par}"
                qn, qv = wst["q"][t]
                k.wait("scalar", qn, qv)
                k.wait("scalar", gn, wst["store"][par])
                k.op("scalar", "cS", lambda e, o=tout, t_=tin: e.activation(
                    out=o, in_=t_, func=AF.Copy), nb=True)
                wst["slot"][(is_int, par)] = k.cnt["cS"]
                k.wait("gpsimd", "cS", k.cnt["cS"])
                k.op("gpsimd", gn, lambda e, d=dst, t_=tout: e.dma_start(
                    out=d, in_=t_), 16)
                wst["store"][par] = k.cnt[gn]
                if wst["iss"] < len(w_tiles):
                    issue_w_load(wst["iss"])
                    wst["iss"] += 1

        NWT = len(w_tiles)

    k.barrier()

    if True:
        a_x = AF32.alloc([128, NXT, CTA])
        a_xb = AB16.alloc([128, NXT, CTA])
        a_xin = AB16.alloc([128, NDT, KC - 1 + CTA])
        a_conv = AB16.alloc([128, 2, CTA])
        a_u = AB16.alloc([128, NDT, CTA])
        a_xdbl = AF32.alloc([XD, CTA])
        b_xdbl = AF32.alloc([XD, CTB])
        b_dtb = AB16.alloc([DR, CTB])
        b_r = AB16.alloc([128, CTB])
        b_a = AF32.alloc([128, CTB])
        b_l = AF32.alloc([128, CTB])
        b_dbf = AB16.alloc([128, NDT, CTB])
        b_du = AB16.alloc([128, NDT, CTB])
        b_u = AB16.alloc([128, NDT, CTB])
        b_bcb = AB16.alloc([2 * NS, CTB])
        b_brep = AB16.alloc([128, 2, CTB])
        b_crep = AB16.alloc([128, 2, CTB])
        b_dA = AF32.alloc([128, 4, CTB])
        b_h = AB16.alloc([128, 2, CTB])
        b_tmp = AB16.alloc([128, 2, CTB])
        b_y = AB16.alloc([128, NDT, CTB])
        xT_t = xT.rearrange("(t p) l -> p t l", p=128)
        d_u_t = d_u.rearrange("(t p) l -> p t l", p=128)

        ast = dict(xb_v=0, evac_s=0, conv_v=0, wx_t=0, ustore=[], xstore=[])

        def emit_A(c):
            lo = c * CTA
            evac_m = {}
            k.wait("sync", "cV", ast["xb_v"])
            k.op("sync", "qA",
                 lambda e, lo=lo: e.dma_start(out=a_x, in_=xT_t[:, :, lo:lo + CTA]), 16)
            k.wait("vector", "qA", k.cnt["qA"])
            k.wait("vector", "cS", ast["evac_s"])
            if c == 0:
                k.op("vector", "cV", lambda e: e.memset(a_xin[:, :, 0:KC - 1], 0.0))
            else:
                k.op("vector", "cV", lambda e: e.tensor_copy(
                    a_xin[:, :, 0:KC - 1], a_xin[:, :, CTA:CTA + KC - 1]))
            halo_v = k.cnt["cV"]
            k.op("vector", "cV", lambda e: e.tensor_copy(a_xb, a_x))
            ast["xb_v"] = k.cnt["cV"]

            k.wait("tensor", "cV", ast["xb_v"])
            for m in range(NDT):
                pi = m % 2
                k.psum_guard(pi)
                for kk in range(NXT):
                    k.op("tensor", "cT", lambda e, pi=pi, kk=kk, m=m: e.matmul(
                        psum[pi][:, 0:CTA],
                        s_wxin[:, kk, m * 128:(m + 1) * 128],
                        a_xb[:, kk, :],
                        start=(kk == 0), stop=(kk == NXT - 1)))
                mm_t = k.cnt["cT"]
                k.wait("scalar", "cT", mm_t)
                k.wait("scalar", "cV", halo_v)
                k.op("scalar", "cS", lambda e, m=m, pi=pi: e.activation(
                    out=a_xin[:, m, KC - 1:KC - 1 + CTA], in_=psum[pi][:, 0:CTA],
                    func=AF.Copy))
                evac_m[m] = k.cnt["cS"]
                k.psum_set_free(pi, ("cS", k.cnt["cS"]))
            ast["evac_s"] = k.cnt["cS"]

            silu_done = {}
            for m in range(NDT):
                cp = m % 2
                k.wait("vector", "cS", evac_m[m])
                if m >= 2:
                    k.wait("vector", "cS", silu_done[m - 2])
                k.op("vector", "cV", lambda e, m=m, cp=cp: e.tensor_scalar(
                    out=a_conv[:, cp, :], in0=a_xin[:, m, 0:CTA],
                    scalar1=s_cw[:, m, 0:1], scalar2=s_cb[:, m:m + 1],
                    op0=OP.mult, op1=OP.add))
                for t in range(1, KC):
                    k.op("vector", "cV", lambda e, m=m, t=t, cp=cp: e.scalar_tensor_tensor(
                        out=a_conv[:, cp, :], in0=a_xin[:, m, t:t + CTA],
                        scalar=s_cw[:, m, t:t + 1], in1=a_conv[:, cp, :],
                        op0=OP.mult, op1=OP.add))
                k.wait("scalar", "cV", k.cnt["cV"])
                k.wait("scalar", "cT", ast["wx_t"])
                if ast["ustore"]:
                    k.wait("scalar", "qU", ast["ustore"][-1])
                k.op("scalar", "cS", lambda e, m=m, cp=cp: e.activation(
                    out=a_u[:, m, :], in_=a_conv[:, cp, :], func=AF.Silu), nb=True)
                silu_done[m] = k.cnt["cS"]
            silu_s = k.cnt["cS"]

            k.wait("tensor", "cS", silu_s)
            k.psum_guard(2)
            for kk in range(NDT):
                k.op("tensor", "cT", lambda e, kk=kk: e.matmul(
                    psum[2][0:XD, 0:CTA], s_wx[:, kk, :], a_u[:, kk, :],
                    start=(kk == 0), stop=(kk == NDT - 1)))
            ast["wx_t"] = k.cnt["cT"]
            k.wait("scalar", "cT", ast["wx_t"])
            if ast["xstore"]:
                k.wait("scalar", "qX", ast["xstore"][-1])
            k.op("scalar", "cS", lambda e: e.activation(
                out=a_xdbl[:], in_=psum[2][0:XD, 0:CTA], func=AF.Copy))
            k.psum_set_free(2, ("cS", k.cnt["cS"]))

            k.wait("gpsimd", "cS", k.cnt["cS"])
            k.op("gpsimd", "qU",
                 lambda e, lo=lo: e.dma_start(out=d_u_t[:, :, lo:lo + CTA], in_=a_u), 16)
            ast["ustore"].append(k.cnt["qU"])
            pp = c // (CTB // CTA)
            off = (c % (CTB // CTA)) * CTA
            k.op("gpsimd", "qX",
                 lambda e, pp=pp, off=off: e.dma_start(
                     out=d_xdblp_p[pp][:, off:off + CTA], in_=a_xdbl), 16)
            ast["xstore"].append(k.cnt["qX"])

        ar_cc = {}
        ag_cc = {}

        def emit_AR(p):
            k.wait("gpsimd", "qU", k.cnt["qU"])
            k.wait("gpsimd", "qX", k.cnt["qX"])
            ar_cc[p] = k.op("gpsimd", "cc", lambda e, p=p: e.collective_compute(
                "AllReduce", OP.add, replica_groups=RG,
                ins=[d_xdblp_p[p]], outs=[d_xdbl_p[p]]))

        scan_done = {}
        add_done = {}
        bst = dict(prev_yinit=0, prev_store=0)
        HTB = CTB // 2
        assert HTB <= 512
        OFFM = dict(zip(list(range(NDT))[-2:], ((5, 6), (7, 3))))
        idmm = {}

        def emit_B(c):
            lo = c * CTB
            k.wait("sync", "cc", ar_cc[c])
            k.wait("sync", "cV", bst["prev_yinit"])
            k.op("sync", "qS", lambda e, c=c: e.dma_start(
                out=b_xdbl, in_=d_xdbl_p[c][:]), 16)
            k.wait("sync", "qU", ast["ustore"][2 * c + 1])
            k.op("sync", "qS", lambda e, lo=lo: e.dma_start(
                out=b_u, in_=d_u_t[:, :, lo:lo + CTB]), 16)
            k.wait("vector", "qS", k.cnt["qS"])
            k.op("vector", "cV", lambda e: e.tensor_copy(b_dtb, b_xdbl[0:DR, :]))
            k.op("vector", "cV", lambda e: e.tensor_copy(b_bcb, b_xdbl[DR:XD, :]))
            cast_v = k.cnt["cV"]
            k.wait("gpsimd", "cV", cast_v)
            k.op("gpsimd", "qG", lambda e, lo=lo: e.dma_start(
                out=d_bc[:, lo:lo + CTB], in_=b_bcb), 16)
            bc_q = k.cnt["qG"]

            k.wait("tensor", "cV", cast_v)
            for m in range(NDT):
                def emit_dbf(m, sl, pln):
                    k.wait("vector", "cS", pln)
                    k.op("vector", "cV", lambda e, m=m, sl=sl: e.tensor_tensor(
                        out=b_dbf[:, m, sl], in0=b_r[:, sl], in1=b_l[:, sl],
                        op=OP.add))
                    k.op("vector", "cV", lambda e, m=m, sl=sl: e.tensor_tensor(
                        out=b_du[:, m, sl], in0=b_dbf[:, m, sl],
                        in1=b_u[:, m, sl], op=OP.mult))

                hp = None
                for ts in range(NTB):
                    pi = 3 + (ts % 2)
                    sl = slice(ts * TSB, (ts + 1) * TSB)
                    k.psum_guard(pi)
                    k.op("tensor", "cT", lambda e, m=m, sl=sl, pi=pi: e.matmul(
                        psum[pi][:, 0:TSB], s_wdt[:, m * 128:(m + 1) * 128],
                        b_dtb[:, sl], start=True, stop=True))
                    mm_t = k.cnt["cT"]
                    k.wait("vector", "cT", mm_t)
                    k.op("vector", "cV", lambda e, pi=pi, sl=sl, m=m: e.tensor_scalar(
                        out=b_r[:, sl], in0=psum[pi][:, 0:TSB],
                        scalar1=s_bdt[:, m:m + 1], scalar2=0.0,
                        op0=OP.add, op1=OP.max))
                    k.op("vector", "cV", lambda e, pi=pi, sl=sl, m=m: e.tensor_scalar(
                        out=b_a[:, sl], in0=psum[pi][:, 0:TSB],
                        scalar1=s_bdt[:, m:m + 1], scalar2=None, op0=OP.add))
                    k.psum_set_free(pi, ("cV", k.cnt["cV"]))
                    k.op("vector", "cV", lambda e, sl=sl: e.scalar_tensor_tensor(
                        out=b_a[:, sl], in0=b_r[:, sl], scalar=-2.0,
                        in1=b_a[:, sl], op0=OP.mult, op1=OP.add))
                    stt_v = k.cnt["cV"]
                    k.wait("scalar", "cV", stt_v)
                    k.op("scalar", "cS", lambda e, sl=sl: e.activation(
                        out=b_l[:, sl], in_=b_a[:, sl], func=AF.Exp), nb=True)
                    k.op("scalar", "cS", lambda e, sl=sl: e.activation(
                        out=b_l[:, sl], in_=b_l[:, sl], func=AF.Ln, bias=1.0))
                    if hp is not None:
                        emit_dbf(m, hp[0], hp[1])
                    hp = (sl, k.cnt["cS"])
                emit_dbf(m, hp[0], hp[1])
            du_v = k.cnt["cV"]

            k.wait("vector", "qYs", bst["prev_store"])
            for m in range(NDT):
                k.op("vector", "cV", lambda e, m=m: e.tensor_scalar(
                    out=b_y[:, m, :], in0=b_u[:, m, :], scalar1=s_D[:, m:m + 1],
                    scalar2=None, op0=OP.mult))
            bst["prev_yinit"] = k.cnt["cV"]

            k.wait("sync", "qG", bc_q)
            for n in range(NS):
                par = n % 2
                qn = f"q{par}"
                gn = c * NS + n
                if gn >= 2:
                    k.wait("sync", "cV", scan_done[(gn - 2, "mul", NDT - 1)])
                k.op("sync", qn, lambda e, n=n, lo=lo, par=par: e.dma_start(
                    out=b_brep[:, par, :],
                    in_=d_bc[n:n + 1, lo:lo + CTB].broadcast_to([128, CTB])), 16)
                k.op("sync", qn, lambda e, n=n, lo=lo, par=par: e.dma_start(
                    out=b_crep[:, par, :],
                    in_=d_bc[NS + n:NS + n + 1, lo:lo + CTB].broadcast_to([128, CTB])), 16)
                rep_q = k.cnt[qn]
                for mp in range(0, NDT, 2):
                    ms = [m for m in (mp, mp + 1) if m < NDT]
                    dA_s = {}
                    for m in ms:
                        u = gn * NDT + m
                        up = u % 4
                        if u >= 4:
                            k.wait("scalar", "cV", scan_done[((u - 4) // NDT, "scan", (u - 4) % NDT)])
                        k.wait("scalar", "cV", du_v)
                        k.op("scalar", "cS", lambda e, m=m, n=n, up=up: e.activation(
                            out=b_dA[:, up, :], in_=b_dbf[:, m, :], func=AF.Exp,
                            scale=s_A[:, m, n:n + 1]), nb=True)
                        dA_s[m] = k.cnt["cS"]
                    k.wait("vector", qn, rep_q)
                    for m in ms:
                        u = gn * NDT + m
                        u2 = u % 2
                        if (u - 2) in idmm:
                            k.wait("vector", "cT", idmm[u - 2])
                        k.op("vector", "cV", lambda e, m=m, u2=u2, par=par: e.tensor_tensor(
                            out=b_tmp[:, u2, :], in0=b_du[:, m, :], in1=b_brep[:, par, :],
                            op=OP.mult))
                    for m in ms:
                        u = gn * NDT + m
                        u2 = u % 2
                        up = u % 4
                        k.wait("vector", "cS", dA_s[m])
                        k.op("vector", "cV", lambda e, m=m, n=n, up=up, u2=u2: e.tensor_tensor_scan(
                            out=b_h[:, u2, :], data0=b_dA[:, up, :], data1=b_tmp[:, u2, :],
                            initial=s_carry[:, m, n:n + 1],
                            op0=OP.mult, op1=OP.add))
                        scan_done[(gn, "scan", m)] = k.cnt["cV"]
                    if len(ms) == 2:
                        k.op("vector", "cV", lambda e, mp=mp, n=n: e.tensor_copy(
                            s_carry[:, mp:mp + 2, n:n + 1],
                            b_h[:, 0:2, CTB - 1:CTB]))
                    else:
                        m0 = ms[0]
                        u2 = (gn * NDT + m0) % 2
                        k.op("vector", "cV", lambda e, m0=m0, n=n, u2=u2: e.tensor_copy(
                            s_carry[:, m0, n:n + 1], b_h[:, u2, CTB - 1:CTB]))
                    for m in ms:
                        u2 = (gn * NDT + m) % 2
                        k.op("vector", "cV", lambda e, u2=u2, par=par: e.tensor_tensor(
                            out=b_tmp[:, u2, :], in0=b_h[:, u2, :], in1=b_crep[:, par, :],
                            op=OP.mult))
                        scan_done[(gn, "mul", m)] = k.cnt["cV"]
                    if ms[0] in OFFM:
                        for m in ms:
                            u = gn * NDT + m
                            u2 = u % 2
                            blo, bhi = OFFM[m]
                            if n == 0:
                                k.psum_guard(blo)
                                k.psum_guard(bhi)
                            k.wait("tensor", "cV", scan_done[(gn, "mul", m)])
                            k.op("tensor", "cT", lambda e, u2=u2, blo=blo, n=n: e.matmul(
                                psum[blo][:, 0:HTB], s_id[:], b_tmp[:, u2, 0:HTB],
                                start=(n == 0), stop=(n == NS - 1)))
                            k.op("tensor", "cT", lambda e, u2=u2, bhi=bhi, n=n: e.matmul(
                                psum[bhi][:, 0:HTB], s_id[:], b_tmp[:, u2, HTB:CTB],
                                start=(n == 0), stop=(n == NS - 1)))
                            idmm[u] = k.cnt["cT"]
                    elif len(ms) == 2:
                        k.op("vector", "cV", lambda e, mp=mp: e.tensor_tensor(
                            out=b_y[:, mp:mp + 2, :], in0=b_y[:, mp:mp + 2, :],
                            in1=b_tmp[:, 0:2, :], op=OP.add))
                    else:
                        m0 = ms[0]
                        u2 = (gn * NDT + m0) % 2
                        k.op("vector", "cV", lambda e, m0=m0, u2=u2: e.tensor_tensor(
                            out=b_y[:, m0, :], in0=b_y[:, m0, :], in1=b_tmp[:, u2, :],
                            op=OP.add))

            for m, (blo, bhi) in OFFM.items():
                k.wait("vector", "cT", idmm[(c * NS + NS - 1) * NDT + m])
                k.op("vector", "cV", lambda e, m=m, blo=blo: e.tensor_tensor(
                    out=b_y[:, m, 0:HTB], in0=b_y[:, m, 0:HTB],
                    in1=psum[blo][:, 0:HTB], op=OP.add))
                k.op("vector", "cV", lambda e, m=m, bhi=bhi: e.tensor_tensor(
                    out=b_y[:, m, HTB:CTB], in0=b_y[:, m, HTB:CTB],
                    in1=psum[bhi][:, 0:HTB], op=OP.add))
                k.psum_set_free(blo, ("cV", k.cnt["cV"]))
                k.psum_set_free(bhi, ("cV", k.cnt["cV"]))

            s = c // (NCHB // 2)
            p = c % (NCHB // 2)
            k.wait("gpsimd", "cV", k.cnt["cV"])
            k.op("gpsimd", "qYs", lambda e, s=s, p=p: e.dma_start(
                out=d_y_p[p][s].rearrange("(t p) l -> p t l", p=128),
                in_=b_y), 16)
            bst["prev_store"] = k.cnt["qYs"]
            if s == 1:
                k.wait("gpsimd", "qYs", k.cnt["qYs"])
                ag_cc[p] = k.op("gpsimd", "cc", lambda e, p=p: e.collective_compute(
                    "AllGather", OP.bypass, replica_groups=RG,
                    ins=[d_y_p[p]], outs=[d_yg_p[p]]))

        RAT = CTB // CTA
        WPC = -(-NWT // NCHB)
        for pp in range(min(2, NCHB)):
            emit_A(RAT * pp)
            emit_A(RAT * pp + 1)
            emit_AR(pp)
        for p in range(NCHB):
            if p + 2 < NCHB:
                emit_A(RAT * (p + 2))
                emit_A(RAT * (p + 2) + 1)
                emit_AR(p + 2)
            emit_B(p)
            emit_w_tiles(WPC)
        emit_w_tiles(NWT)

    k.barrier()
    AF32.reset(); AB16.reset(); AI32.reset()

    if True:
        c_sq = AB16.alloc([128, CTC])
        c_mean = AF32.alloc([1, CTC])
        c_msq = AF32.alloc([1, CTC])
        c_var = AF32.alloc([1, CTC])
        c_mr = AB16.alloc([128, CTC])
        c_rr = AB16.alloc([128, CTC])
        c_lnr0 = AB16.alloc([1, CTC])
        c_lnr1 = AB16.alloc([1, CTC])
        c_out = AF32.alloc([128, CTC])
        c_z = AB16.alloc([128, NXT, CTC])
        c2_f32_mark = AF32.off
        c2_b16_mark = AB16.off
        WSL1 = min(6, max(NDI, NXT)) * 128
        c_wsl = AB16.alloc([128, 4, WSL1])
        c_x = AF32.alloc([128, 2, NXT, CTC])
        c_xb = AB16.alloc([128, 2, NXT, CTC])
        c_y = AB16.alloc([128, NDI, CTC])
        c_yc = AB16.alloc([128, 4, NDT, CTC])
        c_res = AB16.alloc([128, NDI, CTC])
        c_x1 = AB16.alloc([128, NXT, CTC])
        xm_t = xTmine.rearrange("(t p) l -> p t l", p=128)
        outT_t = outT.rearrange("(t p) l -> p t l", p=128)
        dx1_t = d_x1.rearrange("(t p) l -> p t l", p=128)

        def grps(n):
            out, o = [], 0
            while o < n:
                sz = min(6, n - o)
                out.append((o, sz))
                o += sz
            return out

        def layernorm(src3, dst3, out_tile=None, mid=None, dst_wait=None,
                      ln_sync=False):
            k.psum_guard(6)
            k.psum_guard(7)
            k.wait("scalar", "cV", k.cnt["cV"])
            for i in range(NXT):
                k.wait("scalar", "cT", k.cnt["cT"])
                k.op("scalar", "cS", lambda e, i=i: e.activation(
                    out=c_sq, in_=src3[:, i, :], func=AF.Square))
                sq_s = k.cnt["cS"]
                k.wait("tensor", "cS", sq_s)
                k.op("tensor", "cT", lambda e, i=i: e.matmul(
                    psum[6][0:1, 0:CTC], s_ones[:], src3[:, i, :],
                    start=(i == 0), stop=(i == NXT - 1)))
                k.op("tensor", "cT", lambda e, i=i: e.matmul(
                    psum[7][0:1, 0:CTC], s_ones[:], c_sq[:],
                    start=(i == 0), stop=(i == NXT - 1)))
            st_t = k.cnt["cT"]
            k.wait("vector", "cT", st_t)
            k.op("vector", "cV", lambda e: e.tensor_scalar_mul(
                c_mean, psum[6][0:1, 0:CTC], 1.0 / D))
            k.op("vector", "cV", lambda e: e.tensor_scalar_mul(
                c_msq, psum[7][0:1, 0:CTC], 1.0 / D))
            k.psum_set_free(6, ("cV", k.cnt["cV"]))
            k.psum_set_free(7, ("cV", k.cnt["cV"]))
            k.op("vector", "cV", lambda e: e.tensor_tensor(
                out=c_var, in0=c_mean, in1=c_mean, op=OP.mult))
            k.op("vector", "cV", lambda e: e.tensor_tensor(
                out=c_var, in0=c_msq, in1=c_var, op=OP.subtract))
            var_v = k.cnt["cV"]
            k.wait("scalar", "cV", var_v)
            k.op("scalar", "cS", lambda e: e.activation(
                out=c_var, in_=c_var, func=AF.Ln, bias=s_eps[0:1, 0:1]))
            k.op("scalar", "cS", lambda e: e.activation(
                out=c_msq, in_=c_var, func=AF.Exp, scale=-0.5))
            rows_s = k.cnt["cS"]
            k.wait("vector", "cS", rows_s)
            k.op("vector", "cV", lambda e: e.tensor_copy(c_lnr0, c_mean))
            k.op("vector", "cV", lambda e: e.tensor_copy(c_lnr1, c_msq))
            rows_v = k.cnt["cV"]
            if ln_sync:
                k.wait("sync", "cV", rows_v)
                k.wait("sync", "qL", k.cnt["qL"])
                k.op("sync", "qL", lambda e: e.dma_start(
                    out=d_ln[0:1, :], in_=c_lnr0), 16)
                k.op("sync", "qL", lambda e: e.dma_start(
                    out=d_ln[1:2, :], in_=c_lnr1), 16)
                k.wait("sync", "qL", k.cnt["qL"])
            else:
                k.wait("gpsimd", "cV", rows_v)
                k.wait("gpsimd", "qM", k.cnt["qM"])
                k.op("gpsimd", "qG", lambda e: e.dma_start(
                    out=d_ln[0:1, :], in_=c_lnr0), 16)
                k.op("gpsimd", "qG", lambda e: e.dma_start(
                    out=d_ln[1:2, :], in_=c_lnr1), 16)
                k.wait("scalar", "qG", k.cnt["qG"])
            leng = "sync" if ln_sync else "scalar"
            lsem = "qL" if ln_sync else "qM"
            k.op(leng, lsem, lambda e: e.dma_start(
                out=c_mr, in_=d_ln[0:1, :].broadcast_to([128, CTC])), 16, nb=True)
            k.op(leng, lsem, lambda e: e.dma_start(
                out=c_rr, in_=d_ln[1:2, :].broadcast_to([128, CTC])), 16, nb=True)
            if mid is not None:
                mid()
            k.wait("vector", lsem, k.cnt[lsem])
            if dst_wait is not None:
                k.wait("vector", dst_wait[0], dst_wait[1])
            for i in range(NXT):
                k.op("vector", "cV", lambda e, i=i: e.tensor_tensor(
                    out=dst3[:, i, :], in0=src3[:, i, :], in1=c_mr,
                    op=OP.subtract))
                if out_tile is None:
                    k.op("vector", "cV", lambda e, i=i: e.tensor_tensor(
                        out=dst3[:, i, :], in0=dst3[:, i, :], in1=c_rr,
                        op=OP.mult))
                else:
                    k.wait("vector", "qY", k.cnt["qY"])
                    k.op("vector", "cV", lambda e, i=i: e.tensor_tensor(
                        out=c_out[:], in0=dst3[:, i, :], in1=c_rr,
                        op=OP.mult))
                    ov = k.cnt["cV"]
                    k.wait("sync", "cV", ov)
                    k.op("sync", "qY",
                         lambda e, i=i: e.dma_start(out=out_tile(i), in_=c_out), 16)

        NPC = 4
        pieces = [(c, j, s) for c in range(NCHC) for j in range(2)
                  for s in range(2)]
        st = dict(zadd={-1: 0, -2: 0}, wout={-1: 0}, res={-1: 0, -2: 0},
                  selv={}, ycq={}, cast={}, xq={}, x1st={-1: 0}, silu_gm={},
                  g_gm={})
        slab = dict(i=0, done={})

        def slab_load(src, W):
            i = slab["i"]
            par4 = i % 4
            qn = f"q{par4}"
            if i >= 4:
                k.wait("sync", "cT", slab["done"][i - 4])
            k.op("sync", qn, lambda e, s=src, par4=par4, W=W: e.dma_start(
                out=c_wsl[:, par4, 0:W], in_=s), 16)
            k.wait("tensor", qn, k.cnt[qn])
            return par4

        def slab_mark():
            slab["done"][slab["i"]] = k.cnt["cT"]
            slab["i"] += 1

        def x_load(c):
            if c >= NCHC:
                return
            par = c % 2
            k.wait("scalar", "cV", st["zadd"][c - 2])
            k.op("scalar", f"cx{par}", lambda e, lo=c * CTC, par=par: e.dma_start(
                out=c_x[:, par], in_=xm_t[:, :, lo:lo + CTC]), 16, nb=True)
            st["xq"][c] = k.cnt[f"cx{par}"]

        def x_cast(c):
            if c >= NCHC:
                return
            par = c % 2
            k.wait("vector", f"cx{par}", st["xq"][c])
            k.wait("vector", "cT", st["res"][c - 2])
            k.op("vector", "cV", lambda e, par=par: e.tensor_copy(
                c_xb[:, par], c_x[:, par]))
            st["cast"][c] = k.cnt["cV"]

        def piece_load(gi):
            if gi >= len(pieces):
                return
            c, j, s = pieces[gi]
            lo = c * CTC
            pidx = lo // CTB
            poff = lo % CTB
            slot = gi % 4
            sn = f"cy{slot}"
            k.wait("scalar", "cV", st["selv"].get(gi - 4, 0))
            k.wait("scalar", "cc", ag_cc[pidx])
            k.op("scalar", sn, lambda e, j=j, s=s, pidx=pidx, poff=poff, slot=slot:
                 e.dma_start(out=c_yc[:, slot],
                             in_=d_yg_p[pidx][j, s, :, poff:poff + CTC]
                             .rearrange("(t p) l -> p t l", p=128)), 16, nb=True)
            st["ycq"][gi] = (sn, k.cnt[sn])

        def emit_select(c):
            for i in range(NPC):
                gi = NPC * c + i
                _, j, s = pieces[gi]
                slot = gi % 4
                sn, sv = st["ycq"][gi]
                k.wait("vector", sn, sv)
                k.wait("vector", "cT", st["wout"][c - 1])
                for m in range(NDT):
                    if s == 0:
                        k.op("vector", "cV", lambda e, j=j, m=m, slot=slot:
                             e.tensor_scalar(
                                 out=c_y[:, j * NDT + m, :],
                                 in0=c_yc[:, slot, m, :],
                                 scalar1=s_msel[:, 0:1], scalar2=None,
                                 op0=OP.mult))
                    else:
                        k.op("vector", "cV", lambda e, j=j, m=m, slot=slot:
                             e.scalar_tensor_tensor(
                                 out=c_y[:, j * NDT + m, :],
                                 in0=c_yc[:, slot, m, :],
                                 scalar=s_msel[:, 1:2],
                                 in1=c_y[:, j * NDT + m, :],
                                 op0=OP.mult, op1=OP.add))
                st["selv"][gi] = k.cnt["cV"]
                piece_load(gi + 4)

        def emit_res(c):
            par = c % 2
            k.wait("tensor", "cV", st["cast"][c])
            for (gof, gsz) in grps(NDI):
                for pi in range(gsz):
                    k.psum_guard(pi)
                GW = gsz * 128
                for kk in range(NXT):
                    par4 = slab_load(
                        d_wres[kk * 128:(kk + 1) * 128,
                               gof * 128:gof * 128 + GW], GW)
                    for m in range(gsz):
                        k.op("tensor", "cT", lambda e, m=m, kk=kk, par4=par4,
                             par=par: e.matmul(
                                 psum[m][:, 0:CTC],
                                 c_wsl[:, par4, m * 128:(m + 1) * 128],
                                 c_xb[:, par, kk, :],
                                 start=(kk == 0), stop=(kk == NXT - 1)))
                    slab_mark()
                last_mm = k.cnt["cT"]
                for m in range(gsz):
                    gm = gof + m
                    k.wait("scalar", "cT", last_mm)
                    k.op("scalar", "cS", lambda e, m=m, gm=gm: e.activation(
                        out=c_res[:, gm, :], in_=psum[m][:, 0:CTC],
                        func=AF.Silu))
                    st["silu_gm"][gm] = k.cnt["cS"]
                    k.psum_set_free(m, ("cS", k.cnt["cS"]))
            st["res"][c] = k.cnt["cT"]

        def emit_gw(c):
            par = c % 2
            for gm in range(NDI):
                k.wait("vector", "cS", st["silu_gm"][gm])
                k.op("vector", "cV", lambda e, gm=gm: e.tensor_tensor(
                    out=c_y[:, gm, :], in0=c_y[:, gm, :], in1=c_res[:, gm, :],
                    op=OP.mult))
                st["g_gm"][gm] = k.cnt["cV"]
            for (mof, msz) in grps(NXT):
                for pi in range(msz):
                    k.psum_guard(pi)
                MW = msz * 128
                for kk in range(NDI):
                    par4 = slab_load(
                        d_wout[kk * 128:(kk + 1) * 128,
                               mof * 128:mof * 128 + MW], MW)
                    k.wait("tensor", "cV", st["g_gm"][kk])
                    for m in range(msz):
                        k.op("tensor", "cT", lambda e, m=m, kk=kk, par4=par4:
                             e.matmul(
                                 psum[m][:, 0:CTC],
                                 c_wsl[:, par4, m * 128:(m + 1) * 128],
                                 c_y[:, kk, :],
                                 start=(kk == 0), stop=(kk == NDI - 1)))
                    slab_mark()
                k.wait("vector", "cT", k.cnt["cT"])
                k.wait("vector", f"cx{par}", st["xq"][c])
                for m in range(msz):
                    k.op("vector", "cV", lambda e, m=m, mof=mof, par=par:
                         e.tensor_tensor(
                             out=c_z[:, mof + m, :],
                             in0=c_x[:, par, mof + m, :],
                             in1=psum[m][:, 0:CTC], op=OP.add))
                    k.psum_set_free(m, ("cV", k.cnt["cV"]))
            st["wout"][c] = k.cnt["cT"]
            st["zadd"][c] = k.cnt["cV"]

        for gi0 in range(4):
            piece_load(gi0)
        x_load(0)
        for c in range(NCHC):
            x_load(c + 1)
            if c == 0:
                x_cast(0)
                emit_select(0)
            x_cast(c + 1)
            emit_res(c)
            emit_gw(c)
            layernorm(c_z, c_x1,
                      mid=(lambda c=c: emit_select(c + 1))
                      if c + 1 < NCHC else None,
                      dst_wait=("qYs", st["x1st"][c - 1]))
            k.wait("gpsimd", "cV", k.cnt["cV"])
            k.op("gpsimd", "qYs", lambda e, lo=c * CTC: e.dma_start(
                out=dx1_t[:, :, lo:lo + CTC], in_=c_x1), 16)
            st["x1st"][c] = k.cnt["qYs"]

        k.barrier()
        AF32.off = c2_f32_mark
        AB16.off = c2_b16_mark
        x1a = AB16.alloc([128, NXT, MYT])
        hmb = AB16.alloc([128, 2, MYT])
        sgp = AB16.alloc([128, 2, CTC])
        c_wg = AB16.alloc([128, 4, NXT, 128])
        c_wu = AB16.alloc([128, 4, NXT, 128])
        c_wi = AI32.alloc([128, 2, NXT * 128])
        k.op("sync", "qW", lambda e: e.dma_start(out=x1a, in_=dx1_t), 16)
        k.wait("tensor", "qW", k.cnt["qW"])
        wgi = wg_t.rearrange("m p t c -> m p (t c)")
        wui = wu_t.rearrange("m p t c -> m p (t c)")
        NTC2 = MYT // CTC
        ffn_mm = {}
        hst = {}
        wc = {0: 0, 1: 0}
        wgu_done = {}

        def emit_wgu_cast(m):
            if m >= NH:
                return
            par = m % 4
            for wh in range(2):
                pari = (2 * m + wh) % 2
                qn = f"q{pari}"
                k.wait("sync", "cS", wc[pari])
                k.op("sync", qn, lambda e, m=m, pari=pari, wh=wh: e.dma_start(
                    out=c_wi[:, pari], in_=(wgi if wh == 0 else wui)[m]), 16)
                k.wait("scalar", qn, k.cnt[qn])
                k.wait("scalar", "cT", ffn_mm.get(m - 4, 0))
                dst = (c_wg if wh == 0 else c_wu)[:, par].rearrange(
                    "p t c -> p (t c)")
                k.op("scalar", "cS", lambda e, dst=dst, pari=pari: e.activation(
                    out=dst, in_=c_wi[:, pari], func=AF.Copy), nb=True)
                wc[pari] = k.cnt["cS"]
            wgu_done[m] = k.cnt["cS"]

        emit_wgu_cast(0)
        emit_wgu_cast(1)
        for m in range(NH):
            par = m % 4
            emit_wgu_cast(m + 2)
            k.wait("tensor", "cS", wgu_done[m])
            hb = m % 2
            if m >= 2:
                k.wait("vector", hst[m - 2][0], hst[m - 2][1])
            for tc in range(NTC2):
                sl = slice(tc * CTC, (tc + 1) * CTC)
                pg = 4 + 2 * (tc % 2)
                pu = pg + 1
                k.psum_guard(pg)
                k.psum_guard(pu)
                for kk in range(NXT):
                    k.op("tensor", "cT", lambda e, kk=kk, par=par, sl=sl, pg=pg: e.matmul(
                        psum[pg][:, 0:CTC], c_wg[:, par, kk, :], x1a[:, kk, sl],
                        start=(kk == 0), stop=(kk == NXT - 1)))
                for kk in range(NXT):
                    k.op("tensor", "cT", lambda e, kk=kk, par=par, sl=sl, pu=pu: e.matmul(
                        psum[pu][:, 0:CTC], c_wu[:, par, kk, :], x1a[:, kk, sl],
                        start=(kk == 0), stop=(kk == NXT - 1)))
                mm_t = k.cnt["cT"]
                k.wait("scalar", "cT", mm_t)
                sp = tc % 2
                k.op("scalar", "cS", lambda e, sp=sp, pg=pg: e.activation(
                    out=sgp[:, sp, :], in_=psum[pg][:, 0:CTC], func=AF.Silu, scale=QS))
                sg_s = k.cnt["cS"]
                k.psum_set_free(pg, ("cS", sg_s))
                k.wait("vector", "cS", sg_s)
                k.op("vector", "cV", lambda e, hb=hb, sl=sl, sp=sp, pu=pu: e.scalar_tensor_tensor(
                    out=hmb[:, hb, sl], in0=psum[pu][:, 0:CTC], scalar=QS,
                    in1=sgp[:, sp, :], op0=OP.mult, op1=OP.mult))
                k.psum_set_free(pu, ("cV", k.cnt["cV"]))
            ffn_mm[m] = k.cnt["cT"]
            k.wait("gpsimd", "cV", k.cnt["cV"])
            hq = "qU" if hb == 0 else "qX"
            k.op("gpsimd", hq, lambda e, m=m, hb=hb: e.dma_start(
                out=d_h[m], in_=hmb[:, hb, :]), 16)
            hst[m] = (hq, k.cnt[hq])

        k.barrier()
        AF32.off = c2_f32_mark
        AB16.off = c2_b16_mark
        c_z2 = AB16.alloc([128, NXT, CTC])
        c_wd = AB16.alloc([128, 4, WSL1])
        htc = AB16.alloc([128, 2, NH, CTC])
        x1c = AB16.alloc([128, 2, NXT, CTC])
        d_h_t = d_h.rearrange("m p l -> p m l")
        st2 = dict(dn={-1: 0, -2: 0}, z2={-1: 0, -2: 0})
        slab2 = dict(i=0, done={})

        def slab2_load(src, W):
            i = slab2["i"]
            par4 = i % 4
            gn = f"g{par4}"
            if i >= 4:
                k.wait("gpsimd", "cT", slab2["done"][i - 4])
            dst = c_wd[:, par4, 0:W]
            if len(src.shape) == 3:
                dst = dst.rearrange("p (a c) -> p a c", a=src.shape[1])
            k.op("gpsimd", gn, lambda e, s=src, d=dst: e.dma_start(
                out=d, in_=s), 16)
            k.wait("tensor", gn, k.cnt[gn])
            return par4

        def slab2_mark():
            slab2["done"][slab2["i"]] = k.cnt["cT"]
            slab2["i"] += 1

        for tc in range(NTC2):
            lo = tc * CTC
            hb = tc % 2
            hq = "qU" if hb == 0 else "qX"
            k.wait("gpsimd", "cT", st2["dn"][tc - 2])
            k.wait("gpsimd", "cV", st2["z2"][tc - 2])
            k.op("gpsimd", hq, lambda e, lo=lo, hb=hb: e.dma_start(
                out=htc[:, hb], in_=d_h_t[:, :, lo:lo + CTC]), 16)
            k.op("gpsimd", hq, lambda e, lo=lo, hb=hb: e.dma_start(
                out=x1c[:, hb], in_=dx1_t[:, :, lo:lo + CTC]), 16)
            lq = k.cnt[hq]
            k.wait("tensor", hq, lq)
            for (mof, msz) in grps(NXT):
                for pi in range(msz):
                    k.psum_guard(pi)
                MW = msz * 128
                KG = 2 if (2 * MW <= WSL1 and NH % 2 == 0) else 1
                for kk0 in range(0, NH, KG):
                    par4 = slab2_load(
                        d_wd[kk0 * 128:(kk0 + KG) * 128,
                             mof * 128:mof * 128 + MW]
                        .rearrange("(a p) c -> p a c", p=128), KG * MW)
                    for kl in range(KG):
                        kk = kk0 + kl
                        for m in range(msz):
                            k.op("tensor", "cT", lambda e, m=m, kk=kk, kl=kl,
                                 par4=par4, hb=hb, MW=MW: e.matmul(
                                     psum[m][:, 0:CTC],
                                     c_wd[:, par4,
                                          kl * MW + m * 128:kl * MW + (m + 1) * 128],
                                     htc[:, hb, kk, :],
                                     start=(kk == 0), stop=(kk == NH - 1)))
                    slab2_mark()
                k.wait("vector", "cT", k.cnt["cT"])
                k.wait("vector", hq, lq)
                for m in range(msz):
                    k.op("vector", "cV", lambda e, m=m, mof=mof, hb=hb:
                         e.scalar_tensor_tensor(
                             out=c_z2[:, mof + m, :], in0=psum[m][:, 0:CTC],
                             scalar=QS, in1=x1c[:, hb, mof + m, :],
                             op0=OP.mult, op1=OP.add))
                    k.psum_set_free(m, ("cV", k.cnt["cV"]))
            st2["dn"][tc] = k.cnt["cT"]
            st2["z2"][tc] = k.cnt["cV"]
            layernorm(c_z2, c_z,
                      out_tile=lambda i, lo=lo: outT_t[:, i, lo:lo + CTC],
                      ln_sync=True)

    for q in ("qG", "qS", "q0", "q1", "g0", "g1", "g2", "g3", "q2", "q3", "qW", "qL", "qY", "qYs", "qU", "qX", "qA", "cx0", "cx1", "cy0", "cy1", "qi0", "qi1", "qf0", "qf1", "qM", "cy2", "cy3"):
        k.wait("gpsimd", q, k.cnt[q])
    k.play()
    k.stack.close()
    return nc


_NC_CACHE = {}


def _get_nc(key, cfg):
    if key not in _NC_CACHE:
        _NC_CACHE[key] = build_nc(cfg)
    return _NC_CACHE[key]


def make_inputs(cfg, x, W_in, conv_w, conv_b, W_x, W_dt, b_dt, A_log, Dv, W_out,
                wg, wu, wd):
    g = _dv(cfg)
    DI, NS, DR, H = (g[k] for k in ("DI", "NS", "DR", "H"))
    HDD, MYT, NH, NXT = g["HDD"], g["MYT"], g["NH"], g["NXT"]
    NDT, XD, KC = g["NDT"], g["XD"], cfg["KC"]
    C = np.ascontiguousarray
    wgT = wg.T
    wuT = wu.T
    wdT = C(wd.T)
    wg_t = C(wgT.reshape(NXT, 128, NH, 128).transpose(2, 1, 0, 3))
    wu_t = C(wuT.reshape(NXT, 128, NH, 128).transpose(2, 1, 0, 3))
    w_res_c = C(W_in[:, DI:])
    w_out_c = C(W_out)
    ins = []
    for c in range(8):
        b, h = c >> 1, c & 1
        dd = slice(h * HDD, (h + 1) * HDD)
        ins.append({
            "xT": C(x[b].T),
            "xTmine": C(x[b, h * MYT:(h + 1) * MYT, :].T),
            "w_xin": C(W_in[:, dd]),
            "w_res": w_res_c,
            "w_x": C(W_x[dd, :].reshape(NDT, 128, XD).transpose(1, 0, 2)),
            "w_dt": C(W_dt[:, dd]),
            "b_dt": C(b_dt[dd].reshape(NDT, 128).T),
            "a_log": C(A_log[dd, :].reshape(NDT, 128, NS).transpose(1, 0, 2)),
            "d_vec": C(Dv[dd].reshape(NDT, 128).T),
            "conv_w": C(conv_w[dd, :].reshape(NDT, 128, KC).transpose(1, 0, 2)),
            "conv_b": C(conv_b[dd].reshape(NDT, 128).T),
            "w_out": w_out_c,
            "m_sel": C(np.tile(np.array([[1.0 - h, float(h)]], np.float32), (128, 1))),
            "ident": np.eye(128, dtype=np.float32),
            "wg_t": wg_t,
            "wu_t": wu_t,
            "wd_t": wdT,
        })
    return ins


def run(cfg, key, x, W_in, conv_w, conv_b, W_x, W_dt, b_dt, A_log, Dv, W_out,
        wg, wu, wd, runner=None):
    g = _dv(cfg)
    nc = _get_nc(key, cfg)
    ins = make_inputs(cfg, x, W_in, conv_w, conv_b, W_x, W_dt, b_dt, A_log,
                      Dv, W_out, wg, wu, wd)
    if runner is None:
        res = run_bass_kernel_spmd(nc, ins, core_ids=list(range(8)))
        outs = [np.asarray(r["outT"]) for r in res.results]
    else:
        outs = runner(nc, ins)
    B, L, D, MYT = x.shape[0], cfg["L"], cfg["D"], g["MYT"]
    out = np.empty((B, L, D), np.float32)
    for c in range(8):
        b, h = c >> 1, c & 1
        out[b, h * MYT:(h + 1) * MYT, :] = outs[c].T
    return out


def kernel(L0_attn_x, L0_attn_W_in, L0_attn_conv_w, L0_attn_conv_b, L0_attn_W_x,
           L0_attn_W_dt, L0_attn_b_dt, L0_attn_A_log, L0_attn_D, L0_attn_W_out,
           L0_ffn_w_gate, L0_ffn_w_up, L0_ffn_w_down):
    return run(FULL_CFG, "full", L0_attn_x, L0_attn_W_in, L0_attn_conv_w,
               L0_attn_conv_b, L0_attn_W_x, L0_attn_W_dt, L0_attn_b_dt,
               L0_attn_A_log, L0_attn_D, L0_attn_W_out,
               L0_ffn_w_gate, L0_ffn_w_up, L0_ffn_w_down)


# version 50
# speedup vs baseline: 1.0352x; 1.0077x over previous
from contextlib import ExitStack

import numpy as np

import concourse.bass as bass
import concourse.mybir as mybir
from concourse.bass_utils import run_bass_kernel_spmd

F32 = mybir.dt.float32
BF16 = mybir.dt.bfloat16
I32 = mybir.dt.int32
AF = mybir.ActivationFunctionType
OP = mybir.AluOpType

FULL_CFG = dict(L=8192, D=1024, DI=2048, NS=16, DR=64, KC=4, H=3072,
                CTA=512, CTB=1024, CTC=512, EPS=1e-5, QSCALE=1.0 / 127.0)


def _dv(cfg):
    d = dict(cfg)
    d["HDD"] = cfg["DI"] // 2
    d["NDT"] = d["HDD"] // 128
    d["NXT"] = cfg["D"] // 128
    d["XD"] = cfg["DR"] + 2 * cfg["NS"]
    d["MYT"] = cfg["L"] // 2
    d["NCHA"] = cfg["L"] // cfg["CTA"]
    d["NCHB"] = cfg["L"] // cfg["CTB"]
    d["NCHC"] = d["MYT"] // cfg["CTC"]
    d["NH"] = cfg["H"] // 128
    d["NDI"] = cfg["DI"] // 128
    d["TSB"] = min(512, cfg["CTB"])
    d["NTB"] = cfg["CTB"] // d["TSB"]
    assert d["XD"] <= 128 and cfg["DR"] <= 128
    assert cfg["CTA"] <= 512 and cfg["CTC"] <= 512
    assert d["NCHB"] % 2 == 0
    return d


class K:

    ENGS = ("sync", "gpsimd", "vector", "scalar", "tensor")
    OWN = {"vector": "cV", "scalar": "cS", "gpsimd": "cG"}

    def __init__(self, nc):
        self.nc = nc
        self.ops = {e: [] for e in self.ENGS}
        self.stack = ExitStack()
        self.sem = {}
        self.cnt = {}
        self.waited = {e: {} for e in self.ENGS}
        self.psum_free = {}

    def mksem(self, *names):
        for n in names:
            self.sem[n] = self.stack.enter_context(self.nc.semaphore(n))
            self.cnt[n] = 0

    def wait(self, eng, semname, val):
        if val <= 0:
            return
        if self.waited[eng].get(semname, -1) >= val:
            return
        self.waited[eng][semname] = val
        s = self.sem[semname]
        self.ops[eng].append(lambda e, s=s, v=val: e.wait_ge(s, v))

    def op(self, eng, semname, build, inc=1, nb=False):
        own = self.OWN.get(eng)
        if not nb and own is not None and eng in ("vector", "scalar", "gpsimd"):
            self.wait(eng, own, self.cnt[own])
        self.cnt[semname] += inc
        s = self.sem[semname]
        self.ops[eng].append(lambda e, b=build, s=s, i=inc: b(e).then_inc(s, i))
        return self.cnt[semname]

    def psum_guard(self, idx):
        for sname, v in self.psum_free.get(idx, ()):
            self.wait("tensor", sname, v)

    def psum_set_free(self, idx, *pairs):
        self.psum_free[idx] = pairs

    def barrier(self, exclude=("cc",)):
        snap = dict(self.cnt)
        for e in self.ENGS:
            for sname, v in snap.items():
                if sname in exclude:
                    continue
                self.wait(e, sname, v)

    def play(self):
        nc = self.nc
        with nc.Block() as block:
            @block.sync
            def _(e):
                for f in self.ops["sync"]:
                    f(e)

            @block.gpsimd
            def _(e):
                for f in self.ops["gpsimd"]:
                    f(e)

            @block.vector
            def _(e):
                for f in self.ops["vector"]:
                    f(e)

            @block.scalar
            def _(e):
                for f in self.ops["scalar"]:
                    f(e)

            @block.tensor
            def _(e):
                for f in self.ops["tensor"]:
                    f(e)


class Arena:

    def __init__(self, k, name, dtype, n):
        self.t = k.stack.enter_context(k.nc.sbuf_tensor(name, [128, n], dtype))
        self.n = n
        self.off = 0

    def alloc(self, shape):
        p = shape[0]
        cols = 1
        for s in shape[1:]:
            cols *= s
        off = (self.off + 31) & ~31
        ap = self.t[0:p, off:off + cols]
        self.off = off + cols
        assert self.off <= self.n, (self.off, self.n, shape)
        if len(shape) > 2:
            names = " ".join(f"d{i}" for i in range(1, len(shape)))
            kw = {f"d{i}": shape[i] for i in range(1, len(shape))}
            ap = ap.rearrange(f"p ({names}) -> p {names}", **kw)
        return ap

    def reset(self):
        self.off = 0


def build_nc(cfg):
    g = _dv(cfg)
    L, D, DI, NS, DR, KC, H = (g[k] for k in ("L", "D", "DI", "NS", "DR", "KC", "H"))
    CTA, CTB, CTC, EPS, QS = (g[k] for k in ("CTA", "CTB", "CTC", "EPS", "QSCALE"))
    HDD, NDT, NXT, XD, MYT = (g[k] for k in ("HDD", "NDT", "NXT", "XD", "MYT"))
    NCHA, NCHB, NCHC, NH, NDI = (g[k] for k in ("NCHA", "NCHB", "NCHC", "NH", "NDI"))
    TSB, NTB = g["TSB"], g["NTB"]

    nc = bass.Bass("TRN2", target_bir_lowering=False)
    k = K(nc)
    k.mksem("qS", "qG", "cV", "cS", "cT", "cG", "cc", "q0", "q1", "g0", "g1", "g2", "g3", "q2", "q3", "qW", "qL", "qY", "qYs", "qU", "qX", "qA", "cx0", "cx1", "cy0", "cy1", "qi0", "qi1", "qf0", "qf1", "qM", "cy2", "cy3")

    din = lambda n, s, t: nc.dram_tensor(n, list(s), t, kind="ExternalInput").ap()
    dint = lambda n, s, t: nc.dram_tensor(n, list(s), t, kind="Internal").ap()

    xT = din("xT", (D, L), F32)
    xTmine = din("xTmine", (D, MYT), F32)
    w_xin = din("w_xin", (D, HDD), F32)
    w_res = din("w_res", (D, DI), F32)
    w_x = din("w_x", (128, NDT, XD), F32)
    w_dt = din("w_dt", (DR, HDD), F32)
    b_dt = din("b_dt", (128, NDT), F32)
    a_log = din("a_log", (128, NDT, NS), F32)
    d_vecI = din("d_vec", (128, NDT), F32)
    conv_wI = din("conv_w", (128, NDT, KC), F32)
    conv_bI = din("conv_b", (128, NDT), F32)
    w_out = din("w_out", (DI, D), F32)
    m_sel = din("m_sel", (128, 2), F32)
    identI = din("ident", (128, 128), F32)
    wg_t = din("wg_t", (NH, 128, NXT, 128), I32)
    wu_t = din("wu_t", (NH, 128, NXT, 128), I32)
    wd_t = din("wd_t", (H, D), I32)
    outT = nc.dram_tensor("outT", [D, MYT], F32, kind="ExternalOutput").ap()

    d_u = dint("d_u", (HDD, L), BF16)
    NARP = g["NCHB"]
    d_xdblp_p = [dint(f"d_xdblp{p}", (XD, cfg["CTB"]), F32) for p in range(NARP)]
    d_xdbl_p = [dint(f"d_xdbl{p}", (XD, cfg["CTB"]), F32) for p in range(NARP)]
    d_bc = dint("d_bc", (2 * NS, L), BF16)
    NAGP = g["NCHB"] // 2
    d_y_p = [dint(f"d_y{p}", (2, HDD, cfg["CTB"]), BF16) for p in range(NAGP)]
    d_yg_p = [dint(f"d_yg{p}", (2, 2, HDD, cfg["CTB"]), BF16) for p in range(NAGP)]
    d_wres = dint("d_wres", (D, DI), BF16)
    d_wout = dint("d_wout", (DI, D), BF16)
    d_wd = dint("d_wd", (H, D), BF16)
    d_ln = dint("d_ln", (2, CTC), BF16)
    d_x1 = dint("d_x1", (D, MYT), BF16)
    d_h = dint("d_h", (cfg["H"] // 128, 128, MYT), BF16)

    RG = [[0, 1], [2, 3], [4, 5], [6, 7]]

    psum = [k.stack.enter_context(nc.psum_tensor(f"pb{i}", [128, 512], F32))
            for i in range(8)]

    sb = k.stack.enter_context

    s_wx = sb(nc.sbuf_tensor("s_wx", [128, NDT, XD], BF16))
    s_wdt = sb(nc.sbuf_tensor("s_wdt", [DR, HDD], BF16))
    s_bdt = sb(nc.sbuf_tensor("s_bdt", [128, NDT], F32))
    s_A = sb(nc.sbuf_tensor("s_A", [128, NDT, NS], F32))
    s_D = sb(nc.sbuf_tensor("s_D", [128, NDT], F32))
    s_cw = sb(nc.sbuf_tensor("s_cw", [128, NDT, KC], F32))
    s_cb = sb(nc.sbuf_tensor("s_cb", [128, NDT], F32))
    s_carry = sb(nc.sbuf_tensor("s_carry", [128, NDT, NS], F32))
    s_ones = sb(nc.sbuf_tensor("s_ones", [128, 1], BF16))
    s_wxin = sb(nc.sbuf_tensor("s_wxin", [128, NXT, HDD], BF16))
    s_msel = sb(nc.sbuf_tensor("s_msel", [128, 2], F32))
    s_eps = sb(nc.sbuf_tensor("s_eps", [128, 1], F32))
    s_id = sb(nc.sbuf_tensor("s_id", [128, 128], BF16))

    WSLA = min(6, max(NDI, NXT)) * 128
    WTW = max(HDD, D, NXT * 128)
    f32_sz = max(2 * WTW + NDT * NS
                 + NXT * CTA + CTA + 7 * CTB,
                 2 * NXT * CTC + 4 * CTC) + 64 * 14
    b16_sz = max(2 * WTW
                 + (NXT + NDT + 2) * CTA + NDT * (KC - 1 + CTA) + (11 + 4 * NDT) * CTB,
                 (NXT + 5) * CTC
                 + max(4 * WSLA + (3 * NXT + 2 * NDI + 4 * NDT) * CTC,
                       (NXT + 2) * MYT + 2 * CTC + 8 * NXT * 128,
                       4 * WSLA + (3 * NXT + 2 * NH) * CTC)) + 64 * 24
    AF32 = Arena(k, "ar_f32", F32, f32_sz)
    AB16 = Arena(k, "ar_b16", BF16, b16_sz)
    AI32 = Arena(k, "ar_i32", I32, 2 * 1024 + 64)

    r128 = lambda ap: ap.rearrange("(t p) x -> p t x", p=128)

    if True:
        w_tmpf = AF32.alloc([128, 2, WTW])
        w_tmpi = AI32.alloc([128, 2, 1024])
        w_tmpb = AB16.alloc([128, 2, WTW])
        s_alog = AF32.alloc([128, NDT, NS])
        s_wxf = w_tmpf[:, 0, 0:NDT * XD].rearrange("p (t x) -> p t x", t=NDT)
        s_wdtf = w_tmpf[0:DR, 1, 0:HDD]

        def ld(dst_ap, src_ap):
            k.op("sync", "qS", lambda e, d=dst_ap, s=src_ap: e.dma_start(out=d, in_=s), 16)

        ld(s_alog, a_log)
        ld(s_bdt[:], b_dt)
        ld(s_D[:], d_vecI)
        ld(s_cw[:], conv_wI)
        ld(s_cb[:], conv_bI)
        ld(s_wxf, w_x.rearrange("p t x -> p (t x)").rearrange(
            "p (t x) -> p t x", t=NDT))
        ld(s_wdtf, w_dt)
        ld(s_msel[:], m_sel)
        vq = k.cnt["qS"]
        k.wait("vector", "qS", vq)
        k.wait("scalar", "qS", vq)
        k.op("scalar", "cS", lambda e: e.activation(out=s_A[:], in_=s_alog[:], func=AF.Exp))
        k.wait("vector", "cS", k.cnt["cS"])
        k.op("vector", "cV", lambda e: e.tensor_scalar_mul(s_A[:], s_A[:], -1.0))
        k.op("vector", "cV", lambda e: e.tensor_copy(s_wx[:], s_wxf))
        k.op("vector", "cV", lambda e: e.tensor_copy(s_wdt[:], s_wdtf))
        k.op("vector", "cV", lambda e: e.memset(s_carry[:], 0.0))
        k.op("vector", "cV", lambda e: e.memset(s_ones[:], 1.0))
        k.op("vector", "cV", lambda e: e.memset(s_eps[:], EPS))
        wvc = k.cnt["cV"]

        k.wait("sync", "cV", wvc)
        k.op("sync", "qS", lambda e: e.dma_start(
            out=w_tmpf[:, 0, 0:128], in_=identI), 16)
        k.wait("vector", "qS", k.cnt["qS"])
        k.op("vector", "cV", lambda e: e.tensor_copy(s_id[:], w_tmpf[:, 0, 0:128]))

        wxq = {0: k.cnt["cV"], 1: wvc}
        for i in range(D // 128):
            par = i % 2
            tin = w_tmpf[:, par, 0:HDD]
            qn = f"q{2 + par}"
            k.wait("sync", "cV", wxq[par])
            k.op("sync", qn, lambda e, t=tin, i=i: e.dma_start(
                out=t, in_=w_xin[i * 128:(i + 1) * 128, :]), 16)
            k.wait("vector", qn, k.cnt[qn])
            k.op("vector", "cV", lambda e, i=i, t=tin: e.tensor_copy(
                s_wxin[:, i, :], t))
            wxq[par] = k.cnt["cV"]

        w_tiles = []
        for hh in range(2):
            for i in range(D // 128):
                w_tiles.append((w_res[i * 128:(i + 1) * 128,
                                      hh * (DI // 2):(hh + 1) * (DI // 2)],
                                d_wres[i * 128:(i + 1) * 128,
                                       hh * (DI // 2):(hh + 1) * (DI // 2)],
                                DI // 2, False))
        for i in range(DI // 128):
            w_tiles.append((w_out[i * 128:(i + 1) * 128, :],
                            d_wout[i * 128:(i + 1) * 128, :], D, False))
        for i in range(H // 128):
            w_tiles.append((wd_t[i * 128:(i + 1) * 128, :],
                            d_wd[i * 128:(i + 1) * 128, :], D, True))
        wst = dict(t=0, iss=0, q={}, store={0: 0, 1: 0}, slot={})

        def issue_w_load(t):
            if t >= len(w_tiles):
                return
            src, dst, ncols, is_int = w_tiles[t]
            par = t % 2
            tin = (w_tmpi if is_int else w_tmpf)[:, par, 0:ncols]
            qn = f"qi{par}" if is_int else f"qf{par}"
            if not is_int and not wst.get(("fgate", par)):
                k.wait("scalar", "cV", wxq[par])
                wst[("fgate", par)] = True
            k.wait("scalar", "cS", wst["slot"].get((is_int, par), 0))
            k.op("scalar", qn, lambda e, t_=tin, s=src: e.dma_start(
                out=t_, in_=s), 16, nb=True)
            wst["q"][t] = (qn, k.cnt[qn])

        def emit_w_tiles(n):
            while wst["iss"] < min(wst["t"] + 2, len(w_tiles)):
                issue_w_load(wst["iss"])
                wst["iss"] += 1
            for _ in range(n):
                if wst["t"] >= len(w_tiles):
                    return
                t = wst["t"]
                wst["t"] += 1
                src, dst, ncols, is_int = w_tiles[t]
                par = t % 2
                tin = (w_tmpi if is_int else w_tmpf)[:, par, 0:ncols]
                tout = w_tmpb[:, par, 0:ncols]
                gn = f"g{par}"
                qn, qv = wst["q"][t]
                k.wait("scalar", qn, qv)
                k.wait("scalar", gn, wst["store"][par])
                k.op("scalar", "cS", lambda e, o=tout, t_=tin: e.activation(
                    out=o, in_=t_, func=AF.Copy), nb=True)
                wst["slot"][(is_int, par)] = k.cnt["cS"]
                k.wait("gpsimd", "cS", k.cnt["cS"])
                k.op("gpsimd", gn, lambda e, d=dst, t_=tout: e.dma_start(
                    out=d, in_=t_), 16)
                wst["store"][par] = k.cnt[gn]
                if wst["iss"] < len(w_tiles):
                    issue_w_load(wst["iss"])
                    wst["iss"] += 1

        NWT = len(w_tiles)

    k.barrier()

    if True:
        a_x = AF32.alloc([128, NXT, CTA])
        a_xb = AB16.alloc([128, NXT, CTA])
        a_xin = AB16.alloc([128, NDT, KC - 1 + CTA])
        a_conv = AB16.alloc([128, 2, CTA])
        a_u = AB16.alloc([128, NDT, CTA])
        a_xdbl = AF32.alloc([XD, CTA])
        b_xdbl = AF32.alloc([XD, CTB])
        b_dtb = AB16.alloc([DR, CTB])
        b_r = AB16.alloc([128, CTB])
        b_a = AF32.alloc([128, CTB])
        b_l = AF32.alloc([128, CTB])
        b_dbf = AB16.alloc([128, NDT, CTB])
        b_du = AB16.alloc([128, NDT, CTB])
        b_u = AB16.alloc([128, NDT, CTB])
        b_bcb = AB16.alloc([2 * NS, CTB])
        b_brep = AB16.alloc([128, 2, CTB])
        b_crep = AB16.alloc([128, 2, CTB])
        b_dA = AF32.alloc([128, 4, CTB])
        b_h = AB16.alloc([128, 2, CTB])
        b_tmp = AB16.alloc([128, 2, CTB])
        b_y = AB16.alloc([128, NDT, CTB])
        xT_t = xT.rearrange("(t p) l -> p t l", p=128)
        d_u_t = d_u.rearrange("(t p) l -> p t l", p=128)

        ast = dict(xb_v=0, evac_s=0, conv_v=0, wx_t=0, ustore=[], xstore=[])

        def emit_A(c):
            lo = c * CTA
            evac_m = {}
            k.wait("sync", "cV", ast["xb_v"])
            k.op("sync", "qA",
                 lambda e, lo=lo: e.dma_start(out=a_x, in_=xT_t[:, :, lo:lo + CTA]), 16)
            k.wait("vector", "qA", k.cnt["qA"])
            k.wait("vector", "cS", ast["evac_s"])
            if c == 0:
                k.op("vector", "cV", lambda e: e.memset(a_xin[:, :, 0:KC - 1], 0.0))
            else:
                k.op("vector", "cV", lambda e: e.tensor_copy(
                    a_xin[:, :, 0:KC - 1], a_xin[:, :, CTA:CTA + KC - 1]))
            halo_v = k.cnt["cV"]
            k.op("vector", "cV", lambda e: e.tensor_copy(a_xb, a_x))
            ast["xb_v"] = k.cnt["cV"]

            k.wait("tensor", "cV", ast["xb_v"])
            for m in range(NDT):
                pi = m % 2
                k.psum_guard(pi)
                for kk in range(NXT):
                    k.op("tensor", "cT", lambda e, pi=pi, kk=kk, m=m: e.matmul(
                        psum[pi][:, 0:CTA],
                        s_wxin[:, kk, m * 128:(m + 1) * 128],
                        a_xb[:, kk, :],
                        start=(kk == 0), stop=(kk == NXT - 1)))
                mm_t = k.cnt["cT"]
                k.wait("scalar", "cT", mm_t)
                k.wait("scalar", "cV", halo_v)
                k.op("scalar", "cS", lambda e, m=m, pi=pi: e.activation(
                    out=a_xin[:, m, KC - 1:KC - 1 + CTA], in_=psum[pi][:, 0:CTA],
                    func=AF.Copy))
                evac_m[m] = k.cnt["cS"]
                k.psum_set_free(pi, ("cS", k.cnt["cS"]))
            ast["evac_s"] = k.cnt["cS"]

            silu_done = {}
            for m in range(NDT):
                cp = m % 2
                k.wait("vector", "cS", evac_m[m])
                if m >= 2:
                    k.wait("vector", "cS", silu_done[m - 2])
                k.op("vector", "cV", lambda e, m=m, cp=cp: e.tensor_scalar(
                    out=a_conv[:, cp, :], in0=a_xin[:, m, 0:CTA],
                    scalar1=s_cw[:, m, 0:1], scalar2=s_cb[:, m:m + 1],
                    op0=OP.mult, op1=OP.add))
                for t in range(1, KC):
                    k.op("vector", "cV", lambda e, m=m, t=t, cp=cp: e.scalar_tensor_tensor(
                        out=a_conv[:, cp, :], in0=a_xin[:, m, t:t + CTA],
                        scalar=s_cw[:, m, t:t + 1], in1=a_conv[:, cp, :],
                        op0=OP.mult, op1=OP.add))
                k.wait("scalar", "cV", k.cnt["cV"])
                k.wait("scalar", "cT", ast["wx_t"])
                if ast["ustore"]:
                    k.wait("scalar", "qU", ast["ustore"][-1])
                k.op("scalar", "cS", lambda e, m=m, cp=cp: e.activation(
                    out=a_u[:, m, :], in_=a_conv[:, cp, :], func=AF.Silu), nb=True)
                silu_done[m] = k.cnt["cS"]
            silu_s = k.cnt["cS"]

            k.wait("tensor", "cS", silu_s)
            k.psum_guard(2)
            for kk in range(NDT):
                k.op("tensor", "cT", lambda e, kk=kk: e.matmul(
                    psum[2][0:XD, 0:CTA], s_wx[:, kk, :], a_u[:, kk, :],
                    start=(kk == 0), stop=(kk == NDT - 1)))
            ast["wx_t"] = k.cnt["cT"]
            k.wait("scalar", "cT", ast["wx_t"])
            if ast["xstore"]:
                k.wait("scalar", "qX", ast["xstore"][-1])
            k.op("scalar", "cS", lambda e: e.activation(
                out=a_xdbl[:], in_=psum[2][0:XD, 0:CTA], func=AF.Copy))
            k.psum_set_free(2, ("cS", k.cnt["cS"]))

            k.wait("gpsimd", "cS", k.cnt["cS"])
            k.op("gpsimd", "qU",
                 lambda e, lo=lo: e.dma_start(out=d_u_t[:, :, lo:lo + CTA], in_=a_u), 16)
            ast["ustore"].append(k.cnt["qU"])
            pp = c // (CTB // CTA)
            off = (c % (CTB // CTA)) * CTA
            k.op("gpsimd", "qX",
                 lambda e, pp=pp, off=off: e.dma_start(
                     out=d_xdblp_p[pp][:, off:off + CTA], in_=a_xdbl), 16)
            ast["xstore"].append(k.cnt["qX"])

        ar_cc = {}
        ag_cc = {}

        def emit_AR(p):
            k.wait("gpsimd", "qU", k.cnt["qU"])
            k.wait("gpsimd", "qX", k.cnt["qX"])
            ar_cc[p] = k.op("gpsimd", "cc", lambda e, p=p: e.collective_compute(
                "AllReduce", OP.add, replica_groups=RG,
                ins=[d_xdblp_p[p]], outs=[d_xdbl_p[p]]))

        scan_done = {}
        add_done = {}
        bst = dict(prev_yinit=0, prev_store=0)
        HTB = CTB // 2
        assert HTB <= 512
        OFFM = dict(zip(list(range(NDT))[-2:], ((5, 6), (7, 3))))
        idmm = {}

        def emit_B(c):
            lo = c * CTB
            k.wait("sync", "cc", ar_cc[c])
            k.wait("sync", "cV", bst["prev_yinit"])
            k.op("sync", "qS", lambda e, c=c: e.dma_start(
                out=b_xdbl, in_=d_xdbl_p[c][:]), 16)
            k.wait("sync", "qU", ast["ustore"][2 * c + 1])
            k.op("sync", "qS", lambda e, lo=lo: e.dma_start(
                out=b_u, in_=d_u_t[:, :, lo:lo + CTB]), 16)
            k.wait("vector", "qS", k.cnt["qS"])
            k.op("vector", "cV", lambda e: e.tensor_copy(b_dtb, b_xdbl[0:DR, :]))
            k.op("vector", "cV", lambda e: e.tensor_copy(b_bcb, b_xdbl[DR:XD, :]))
            cast_v = k.cnt["cV"]
            k.wait("gpsimd", "cV", cast_v)
            k.op("gpsimd", "qG", lambda e, lo=lo: e.dma_start(
                out=d_bc[:, lo:lo + CTB], in_=b_bcb), 16)
            bc_q = k.cnt["qG"]

            k.wait("tensor", "cV", cast_v)
            for m in range(NDT):
                def emit_dbf(m, sl, pln):
                    k.wait("vector", "cS", pln)
                    k.op("vector", "cV", lambda e, m=m, sl=sl: e.tensor_tensor(
                        out=b_dbf[:, m, sl], in0=b_r[:, sl], in1=b_l[:, sl],
                        op=OP.add))
                    k.op("vector", "cV", lambda e, m=m, sl=sl: e.tensor_tensor(
                        out=b_du[:, m, sl], in0=b_dbf[:, m, sl],
                        in1=b_u[:, m, sl], op=OP.mult))

                hp = None
                for ts in range(NTB):
                    pi = 3 + (ts % 2)
                    sl = slice(ts * TSB, (ts + 1) * TSB)
                    k.psum_guard(pi)
                    k.op("tensor", "cT", lambda e, m=m, sl=sl, pi=pi: e.matmul(
                        psum[pi][:, 0:TSB], s_wdt[:, m * 128:(m + 1) * 128],
                        b_dtb[:, sl], start=True, stop=True))
                    mm_t = k.cnt["cT"]
                    k.wait("vector", "cT", mm_t)
                    k.op("vector", "cV", lambda e, pi=pi, sl=sl, m=m: e.tensor_scalar(
                        out=b_r[:, sl], in0=psum[pi][:, 0:TSB],
                        scalar1=s_bdt[:, m:m + 1], scalar2=0.0,
                        op0=OP.add, op1=OP.max))
                    k.op("vector", "cV", lambda e, pi=pi, sl=sl, m=m: e.tensor_scalar(
                        out=b_a[:, sl], in0=psum[pi][:, 0:TSB],
                        scalar1=s_bdt[:, m:m + 1], scalar2=None, op0=OP.add))
                    k.psum_set_free(pi, ("cV", k.cnt["cV"]))
                    k.op("vector", "cV", lambda e, sl=sl: e.scalar_tensor_tensor(
                        out=b_a[:, sl], in0=b_r[:, sl], scalar=-2.0,
                        in1=b_a[:, sl], op0=OP.mult, op1=OP.add))
                    stt_v = k.cnt["cV"]
                    k.wait("scalar", "cV", stt_v)
                    k.op("scalar", "cS", lambda e, sl=sl: e.activation(
                        out=b_l[:, sl], in_=b_a[:, sl], func=AF.Exp), nb=True)
                    k.op("scalar", "cS", lambda e, sl=sl: e.activation(
                        out=b_l[:, sl], in_=b_l[:, sl], func=AF.Ln, bias=1.0))
                    if hp is not None:
                        emit_dbf(m, hp[0], hp[1])
                    hp = (sl, k.cnt["cS"])
                emit_dbf(m, hp[0], hp[1])
            du_v = k.cnt["cV"]

            k.wait("vector", "qYs", bst["prev_store"])
            for m in range(NDT):
                k.op("vector", "cV", lambda e, m=m: e.tensor_scalar(
                    out=b_y[:, m, :], in0=b_u[:, m, :], scalar1=s_D[:, m:m + 1],
                    scalar2=None, op0=OP.mult))
            bst["prev_yinit"] = k.cnt["cV"]

            k.wait("sync", "qG", bc_q)
            for n in range(NS):
                par = n % 2
                qn = f"q{par}"
                gn = c * NS + n
                if gn >= 2:
                    k.wait("sync", "cV", scan_done[(gn - 2, "mul", NDT - 1)])
                k.op("sync", qn, lambda e, n=n, lo=lo, par=par: e.dma_start(
                    out=b_brep[:, par, :],
                    in_=d_bc[n:n + 1, lo:lo + CTB].broadcast_to([128, CTB])), 16)
                k.op("sync", qn, lambda e, n=n, lo=lo, par=par: e.dma_start(
                    out=b_crep[:, par, :],
                    in_=d_bc[NS + n:NS + n + 1, lo:lo + CTB].broadcast_to([128, CTB])), 16)
                rep_q = k.cnt[qn]
                for mp in range(0, NDT, 2):
                    ms = [m for m in (mp, mp + 1) if m < NDT]
                    dA_s = {}
                    for m in ms:
                        u = gn * NDT + m
                        up = u % 4
                        if u >= 4:
                            k.wait("scalar", "cV", scan_done[((u - 4) // NDT, "scan", (u - 4) % NDT)])
                        k.wait("scalar", "cV", du_v)
                        k.op("scalar", "cS", lambda e, m=m, n=n, up=up: e.activation(
                            out=b_dA[:, up, :], in_=b_dbf[:, m, :], func=AF.Exp,
                            scale=s_A[:, m, n:n + 1]), nb=True)
                        dA_s[m] = k.cnt["cS"]
                    k.wait("vector", qn, rep_q)
                    for m in ms:
                        u = gn * NDT + m
                        u2 = u % 2
                        if (u - 2) in idmm:
                            k.wait("vector", "cT", idmm[u - 2])
                        k.op("vector", "cV", lambda e, m=m, u2=u2, par=par: e.tensor_tensor(
                            out=b_tmp[:, u2, :], in0=b_du[:, m, :], in1=b_brep[:, par, :],
                            op=OP.mult))
                    for m in ms:
                        u = gn * NDT + m
                        u2 = u % 2
                        up = u % 4
                        k.wait("vector", "cS", dA_s[m])
                        k.op("vector", "cV", lambda e, m=m, n=n, up=up, u2=u2: e.tensor_tensor_scan(
                            out=b_h[:, u2, :], data0=b_dA[:, up, :], data1=b_tmp[:, u2, :],
                            initial=s_carry[:, m, n:n + 1],
                            op0=OP.mult, op1=OP.add))
                        scan_done[(gn, "scan", m)] = k.cnt["cV"]
                    if len(ms) == 2:
                        k.op("vector", "cV", lambda e, mp=mp, n=n: e.tensor_copy(
                            s_carry[:, mp:mp + 2, n:n + 1],
                            b_h[:, 0:2, CTB - 1:CTB]))
                    else:
                        m0 = ms[0]
                        u2 = (gn * NDT + m0) % 2
                        k.op("vector", "cV", lambda e, m0=m0, n=n, u2=u2: e.tensor_copy(
                            s_carry[:, m0, n:n + 1], b_h[:, u2, CTB - 1:CTB]))
                    for m in ms:
                        u2 = (gn * NDT + m) % 2
                        k.op("vector", "cV", lambda e, u2=u2, par=par: e.tensor_tensor(
                            out=b_tmp[:, u2, :], in0=b_h[:, u2, :], in1=b_crep[:, par, :],
                            op=OP.mult))
                        scan_done[(gn, "mul", m)] = k.cnt["cV"]
                    if ms[0] in OFFM:
                        for m in ms:
                            u = gn * NDT + m
                            u2 = u % 2
                            blo, bhi = OFFM[m]
                            if n == 0:
                                k.psum_guard(blo)
                                k.psum_guard(bhi)
                            k.wait("tensor", "cV", scan_done[(gn, "mul", m)])
                            k.op("tensor", "cT", lambda e, u2=u2, blo=blo, n=n: e.matmul(
                                psum[blo][:, 0:HTB], s_id[:], b_tmp[:, u2, 0:HTB],
                                start=(n == 0), stop=(n == NS - 1)))
                            k.op("tensor", "cT", lambda e, u2=u2, bhi=bhi, n=n: e.matmul(
                                psum[bhi][:, 0:HTB], s_id[:], b_tmp[:, u2, HTB:CTB],
                                start=(n == 0), stop=(n == NS - 1)))
                            idmm[u] = k.cnt["cT"]
                    elif len(ms) == 2:
                        k.op("vector", "cV", lambda e, mp=mp: e.tensor_tensor(
                            out=b_y[:, mp:mp + 2, :], in0=b_y[:, mp:mp + 2, :],
                            in1=b_tmp[:, 0:2, :], op=OP.add))
                    else:
                        m0 = ms[0]
                        u2 = (gn * NDT + m0) % 2
                        k.op("vector", "cV", lambda e, m0=m0, u2=u2: e.tensor_tensor(
                            out=b_y[:, m0, :], in0=b_y[:, m0, :], in1=b_tmp[:, u2, :],
                            op=OP.add))

            for m, (blo, bhi) in OFFM.items():
                k.wait("vector", "cT", idmm[(c * NS + NS - 1) * NDT + m])
                k.op("vector", "cV", lambda e, m=m, blo=blo: e.tensor_tensor(
                    out=b_y[:, m, 0:HTB], in0=b_y[:, m, 0:HTB],
                    in1=psum[blo][:, 0:HTB], op=OP.add))
                k.op("vector", "cV", lambda e, m=m, bhi=bhi: e.tensor_tensor(
                    out=b_y[:, m, HTB:CTB], in0=b_y[:, m, HTB:CTB],
                    in1=psum[bhi][:, 0:HTB], op=OP.add))
                k.psum_set_free(blo, ("cV", k.cnt["cV"]))
                k.psum_set_free(bhi, ("cV", k.cnt["cV"]))

            s = c // (NCHB // 2)
            p = c % (NCHB // 2)
            k.wait("gpsimd", "cV", k.cnt["cV"])
            k.op("gpsimd", "qYs", lambda e, s=s, p=p: e.dma_start(
                out=d_y_p[p][s].rearrange("(t p) l -> p t l", p=128),
                in_=b_y), 16)
            bst["prev_store"] = k.cnt["qYs"]
            if s == 1:
                k.wait("gpsimd", "qYs", k.cnt["qYs"])
                ag_cc[p] = k.op("gpsimd", "cc", lambda e, p=p: e.collective_compute(
                    "AllGather", OP.bypass, replica_groups=RG,
                    ins=[d_y_p[p]], outs=[d_yg_p[p]]))

        RAT = CTB // CTA
        WPC = -(-NWT // NCHB)
        for pp in range(min(2, NCHB)):
            emit_A(RAT * pp)
            emit_A(RAT * pp + 1)
            emit_AR(pp)
        for p in range(NCHB):
            if p + 2 < NCHB:
                emit_A(RAT * (p + 2))
                emit_A(RAT * (p + 2) + 1)
                emit_AR(p + 2)
            emit_B(p)
            emit_w_tiles(WPC)
        emit_w_tiles(NWT)

    k.barrier()
    AF32.reset(); AB16.reset(); AI32.reset()

    if True:
        c_sq = AB16.alloc([128, CTC])
        c_mean = AF32.alloc([1, CTC])
        c_msq = AF32.alloc([1, CTC])
        c_var = AF32.alloc([1, CTC])
        c_mr = AB16.alloc([128, CTC])
        c_rr = AB16.alloc([128, CTC])
        c_lnr0 = AB16.alloc([1, CTC])
        c_lnr1 = AB16.alloc([1, CTC])
        c_out = AF32.alloc([128, CTC])
        c_z = AB16.alloc([128, NXT, CTC])
        c2_f32_mark = AF32.off
        c2_b16_mark = AB16.off
        WSL1 = min(6, max(NDI, NXT)) * 128
        c_wsl = AB16.alloc([128, 4, WSL1])
        c_x = AF32.alloc([128, 2, NXT, CTC])
        c_xb = AB16.alloc([128, 2, NXT, CTC])
        c_y = AB16.alloc([128, NDI, CTC])
        c_yc = AB16.alloc([128, 4, NDT, CTC])
        c_res = AB16.alloc([128, NDI, CTC])
        c_x1 = AB16.alloc([128, NXT, CTC])
        xm_t = xTmine.rearrange("(t p) l -> p t l", p=128)
        outT_t = outT.rearrange("(t p) l -> p t l", p=128)
        dx1_t = d_x1.rearrange("(t p) l -> p t l", p=128)

        def grps(n):
            out, o = [], 0
            while o < n:
                sz = min(6, n - o)
                out.append((o, sz))
                o += sz
            return out

        def layernorm(src3, dst3, out_tile=None, mid=None, dst_wait=None,
                      ln_sync=False):
            k.psum_guard(6)
            k.psum_guard(7)
            k.wait("scalar", "cV", k.cnt["cV"])
            for i in range(NXT):
                k.wait("scalar", "cT", k.cnt["cT"])
                k.op("scalar", "cS", lambda e, i=i: e.activation(
                    out=c_sq, in_=src3[:, i, :], func=AF.Square))
                sq_s = k.cnt["cS"]
                k.wait("tensor", "cS", sq_s)
                k.op("tensor", "cT", lambda e, i=i: e.matmul(
                    psum[6][0:1, 0:CTC], s_ones[:], src3[:, i, :],
                    start=(i == 0), stop=(i == NXT - 1)))
                k.op("tensor", "cT", lambda e, i=i: e.matmul(
                    psum[7][0:1, 0:CTC], s_ones[:], c_sq[:],
                    start=(i == 0), stop=(i == NXT - 1)))
            st_t = k.cnt["cT"]
            k.wait("vector", "cT", st_t)
            k.op("vector", "cV", lambda e: e.tensor_scalar_mul(
                c_mean, psum[6][0:1, 0:CTC], 1.0 / D))
            k.op("vector", "cV", lambda e: e.tensor_scalar_mul(
                c_msq, psum[7][0:1, 0:CTC], 1.0 / D))
            k.psum_set_free(6, ("cV", k.cnt["cV"]))
            k.psum_set_free(7, ("cV", k.cnt["cV"]))
            k.op("vector", "cV", lambda e: e.tensor_tensor(
                out=c_var, in0=c_mean, in1=c_mean, op=OP.mult))
            k.op("vector", "cV", lambda e: e.tensor_tensor(
                out=c_var, in0=c_msq, in1=c_var, op=OP.subtract))
            var_v = k.cnt["cV"]
            k.wait("scalar", "cV", var_v)
            k.op("scalar", "cS", lambda e: e.activation(
                out=c_var, in_=c_var, func=AF.Ln, bias=s_eps[0:1, 0:1]))
            k.op("scalar", "cS", lambda e: e.activation(
                out=c_msq, in_=c_var, func=AF.Exp, scale=-0.5))
            rows_s = k.cnt["cS"]
            k.wait("vector", "cS", rows_s)
            k.op("vector", "cV", lambda e: e.tensor_copy(c_lnr0, c_mean))
            k.op("vector", "cV", lambda e: e.tensor_copy(c_lnr1, c_msq))
            rows_v = k.cnt["cV"]
            if ln_sync:
                k.wait("sync", "cV", rows_v)
                k.wait("sync", "qL", k.cnt["qL"])
                k.op("sync", "qL", lambda e: e.dma_start(
                    out=d_ln[0:1, :], in_=c_lnr0), 16)
                k.op("sync", "qL", lambda e: e.dma_start(
                    out=d_ln[1:2, :], in_=c_lnr1), 16)
                k.wait("sync", "qL", k.cnt["qL"])
            else:
                k.wait("gpsimd", "cV", rows_v)
                k.wait("gpsimd", "qM", k.cnt["qM"])
                k.op("gpsimd", "qG", lambda e: e.dma_start(
                    out=d_ln[0:1, :], in_=c_lnr0), 16)
                k.op("gpsimd", "qG", lambda e: e.dma_start(
                    out=d_ln[1:2, :], in_=c_lnr1), 16)
                k.wait("scalar", "qG", k.cnt["qG"])
            leng = "sync" if ln_sync else "scalar"
            lsem = "qL" if ln_sync else "qM"
            k.op(leng, lsem, lambda e: e.dma_start(
                out=c_mr, in_=d_ln[0:1, :].broadcast_to([128, CTC])), 16, nb=True)
            k.op(leng, lsem, lambda e: e.dma_start(
                out=c_rr, in_=d_ln[1:2, :].broadcast_to([128, CTC])), 16, nb=True)
            if mid is not None:
                mid()
            k.wait("vector", lsem, k.cnt[lsem])
            if dst_wait is not None:
                k.wait("vector", dst_wait[0], dst_wait[1])
            for i in range(NXT):
                k.op("vector", "cV", lambda e, i=i: e.tensor_tensor(
                    out=dst3[:, i, :], in0=src3[:, i, :], in1=c_mr,
                    op=OP.subtract))
                if out_tile is None:
                    k.op("vector", "cV", lambda e, i=i: e.tensor_tensor(
                        out=dst3[:, i, :], in0=dst3[:, i, :], in1=c_rr,
                        op=OP.mult))
                else:
                    k.wait("vector", "qY", k.cnt["qY"])
                    k.op("vector", "cV", lambda e, i=i: e.tensor_tensor(
                        out=c_out[:], in0=dst3[:, i, :], in1=c_rr,
                        op=OP.mult))
                    ov = k.cnt["cV"]
                    k.wait("sync", "cV", ov)
                    k.op("sync", "qY",
                         lambda e, i=i: e.dma_start(out=out_tile(i), in_=c_out), 16)

        NPC = 4
        pieces = [(c, j, s) for c in range(NCHC) for j in range(2)
                  for s in range(2)]
        st = dict(zadd={-1: 0, -2: 0}, wout={-1: 0}, res={-1: 0, -2: 0},
                  selv={}, ycq={}, cast={}, xq={}, x1st={-1: 0}, silu_gm={},
                  g_gm={})
        slab = dict(i=0, done={})

        def slab_load(src, W):
            i = slab["i"]
            par4 = i % 4
            qn = f"q{par4}"
            if i >= 4:
                k.wait("sync", "cT", slab["done"][i - 4])
            k.op("sync", qn, lambda e, s=src, par4=par4, W=W: e.dma_start(
                out=c_wsl[:, par4, 0:W], in_=s), 16)
            k.wait("tensor", qn, k.cnt[qn])
            return par4

        def slab_mark():
            slab["done"][slab["i"]] = k.cnt["cT"]
            slab["i"] += 1

        def x_load(c):
            if c >= NCHC:
                return
            par = c % 2
            k.wait("scalar", "cV", st["zadd"][c - 2])
            k.op("scalar", f"cx{par}", lambda e, lo=c * CTC, par=par: e.dma_start(
                out=c_x[:, par], in_=xm_t[:, :, lo:lo + CTC]), 16, nb=True)
            st["xq"][c] = k.cnt[f"cx{par}"]

        def x_cast(c):
            if c >= NCHC:
                return
            par = c % 2
            k.wait("vector", f"cx{par}", st["xq"][c])
            k.wait("vector", "cT", st["res"][c - 2])
            k.op("vector", "cV", lambda e, par=par: e.tensor_copy(
                c_xb[:, par], c_x[:, par]))
            st["cast"][c] = k.cnt["cV"]

        def piece_load(gi):
            if gi >= len(pieces):
                return
            c, j, s = pieces[gi]
            lo = c * CTC
            pidx = lo // CTB
            poff = lo % CTB
            slot = gi % 4
            sn = f"cy{slot}"
            k.wait("scalar", "cV", st["selv"].get(gi - 4, 0))
            k.wait("scalar", "cc", ag_cc[pidx])
            k.op("scalar", sn, lambda e, j=j, s=s, pidx=pidx, poff=poff, slot=slot:
                 e.dma_start(out=c_yc[:, slot],
                             in_=d_yg_p[pidx][j, s, :, poff:poff + CTC]
                             .rearrange("(t p) l -> p t l", p=128)), 16, nb=True)
            st["ycq"][gi] = (sn, k.cnt[sn])

        def emit_select(c):
            for i in range(NPC):
                gi = NPC * c + i
                _, j, s = pieces[gi]
                slot = gi % 4
                sn, sv = st["ycq"][gi]
                k.wait("vector", sn, sv)
                k.wait("vector", "cT", st["wout"][c - 1])
                for m in range(NDT):
                    if s == 0:
                        k.op("vector", "cV", lambda e, j=j, m=m, slot=slot:
                             e.tensor_scalar(
                                 out=c_y[:, j * NDT + m, :],
                                 in0=c_yc[:, slot, m, :],
                                 scalar1=s_msel[:, 0:1], scalar2=None,
                                 op0=OP.mult))
                    else:
                        k.op("vector", "cV", lambda e, j=j, m=m, slot=slot:
                             e.scalar_tensor_tensor(
                                 out=c_y[:, j * NDT + m, :],
                                 in0=c_yc[:, slot, m, :],
                                 scalar=s_msel[:, 1:2],
                                 in1=c_y[:, j * NDT + m, :],
                                 op0=OP.mult, op1=OP.add))
                st["selv"][gi] = k.cnt["cV"]
                piece_load(gi + 4)

        def emit_res(c):
            par = c % 2
            k.wait("tensor", "cV", st["cast"][c])
            for (gof, gsz) in grps(NDI):
                for pi in range(gsz):
                    k.psum_guard(pi)
                GW = gsz * 128
                for kk in range(NXT):
                    par4 = slab_load(
                        d_wres[kk * 128:(kk + 1) * 128,
                               gof * 128:gof * 128 + GW], GW)
                    for m in range(gsz):
                        k.op("tensor", "cT", lambda e, m=m, kk=kk, par4=par4,
                             par=par: e.matmul(
                                 psum[m][:, 0:CTC],
                                 c_wsl[:, par4, m * 128:(m + 1) * 128],
                                 c_xb[:, par, kk, :],
                                 start=(kk == 0), stop=(kk == NXT - 1)))
                    slab_mark()
                last_mm = k.cnt["cT"]
                for m in range(gsz):
                    gm = gof + m
                    k.wait("scalar", "cT", last_mm)
                    k.op("scalar", "cS", lambda e, m=m, gm=gm: e.activation(
                        out=c_res[:, gm, :], in_=psum[m][:, 0:CTC],
                        func=AF.Silu))
                    st["silu_gm"][gm] = k.cnt["cS"]
                    k.psum_set_free(m, ("cS", k.cnt["cS"]))
            st["res"][c] = k.cnt["cT"]

        def emit_gw(c):
            par = c % 2
            for gm in range(NDI):
                k.wait("vector", "cS", st["silu_gm"][gm])
                k.op("vector", "cV", lambda e, gm=gm: e.tensor_tensor(
                    out=c_y[:, gm, :], in0=c_y[:, gm, :], in1=c_res[:, gm, :],
                    op=OP.mult))
                st["g_gm"][gm] = k.cnt["cV"]
            for (mof, msz) in grps(NXT):
                for pi in range(msz):
                    k.psum_guard(pi)
                MW = msz * 128
                KG = 2 if (2 * MW <= WSL1 and NDI % 2 == 0) else 1
                for kk0 in range(0, NDI, KG):
                    src_ap = d_wout[kk0 * 128:(kk0 + KG) * 128,
                                    mof * 128:mof * 128 + MW]
                    if KG == 2:
                        src_ap = src_ap.rearrange("(a p) c -> p a c", p=128)
                    par4 = slab_load(src_ap, KG * MW)
                    for kl in range(KG):
                        kk = kk0 + kl
                        k.wait("tensor", "cV", st["g_gm"][kk])
                        for m in range(msz):
                            k.op("tensor", "cT", lambda e, m=m, kk=kk, kl=kl,
                                 par4=par4, MW=MW: e.matmul(
                                     psum[m][:, 0:CTC],
                                     c_wsl[:, par4,
                                           kl * MW + m * 128:kl * MW + (m + 1) * 128],
                                     c_y[:, kk, :],
                                     start=(kk == 0), stop=(kk == NDI - 1)))
                    slab_mark()
                k.wait("vector", "cT", k.cnt["cT"])
                k.wait("vector", f"cx{par}", st["xq"][c])
                for m in range(msz):
                    k.op("vector", "cV", lambda e, m=m, mof=mof, par=par:
                         e.tensor_tensor(
                             out=c_z[:, mof + m, :],
                             in0=c_x[:, par, mof + m, :],
                             in1=psum[m][:, 0:CTC], op=OP.add))
                    k.psum_set_free(m, ("cV", k.cnt["cV"]))
            st["wout"][c] = k.cnt["cT"]
            st["zadd"][c] = k.cnt["cV"]

        for gi0 in range(4):
            piece_load(gi0)
        x_load(0)
        for c in range(NCHC):
            x_load(c + 1)
            if c == 0:
                x_cast(0)
                emit_select(0)
            x_cast(c + 1)
            emit_res(c)
            emit_gw(c)
            layernorm(c_z, c_x1,
                      mid=(lambda c=c: emit_select(c + 1))
                      if c + 1 < NCHC else None,
                      dst_wait=("qYs", st["x1st"][c - 1]))
            k.wait("gpsimd", "cV", k.cnt["cV"])
            k.op("gpsimd", "qYs", lambda e, lo=c * CTC: e.dma_start(
                out=dx1_t[:, :, lo:lo + CTC], in_=c_x1), 16)
            st["x1st"][c] = k.cnt["qYs"]

        k.barrier()
        AF32.off = c2_f32_mark
        AB16.off = c2_b16_mark
        x1a = AB16.alloc([128, NXT, MYT])
        hmb = AB16.alloc([128, 2, MYT])
        sgp = AB16.alloc([128, 2, CTC])
        c_wg = AB16.alloc([128, 4, NXT, 128])
        c_wu = AB16.alloc([128, 4, NXT, 128])
        c_wi = AI32.alloc([128, 2, NXT * 128])
        k.op("sync", "qW", lambda e: e.dma_start(out=x1a, in_=dx1_t), 16)
        k.wait("tensor", "qW", k.cnt["qW"])
        wgi = wg_t.rearrange("m p t c -> m p (t c)")
        wui = wu_t.rearrange("m p t c -> m p (t c)")
        NTC2 = MYT // CTC
        ffn_mm = {}
        hst = {}
        wc = {0: 0, 1: 0}
        wgu_done = {}

        def emit_wgu_cast(m):
            if m >= NH:
                return
            par = m % 4
            for wh in range(2):
                pari = (2 * m + wh) % 2
                qn = f"q{pari}"
                k.wait("sync", "cS", wc[pari])
                k.op("sync", qn, lambda e, m=m, pari=pari, wh=wh: e.dma_start(
                    out=c_wi[:, pari], in_=(wgi if wh == 0 else wui)[m]), 16)
                k.wait("scalar", qn, k.cnt[qn])
                k.wait("scalar", "cT", ffn_mm.get(m - 4, 0))
                dst = (c_wg if wh == 0 else c_wu)[:, par].rearrange(
                    "p t c -> p (t c)")
                k.op("scalar", "cS", lambda e, dst=dst, pari=pari: e.activation(
                    out=dst, in_=c_wi[:, pari], func=AF.Copy), nb=True)
                wc[pari] = k.cnt["cS"]
            wgu_done[m] = k.cnt["cS"]

        emit_wgu_cast(0)
        emit_wgu_cast(1)
        for m in range(NH):
            par = m % 4
            emit_wgu_cast(m + 2)
            k.wait("tensor", "cS", wgu_done[m])
            hb = m % 2
            if m >= 2:
                k.wait("vector", hst[m - 2][0], hst[m - 2][1])
            for tc in range(NTC2):
                sl = slice(tc * CTC, (tc + 1) * CTC)
                pg = 4 + 2 * (tc % 2)
                pu = pg + 1
                k.psum_guard(pg)
                k.psum_guard(pu)
                for kk in range(NXT):
                    k.op("tensor", "cT", lambda e, kk=kk, par=par, sl=sl, pg=pg: e.matmul(
                        psum[pg][:, 0:CTC], c_wg[:, par, kk, :], x1a[:, kk, sl],
                        start=(kk == 0), stop=(kk == NXT - 1)))
                for kk in range(NXT):
                    k.op("tensor", "cT", lambda e, kk=kk, par=par, sl=sl, pu=pu: e.matmul(
                        psum[pu][:, 0:CTC], c_wu[:, par, kk, :], x1a[:, kk, sl],
                        start=(kk == 0), stop=(kk == NXT - 1)))
                mm_t = k.cnt["cT"]
                k.wait("scalar", "cT", mm_t)
                sp = tc % 2
                k.op("scalar", "cS", lambda e, sp=sp, pg=pg: e.activation(
                    out=sgp[:, sp, :], in_=psum[pg][:, 0:CTC], func=AF.Silu, scale=QS))
                sg_s = k.cnt["cS"]
                k.psum_set_free(pg, ("cS", sg_s))
                k.wait("vector", "cS", sg_s)
                k.op("vector", "cV", lambda e, hb=hb, sl=sl, sp=sp, pu=pu: e.scalar_tensor_tensor(
                    out=hmb[:, hb, sl], in0=psum[pu][:, 0:CTC], scalar=QS,
                    in1=sgp[:, sp, :], op0=OP.mult, op1=OP.mult))
                k.psum_set_free(pu, ("cV", k.cnt["cV"]))
            ffn_mm[m] = k.cnt["cT"]
            k.wait("gpsimd", "cV", k.cnt["cV"])
            hq = "qU" if hb == 0 else "qX"
            k.op("gpsimd", hq, lambda e, m=m, hb=hb: e.dma_start(
                out=d_h[m], in_=hmb[:, hb, :]), 16)
            hst[m] = (hq, k.cnt[hq])

        k.barrier()
        AF32.off = c2_f32_mark
        AB16.off = c2_b16_mark
        c_z2 = AB16.alloc([128, NXT, CTC])
        c_wd = AB16.alloc([128, 4, WSL1])
        htc = AB16.alloc([128, 2, NH, CTC])
        x1c = AB16.alloc([128, 2, NXT, CTC])
        d_h_t = d_h.rearrange("m p l -> p m l")
        st2 = dict(dn={-1: 0, -2: 0}, z2={-1: 0, -2: 0})
        slab2 = dict(i=0, done={})

        def slab2_load(src, W):
            i = slab2["i"]
            par4 = i % 4
            gn = f"g{par4}"
            if i >= 4:
                k.wait("gpsimd", "cT", slab2["done"][i - 4])
            dst = c_wd[:, par4, 0:W]
            if len(src.shape) == 3:
                dst = dst.rearrange("p (a c) -> p a c", a=src.shape[1])
            k.op("gpsimd", gn, lambda e, s=src, d=dst: e.dma_start(
                out=d, in_=s), 16)
            k.wait("tensor", gn, k.cnt[gn])
            return par4

        def slab2_mark():
            slab2["done"][slab2["i"]] = k.cnt["cT"]
            slab2["i"] += 1

        for tc in range(NTC2):
            lo = tc * CTC
            hb = tc % 2
            hq = "qU" if hb == 0 else "qX"
            k.wait("gpsimd", "cT", st2["dn"][tc - 2])
            k.wait("gpsimd", "cV", st2["z2"][tc - 2])
            k.op("gpsimd", hq, lambda e, lo=lo, hb=hb: e.dma_start(
                out=htc[:, hb], in_=d_h_t[:, :, lo:lo + CTC]), 16)
            k.op("gpsimd", hq, lambda e, lo=lo, hb=hb: e.dma_start(
                out=x1c[:, hb], in_=dx1_t[:, :, lo:lo + CTC]), 16)
            lq = k.cnt[hq]
            k.wait("tensor", hq, lq)
            for (mof, msz) in grps(NXT):
                for pi in range(msz):
                    k.psum_guard(pi)
                MW = msz * 128
                KG = 2 if (2 * MW <= WSL1 and NH % 2 == 0) else 1
                for kk0 in range(0, NH, KG):
                    par4 = slab2_load(
                        d_wd[kk0 * 128:(kk0 + KG) * 128,
                             mof * 128:mof * 128 + MW]
                        .rearrange("(a p) c -> p a c", p=128), KG * MW)
                    for kl in range(KG):
                        kk = kk0 + kl
                        for m in range(msz):
                            k.op("tensor", "cT", lambda e, m=m, kk=kk, kl=kl,
                                 par4=par4, hb=hb, MW=MW: e.matmul(
                                     psum[m][:, 0:CTC],
                                     c_wd[:, par4,
                                          kl * MW + m * 128:kl * MW + (m + 1) * 128],
                                     htc[:, hb, kk, :],
                                     start=(kk == 0), stop=(kk == NH - 1)))
                    slab2_mark()
                k.wait("vector", "cT", k.cnt["cT"])
                k.wait("vector", hq, lq)
                for m in range(msz):
                    k.op("vector", "cV", lambda e, m=m, mof=mof, hb=hb:
                         e.scalar_tensor_tensor(
                             out=c_z2[:, mof + m, :], in0=psum[m][:, 0:CTC],
                             scalar=QS, in1=x1c[:, hb, mof + m, :],
                             op0=OP.mult, op1=OP.add))
                    k.psum_set_free(m, ("cV", k.cnt["cV"]))
            st2["dn"][tc] = k.cnt["cT"]
            st2["z2"][tc] = k.cnt["cV"]
            layernorm(c_z2, c_z,
                      out_tile=lambda i, lo=lo: outT_t[:, i, lo:lo + CTC],
                      ln_sync=True)

    for q in ("qG", "qS", "q0", "q1", "g0", "g1", "g2", "g3", "q2", "q3", "qW", "qL", "qY", "qYs", "qU", "qX", "qA", "cx0", "cx1", "cy0", "cy1", "qi0", "qi1", "qf0", "qf1", "qM", "cy2", "cy3"):
        k.wait("gpsimd", q, k.cnt[q])
    k.play()
    k.stack.close()
    return nc


_NC_CACHE = {}


def _get_nc(key, cfg):
    if key not in _NC_CACHE:
        _NC_CACHE[key] = build_nc(cfg)
    return _NC_CACHE[key]


def make_inputs(cfg, x, W_in, conv_w, conv_b, W_x, W_dt, b_dt, A_log, Dv, W_out,
                wg, wu, wd):
    g = _dv(cfg)
    DI, NS, DR, H = (g[k] for k in ("DI", "NS", "DR", "H"))
    HDD, MYT, NH, NXT = g["HDD"], g["MYT"], g["NH"], g["NXT"]
    NDT, XD, KC = g["NDT"], g["XD"], cfg["KC"]
    C = np.ascontiguousarray
    wgT = wg.T
    wuT = wu.T
    wdT = C(wd.T)
    wg_t = C(wgT.reshape(NXT, 128, NH, 128).transpose(2, 1, 0, 3))
    wu_t = C(wuT.reshape(NXT, 128, NH, 128).transpose(2, 1, 0, 3))
    w_res_c = C(W_in[:, DI:])
    w_out_c = C(W_out)
    ins = []
    for c in range(8):
        b, h = c >> 1, c & 1
        dd = slice(h * HDD, (h + 1) * HDD)
        ins.append({
            "xT": C(x[b].T),
            "xTmine": C(x[b, h * MYT:(h + 1) * MYT, :].T),
            "w_xin": C(W_in[:, dd]),
            "w_res": w_res_c,
            "w_x": C(W_x[dd, :].reshape(NDT, 128, XD).transpose(1, 0, 2)),
            "w_dt": C(W_dt[:, dd]),
            "b_dt": C(b_dt[dd].reshape(NDT, 128).T),
            "a_log": C(A_log[dd, :].reshape(NDT, 128, NS).transpose(1, 0, 2)),
            "d_vec": C(Dv[dd].reshape(NDT, 128).T),
            "conv_w": C(conv_w[dd, :].reshape(NDT, 128, KC).transpose(1, 0, 2)),
            "conv_b": C(conv_b[dd].reshape(NDT, 128).T),
            "w_out": w_out_c,
            "m_sel": C(np.tile(np.array([[1.0 - h, float(h)]], np.float32), (128, 1))),
            "ident": np.eye(128, dtype=np.float32),
            "wg_t": wg_t,
            "wu_t": wu_t,
            "wd_t": wdT,
        })
    return ins


def run(cfg, key, x, W_in, conv_w, conv_b, W_x, W_dt, b_dt, A_log, Dv, W_out,
        wg, wu, wd, runner=None):
    g = _dv(cfg)
    nc = _get_nc(key, cfg)
    ins = make_inputs(cfg, x, W_in, conv_w, conv_b, W_x, W_dt, b_dt, A_log,
                      Dv, W_out, wg, wu, wd)
    if runner is None:
        res = run_bass_kernel_spmd(nc, ins, core_ids=list(range(8)))
        outs = [np.asarray(r["outT"]) for r in res.results]
    else:
        outs = runner(nc, ins)
    B, L, D, MYT = x.shape[0], cfg["L"], cfg["D"], g["MYT"]
    out = np.empty((B, L, D), np.float32)
    for c in range(8):
        b, h = c >> 1, c & 1
        out[b, h * MYT:(h + 1) * MYT, :] = outs[c].T
    return out


def kernel(L0_attn_x, L0_attn_W_in, L0_attn_conv_w, L0_attn_conv_b, L0_attn_W_x,
           L0_attn_W_dt, L0_attn_b_dt, L0_attn_A_log, L0_attn_D, L0_attn_W_out,
           L0_ffn_w_gate, L0_ffn_w_up, L0_ffn_w_down):
    return run(FULL_CFG, "full", L0_attn_x, L0_attn_W_in, L0_attn_conv_w,
               L0_attn_conv_b, L0_attn_W_x, L0_attn_W_dt, L0_attn_b_dt,
               L0_attn_A_log, L0_attn_D, L0_attn_W_out,
               L0_ffn_w_gate, L0_ffn_w_up, L0_ffn_w_down)


# version 51
# speedup vs baseline: 1.0486x; 1.0130x over previous
from contextlib import ExitStack

import numpy as np

import concourse.bass as bass
import concourse.mybir as mybir
from concourse.bass_utils import run_bass_kernel_spmd

F32 = mybir.dt.float32
BF16 = mybir.dt.bfloat16
I32 = mybir.dt.int32
AF = mybir.ActivationFunctionType
OP = mybir.AluOpType

FULL_CFG = dict(L=8192, D=1024, DI=2048, NS=16, DR=64, KC=4, H=3072,
                CTA=512, CTB=1024, CTC=512, EPS=1e-5, QSCALE=1.0 / 127.0)


def _dv(cfg):
    d = dict(cfg)
    d["HDD"] = cfg["DI"] // 2
    d["NDT"] = d["HDD"] // 128
    d["NXT"] = cfg["D"] // 128
    d["XD"] = cfg["DR"] + 2 * cfg["NS"]
    d["MYT"] = cfg["L"] // 2
    d["NCHA"] = cfg["L"] // cfg["CTA"]
    d["NCHB"] = cfg["L"] // cfg["CTB"]
    d["NCHC"] = d["MYT"] // cfg["CTC"]
    d["NH"] = cfg["H"] // 128
    d["NDI"] = cfg["DI"] // 128
    d["TSB"] = min(512, cfg["CTB"])
    d["NTB"] = cfg["CTB"] // d["TSB"]
    assert d["XD"] <= 128 and cfg["DR"] <= 128
    assert cfg["CTA"] <= 512 and cfg["CTC"] <= 512
    assert d["NCHB"] % 2 == 0
    return d


class K:

    ENGS = ("sync", "gpsimd", "vector", "scalar", "tensor")
    OWN = {"vector": "cV", "scalar": "cS", "gpsimd": "cG"}

    def __init__(self, nc):
        self.nc = nc
        self.ops = {e: [] for e in self.ENGS}
        self.stack = ExitStack()
        self.sem = {}
        self.cnt = {}
        self.waited = {e: {} for e in self.ENGS}
        self.psum_free = {}

    def mksem(self, *names):
        for n in names:
            self.sem[n] = self.stack.enter_context(self.nc.semaphore(n))
            self.cnt[n] = 0

    def wait(self, eng, semname, val):
        if val <= 0:
            return
        if self.waited[eng].get(semname, -1) >= val:
            return
        self.waited[eng][semname] = val
        s = self.sem[semname]
        self.ops[eng].append(lambda e, s=s, v=val: e.wait_ge(s, v))

    def op(self, eng, semname, build, inc=1, nb=False):
        own = self.OWN.get(eng)
        if not nb and own is not None and eng in ("vector", "scalar", "gpsimd"):
            self.wait(eng, own, self.cnt[own])
        self.cnt[semname] += inc
        s = self.sem[semname]
        self.ops[eng].append(lambda e, b=build, s=s, i=inc: b(e).then_inc(s, i))
        return self.cnt[semname]

    def psum_guard(self, idx):
        for sname, v in self.psum_free.get(idx, ()):
            self.wait("tensor", sname, v)

    def psum_set_free(self, idx, *pairs):
        self.psum_free[idx] = pairs

    def barrier(self, exclude=("cc",)):
        snap = dict(self.cnt)
        for e in self.ENGS:
            for sname, v in snap.items():
                if sname in exclude:
                    continue
                self.wait(e, sname, v)

    def play(self):
        nc = self.nc
        with nc.Block() as block:
            @block.sync
            def _(e):
                for f in self.ops["sync"]:
                    f(e)

            @block.gpsimd
            def _(e):
                for f in self.ops["gpsimd"]:
                    f(e)

            @block.vector
            def _(e):
                for f in self.ops["vector"]:
                    f(e)

            @block.scalar
            def _(e):
                for f in self.ops["scalar"]:
                    f(e)

            @block.tensor
            def _(e):
                for f in self.ops["tensor"]:
                    f(e)


class Arena:

    def __init__(self, k, name, dtype, n):
        self.t = k.stack.enter_context(k.nc.sbuf_tensor(name, [128, n], dtype))
        self.n = n
        self.off = 0

    def alloc(self, shape):
        p = shape[0]
        cols = 1
        for s in shape[1:]:
            cols *= s
        off = (self.off + 31) & ~31
        ap = self.t[0:p, off:off + cols]
        self.off = off + cols
        assert self.off <= self.n, (self.off, self.n, shape)
        if len(shape) > 2:
            names = " ".join(f"d{i}" for i in range(1, len(shape)))
            kw = {f"d{i}": shape[i] for i in range(1, len(shape))}
            ap = ap.rearrange(f"p ({names}) -> p {names}", **kw)
        return ap

    def reset(self):
        self.off = 0


def build_nc(cfg):
    g = _dv(cfg)
    L, D, DI, NS, DR, KC, H = (g[k] for k in ("L", "D", "DI", "NS", "DR", "KC", "H"))
    CTA, CTB, CTC, EPS, QS = (g[k] for k in ("CTA", "CTB", "CTC", "EPS", "QSCALE"))
    HDD, NDT, NXT, XD, MYT = (g[k] for k in ("HDD", "NDT", "NXT", "XD", "MYT"))
    NCHA, NCHB, NCHC, NH, NDI = (g[k] for k in ("NCHA", "NCHB", "NCHC", "NH", "NDI"))
    TSB, NTB = g["TSB"], g["NTB"]

    nc = bass.Bass("TRN2", target_bir_lowering=False)
    k = K(nc)
    k.mksem("qS", "qG", "cV", "cS", "cT", "cG", "cc", "q0", "q1", "g0", "g1", "g2", "g3", "q2", "q3", "qW", "qL", "qY", "qYs", "qU", "qX", "qA", "cx0", "cx1", "cy0", "cy1", "qi0", "qi1", "qf0", "qf1", "qM", "cy2", "cy3")

    din = lambda n, s, t: nc.dram_tensor(n, list(s), t, kind="ExternalInput").ap()
    dint = lambda n, s, t: nc.dram_tensor(n, list(s), t, kind="Internal").ap()

    xT = din("xT", (D, L), F32)
    xTmine = din("xTmine", (D, MYT), F32)
    w_xin = din("w_xin", (D, HDD), F32)
    w_res = din("w_res", (D, DI), F32)
    w_x = din("w_x", (128, NDT, XD), F32)
    w_dt = din("w_dt", (DR, HDD), F32)
    b_dt = din("b_dt", (128, NDT), F32)
    a_log = din("a_log", (128, NDT, NS), F32)
    d_vecI = din("d_vec", (128, NDT), F32)
    conv_wI = din("conv_w", (128, NDT, KC), F32)
    conv_bI = din("conv_b", (128, NDT), F32)
    w_out = din("w_out", (DI, D), F32)
    m_sel = din("m_sel", (128, 2), F32)
    identI = din("ident", (128, 128), F32)
    wg_t = din("wg_t", (NH, 128, NXT, 128), I32)
    wu_t = din("wu_t", (NH, 128, NXT, 128), I32)
    wd_t = din("wd_t", (H, D), I32)
    outT = nc.dram_tensor("outT", [D, MYT], F32, kind="ExternalOutput").ap()

    d_u = dint("d_u", (HDD, L), BF16)
    NARP = g["NCHB"]
    d_xdblp_p = [dint(f"d_xdblp{p}", (XD, cfg["CTB"]), F32) for p in range(NARP)]
    d_xdbl_p = [dint(f"d_xdbl{p}", (XD, cfg["CTB"]), F32) for p in range(NARP)]
    d_bc = dint("d_bc", (2 * NS, L), BF16)
    NAGP = g["NCHB"] // 2
    d_y_p = [dint(f"d_y{p}", (2, HDD, cfg["CTB"]), BF16) for p in range(NAGP)]
    d_yg_p = [dint(f"d_yg{p}", (2, 2, HDD, cfg["CTB"]), BF16) for p in range(NAGP)]
    d_wres = dint("d_wres", (D, DI), BF16)
    d_wout = dint("d_wout", (DI, D), BF16)
    d_wd = dint("d_wd", (H, D), BF16)
    d_ln = dint("d_ln", (2, CTC), BF16)
    d_x1 = dint("d_x1", (D, MYT), BF16)
    d_h = dint("d_h", (cfg["H"] // 128, 128, MYT), BF16)

    RG = [[0, 1], [2, 3], [4, 5], [6, 7]]

    psum = [k.stack.enter_context(nc.psum_tensor(f"pb{i}", [128, 512], F32))
            for i in range(8)]

    sb = k.stack.enter_context

    s_wx = sb(nc.sbuf_tensor("s_wx", [128, NDT, XD], BF16))
    s_wdt = sb(nc.sbuf_tensor("s_wdt", [DR, HDD], BF16))
    s_bdt = sb(nc.sbuf_tensor("s_bdt", [128, NDT], F32))
    s_A = sb(nc.sbuf_tensor("s_A", [128, NDT, NS], F32))
    s_D = sb(nc.sbuf_tensor("s_D", [128, NDT], F32))
    s_cw = sb(nc.sbuf_tensor("s_cw", [128, NDT, KC], F32))
    s_cb = sb(nc.sbuf_tensor("s_cb", [128, NDT], F32))
    s_carry = sb(nc.sbuf_tensor("s_carry", [128, NDT, NS], F32))
    s_ones = sb(nc.sbuf_tensor("s_ones", [128, 1], BF16))
    s_wxin = sb(nc.sbuf_tensor("s_wxin", [128, NXT, HDD], BF16))
    s_msel = sb(nc.sbuf_tensor("s_msel", [128, 2], F32))
    s_eps = sb(nc.sbuf_tensor("s_eps", [128, 1], F32))
    s_id = sb(nc.sbuf_tensor("s_id", [128, 128], BF16))

    WSLA = min(6, max(NDI, NXT)) * 128
    WTW = max(HDD, D, NXT * 128)
    f32_sz = max(2 * WTW + NDT * NS
                 + NXT * CTA + CTA + 7 * CTB,
                 2 * NXT * CTC + 4 * CTC) + 64 * 14
    b16_sz = max(2 * WTW
                 + (NXT + NDT + 2) * CTA + NDT * (KC - 1 + CTA) + (11 + 4 * NDT) * CTB,
                 (NXT + 5) * CTC
                 + max(4 * WSLA + (3 * NXT + 2 * NDI + 4 * NDT) * CTC,
                       (NXT + 2) * MYT + 2 * CTC + 8 * NXT * 128,
                       4 * WSLA + (3 * NXT + 2 * NH) * CTC)) + 64 * 24
    AF32 = Arena(k, "ar_f32", F32, f32_sz)
    AB16 = Arena(k, "ar_b16", BF16, b16_sz)
    AI32 = Arena(k, "ar_i32", I32, 2 * 1024 + 64)

    r128 = lambda ap: ap.rearrange("(t p) x -> p t x", p=128)

    if True:
        w_tmpf = AF32.alloc([128, 2, WTW])
        w_tmpi = AI32.alloc([128, 2, 1024])
        w_tmpb = AB16.alloc([128, 2, WTW])
        s_alog = AF32.alloc([128, NDT, NS])
        s_wxf = w_tmpf[:, 0, 0:NDT * XD].rearrange("p (t x) -> p t x", t=NDT)
        s_wdtf = w_tmpf[0:DR, 1, 0:HDD]

        def ld(dst_ap, src_ap):
            k.op("sync", "qS", lambda e, d=dst_ap, s=src_ap: e.dma_start(out=d, in_=s), 16)

        ld(s_alog, a_log)
        ld(s_bdt[:], b_dt)
        ld(s_D[:], d_vecI)
        ld(s_cw[:], conv_wI)
        ld(s_cb[:], conv_bI)
        ld(s_wxf, w_x.rearrange("p t x -> p (t x)").rearrange(
            "p (t x) -> p t x", t=NDT))
        ld(s_wdtf, w_dt)
        ld(s_msel[:], m_sel)
        vq = k.cnt["qS"]
        k.wait("vector", "qS", vq)
        k.wait("scalar", "qS", vq)
        k.op("scalar", "cS", lambda e: e.activation(out=s_A[:], in_=s_alog[:], func=AF.Exp))
        k.wait("vector", "cS", k.cnt["cS"])
        k.op("vector", "cV", lambda e: e.tensor_scalar_mul(s_A[:], s_A[:], -1.0))
        k.op("vector", "cV", lambda e: e.tensor_copy(s_wx[:], s_wxf))
        k.op("vector", "cV", lambda e: e.tensor_copy(s_wdt[:], s_wdtf))
        k.op("vector", "cV", lambda e: e.memset(s_carry[:], 0.0))
        k.op("vector", "cV", lambda e: e.memset(s_ones[:], 1.0))
        k.op("vector", "cV", lambda e: e.memset(s_eps[:], EPS))
        wvc = k.cnt["cV"]

        k.wait("sync", "cV", wvc)
        k.op("sync", "qS", lambda e: e.dma_start(
            out=w_tmpf[:, 0, 0:128], in_=identI), 16)
        k.wait("vector", "qS", k.cnt["qS"])
        k.op("vector", "cV", lambda e: e.tensor_copy(s_id[:], w_tmpf[:, 0, 0:128]))

        wxq = {0: k.cnt["cV"], 1: wvc}
        for i in range(D // 128):
            par = i % 2
            tin = w_tmpf[:, par, 0:HDD]
            qn = f"q{2 + par}"
            k.wait("sync", "cV", wxq[par])
            k.op("sync", qn, lambda e, t=tin, i=i: e.dma_start(
                out=t, in_=w_xin[i * 128:(i + 1) * 128, :]), 16)
            k.wait("vector", qn, k.cnt[qn])
            k.op("vector", "cV", lambda e, i=i, t=tin: e.tensor_copy(
                s_wxin[:, i, :], t))
            wxq[par] = k.cnt["cV"]

        w_tiles = []
        for hh in range(2):
            for i in range(D // 128):
                w_tiles.append((w_res[i * 128:(i + 1) * 128,
                                      hh * (DI // 2):(hh + 1) * (DI // 2)],
                                d_wres[i * 128:(i + 1) * 128,
                                       hh * (DI // 2):(hh + 1) * (DI // 2)],
                                DI // 2, False))
        for i in range(DI // 128):
            w_tiles.append((w_out[i * 128:(i + 1) * 128, :],
                            d_wout[i * 128:(i + 1) * 128, :], D, False))
        for i in range(H // 128):
            w_tiles.append((wd_t[i * 128:(i + 1) * 128, :],
                            d_wd[i * 128:(i + 1) * 128, :], D, True))
        wst = dict(t=0, iss=0, q={}, store={0: 0, 1: 0}, slot={})

        def issue_w_load(t):
            if t >= len(w_tiles):
                return
            src, dst, ncols, is_int = w_tiles[t]
            par = t % 2
            tin = (w_tmpi if is_int else w_tmpf)[:, par, 0:ncols]
            qn = f"qi{par}" if is_int else f"qf{par}"
            if not is_int and not wst.get(("fgate", par)):
                k.wait("scalar", "cV", wxq[par])
                wst[("fgate", par)] = True
            k.wait("scalar", "cS", wst["slot"].get((is_int, par), 0))
            k.op("scalar", qn, lambda e, t_=tin, s=src: e.dma_start(
                out=t_, in_=s), 16, nb=True)
            wst["q"][t] = (qn, k.cnt[qn])

        def emit_w_tiles(n):
            while wst["iss"] < min(wst["t"] + 2, len(w_tiles)):
                issue_w_load(wst["iss"])
                wst["iss"] += 1
            for _ in range(n):
                if wst["t"] >= len(w_tiles):
                    return
                t = wst["t"]
                wst["t"] += 1
                src, dst, ncols, is_int = w_tiles[t]
                par = t % 2
                tin = (w_tmpi if is_int else w_tmpf)[:, par, 0:ncols]
                tout = w_tmpb[:, par, 0:ncols]
                gn = f"g{par}"
                qn, qv = wst["q"][t]
                k.wait("scalar", qn, qv)
                k.wait("scalar", gn, wst["store"][par])
                k.op("scalar", "cS", lambda e, o=tout, t_=tin: e.activation(
                    out=o, in_=t_, func=AF.Copy), nb=True)
                wst["slot"][(is_int, par)] = k.cnt["cS"]
                k.wait("gpsimd", "cS", k.cnt["cS"])
                k.op("gpsimd", gn, lambda e, d=dst, t_=tout: e.dma_start(
                    out=d, in_=t_), 16)
                wst["store"][par] = k.cnt[gn]
                if wst["iss"] < len(w_tiles):
                    issue_w_load(wst["iss"])
                    wst["iss"] += 1

        NWT = len(w_tiles)

    k.barrier()

    if True:
        a_x = AF32.alloc([128, NXT, CTA])
        a_xb = AB16.alloc([128, NXT, CTA])
        a_xin = AB16.alloc([128, NDT, KC - 1 + CTA])
        a_conv = AB16.alloc([128, 2, CTA])
        a_u = AB16.alloc([128, NDT, CTA])
        a_xdbl = AF32.alloc([XD, CTA])
        b_xdbl = AF32.alloc([XD, CTB])
        b_dtb = AB16.alloc([DR, CTB])
        b_r = AB16.alloc([128, CTB])
        b_a = AF32.alloc([128, CTB])
        b_l = AF32.alloc([128, CTB])
        b_dbf = AB16.alloc([128, NDT, CTB])
        b_du = AB16.alloc([128, NDT, CTB])
        b_u = AB16.alloc([128, NDT, CTB])
        b_bcb = AB16.alloc([2 * NS, CTB])
        b_brep = AB16.alloc([128, 2, CTB])
        b_crep = AB16.alloc([128, 2, CTB])
        b_dA = AF32.alloc([128, 4, CTB])
        b_h = AB16.alloc([128, 2, CTB])
        b_tmp = AB16.alloc([128, 2, CTB])
        b_y = AB16.alloc([128, NDT, CTB])
        xT_t = xT.rearrange("(t p) l -> p t l", p=128)
        d_u_t = d_u.rearrange("(t p) l -> p t l", p=128)

        ast = dict(xb_v=0, evac_s=0, conv_v=0, wx_t=0, ustore=[], xstore=[])

        def emit_A(c):
            lo = c * CTA
            evac_m = {}
            k.wait("sync", "cV", ast["xb_v"])
            k.op("sync", "qA",
                 lambda e, lo=lo: e.dma_start(out=a_x, in_=xT_t[:, :, lo:lo + CTA]), 16)
            k.wait("vector", "qA", k.cnt["qA"])
            k.wait("vector", "cS", ast["evac_s"])
            if c == 0:
                k.op("vector", "cV", lambda e: e.memset(a_xin[:, :, 0:KC - 1], 0.0))
            else:
                k.op("vector", "cV", lambda e: e.tensor_copy(
                    a_xin[:, :, 0:KC - 1], a_xin[:, :, CTA:CTA + KC - 1]))
            halo_v = k.cnt["cV"]
            k.op("vector", "cV", lambda e: e.tensor_copy(a_xb, a_x))
            ast["xb_v"] = k.cnt["cV"]

            k.wait("tensor", "cV", ast["xb_v"])
            for m in range(NDT):
                pi = m % 2
                k.psum_guard(pi)
                for kk in range(NXT):
                    k.op("tensor", "cT", lambda e, pi=pi, kk=kk, m=m: e.matmul(
                        psum[pi][:, 0:CTA],
                        s_wxin[:, kk, m * 128:(m + 1) * 128],
                        a_xb[:, kk, :],
                        start=(kk == 0), stop=(kk == NXT - 1)))
                mm_t = k.cnt["cT"]
                k.wait("scalar", "cT", mm_t)
                k.wait("scalar", "cV", halo_v)
                k.op("scalar", "cS", lambda e, m=m, pi=pi: e.activation(
                    out=a_xin[:, m, KC - 1:KC - 1 + CTA], in_=psum[pi][:, 0:CTA],
                    func=AF.Copy))
                evac_m[m] = k.cnt["cS"]
                k.psum_set_free(pi, ("cS", k.cnt["cS"]))
            ast["evac_s"] = k.cnt["cS"]

            silu_done = {}
            for m in range(NDT):
                cp = m % 2
                k.wait("vector", "cS", evac_m[m])
                if m >= 2:
                    k.wait("vector", "cS", silu_done[m - 2])
                k.op("vector", "cV", lambda e, m=m, cp=cp: e.tensor_scalar(
                    out=a_conv[:, cp, :], in0=a_xin[:, m, 0:CTA],
                    scalar1=s_cw[:, m, 0:1], scalar2=s_cb[:, m:m + 1],
                    op0=OP.mult, op1=OP.add))
                for t in range(1, KC):
                    k.op("vector", "cV", lambda e, m=m, t=t, cp=cp: e.scalar_tensor_tensor(
                        out=a_conv[:, cp, :], in0=a_xin[:, m, t:t + CTA],
                        scalar=s_cw[:, m, t:t + 1], in1=a_conv[:, cp, :],
                        op0=OP.mult, op1=OP.add))
                k.wait("scalar", "cV", k.cnt["cV"])
                k.wait("scalar", "cT", ast["wx_t"])
                if ast["ustore"]:
                    k.wait("scalar", "qU", ast["ustore"][-1])
                k.op("scalar", "cS", lambda e, m=m, cp=cp: e.activation(
                    out=a_u[:, m, :], in_=a_conv[:, cp, :], func=AF.Silu), nb=True)
                silu_done[m] = k.cnt["cS"]
            silu_s = k.cnt["cS"]

            k.wait("tensor", "cS", silu_s)
            k.psum_guard(2)
            for kk in range(NDT):
                k.op("tensor", "cT", lambda e, kk=kk: e.matmul(
                    psum[2][0:XD, 0:CTA], s_wx[:, kk, :], a_u[:, kk, :],
                    start=(kk == 0), stop=(kk == NDT - 1)))
            ast["wx_t"] = k.cnt["cT"]
            k.wait("scalar", "cT", ast["wx_t"])
            if ast["xstore"]:
                k.wait("scalar", "qX", ast["xstore"][-1])
            k.op("scalar", "cS", lambda e: e.activation(
                out=a_xdbl[:], in_=psum[2][0:XD, 0:CTA], func=AF.Copy))
            k.psum_set_free(2, ("cS", k.cnt["cS"]))

            k.wait("gpsimd", "cS", k.cnt["cS"])
            k.op("gpsimd", "qU",
                 lambda e, lo=lo: e.dma_start(out=d_u_t[:, :, lo:lo + CTA], in_=a_u), 16)
            ast["ustore"].append(k.cnt["qU"])
            pp = c // (CTB // CTA)
            off = (c % (CTB // CTA)) * CTA
            k.op("gpsimd", "qX",
                 lambda e, pp=pp, off=off: e.dma_start(
                     out=d_xdblp_p[pp][:, off:off + CTA], in_=a_xdbl), 16)
            ast["xstore"].append(k.cnt["qX"])

        ar_cc = {}
        ag_cc = {}

        def emit_AR(p):
            k.wait("gpsimd", "qU", k.cnt["qU"])
            k.wait("gpsimd", "qX", k.cnt["qX"])
            ar_cc[p] = k.op("gpsimd", "cc", lambda e, p=p: e.collective_compute(
                "AllReduce", OP.add, replica_groups=RG,
                ins=[d_xdblp_p[p]], outs=[d_xdbl_p[p]]))

        scan_done = {}
        add_done = {}
        bst = dict(prev_yinit=0, prev_store=0)
        HTB = CTB // 2
        assert HTB <= 512
        OFFM = dict(zip(list(range(NDT))[-2:], ((5, 6), (7, 3))))
        idmm = {}

        def emit_B(c):
            lo = c * CTB
            k.wait("sync", "cc", ar_cc[c])
            k.wait("sync", "cV", bst["prev_yinit"])
            k.op("sync", "qS", lambda e, c=c: e.dma_start(
                out=b_xdbl, in_=d_xdbl_p[c][:]), 16)
            k.wait("sync", "qU", ast["ustore"][2 * c + 1])
            k.op("sync", "qS", lambda e, lo=lo: e.dma_start(
                out=b_u, in_=d_u_t[:, :, lo:lo + CTB]), 16)
            k.wait("vector", "qS", k.cnt["qS"])
            k.op("vector", "cV", lambda e: e.tensor_copy(b_dtb, b_xdbl[0:DR, :]))
            k.op("vector", "cV", lambda e: e.tensor_copy(b_bcb, b_xdbl[DR:XD, :]))
            cast_v = k.cnt["cV"]
            k.wait("gpsimd", "cV", cast_v)
            k.op("gpsimd", "qG", lambda e, lo=lo: e.dma_start(
                out=d_bc[:, lo:lo + CTB], in_=b_bcb), 16)
            bc_q = k.cnt["qG"]

            k.wait("tensor", "cV", cast_v)
            for m in range(NDT):
                def emit_dbf(m, sl, pln):
                    k.wait("vector", "cS", pln)
                    k.op("vector", "cV", lambda e, m=m, sl=sl: e.tensor_tensor(
                        out=b_dbf[:, m, sl], in0=b_r[:, sl], in1=b_l[:, sl],
                        op=OP.add))
                    k.op("vector", "cV", lambda e, m=m, sl=sl: e.tensor_tensor(
                        out=b_du[:, m, sl], in0=b_dbf[:, m, sl],
                        in1=b_u[:, m, sl], op=OP.mult))

                hp = None
                for ts in range(NTB):
                    pi = 3 + (ts % 2)
                    sl = slice(ts * TSB, (ts + 1) * TSB)
                    k.psum_guard(pi)
                    k.op("tensor", "cT", lambda e, m=m, sl=sl, pi=pi: e.matmul(
                        psum[pi][:, 0:TSB], s_wdt[:, m * 128:(m + 1) * 128],
                        b_dtb[:, sl], start=True, stop=True))
                    mm_t = k.cnt["cT"]
                    k.wait("vector", "cT", mm_t)
                    k.op("vector", "cV", lambda e, pi=pi, sl=sl, m=m: e.tensor_scalar(
                        out=b_r[:, sl], in0=psum[pi][:, 0:TSB],
                        scalar1=s_bdt[:, m:m + 1], scalar2=0.0,
                        op0=OP.add, op1=OP.max))
                    k.op("vector", "cV", lambda e, pi=pi, sl=sl, m=m: e.tensor_scalar(
                        out=b_a[:, sl], in0=psum[pi][:, 0:TSB],
                        scalar1=s_bdt[:, m:m + 1], scalar2=None, op0=OP.add))
                    k.psum_set_free(pi, ("cV", k.cnt["cV"]))
                    k.op("vector", "cV", lambda e, sl=sl: e.scalar_tensor_tensor(
                        out=b_a[:, sl], in0=b_r[:, sl], scalar=-2.0,
                        in1=b_a[:, sl], op0=OP.mult, op1=OP.add))
                    stt_v = k.cnt["cV"]
                    k.wait("scalar", "cV", stt_v)
                    k.op("scalar", "cS", lambda e, sl=sl: e.activation(
                        out=b_l[:, sl], in_=b_a[:, sl], func=AF.Exp), nb=True)
                    k.op("scalar", "cS", lambda e, sl=sl: e.activation(
                        out=b_l[:, sl], in_=b_l[:, sl], func=AF.Ln, bias=1.0))
                    if hp is not None:
                        emit_dbf(m, hp[0], hp[1])
                    hp = (sl, k.cnt["cS"])
                emit_dbf(m, hp[0], hp[1])
            du_v = k.cnt["cV"]

            k.wait("vector", "qYs", bst["prev_store"])
            for m in range(NDT):
                k.op("vector", "cV", lambda e, m=m: e.tensor_scalar(
                    out=b_y[:, m, :], in0=b_u[:, m, :], scalar1=s_D[:, m:m + 1],
                    scalar2=None, op0=OP.mult))
            bst["prev_yinit"] = k.cnt["cV"]

            k.wait("sync", "qG", bc_q)
            for n in range(NS):
                par = n % 2
                qn = f"q{par}"
                gn = c * NS + n
                if gn >= 2:
                    k.wait("sync", "cV", scan_done[(gn - 2, "mul", NDT - 1)])
                k.op("sync", qn, lambda e, n=n, lo=lo, par=par: e.dma_start(
                    out=b_brep[:, par, :],
                    in_=d_bc[n:n + 1, lo:lo + CTB].broadcast_to([128, CTB])), 16)
                k.op("sync", qn, lambda e, n=n, lo=lo, par=par: e.dma_start(
                    out=b_crep[:, par, :],
                    in_=d_bc[NS + n:NS + n + 1, lo:lo + CTB].broadcast_to([128, CTB])), 16)
                rep_q = k.cnt[qn]
                for mp in range(0, NDT, 2):
                    ms = [m for m in (mp, mp + 1) if m < NDT]
                    dA_s = {}
                    for m in ms:
                        u = gn * NDT + m
                        up = u % 4
                        if u >= 4:
                            k.wait("scalar", "cV", scan_done[((u - 4) // NDT, "scan", (u - 4) % NDT)])
                        k.wait("scalar", "cV", du_v)
                        k.op("scalar", "cS", lambda e, m=m, n=n, up=up: e.activation(
                            out=b_dA[:, up, :], in_=b_dbf[:, m, :], func=AF.Exp,
                            scale=s_A[:, m, n:n + 1]), nb=True)
                        dA_s[m] = k.cnt["cS"]
                    k.wait("vector", qn, rep_q)
                    for m in ms:
                        u = gn * NDT + m
                        if (u - 2) in idmm:
                            k.wait("vector", "cT", idmm[u - 2])
                    if len(ms) == 2:
                        k.op("vector", "cV", lambda e, mp=mp, par=par: e.tensor_tensor(
                            out=b_tmp[:, 0:2, :], in0=b_du[:, mp:mp + 2, :],
                            in1=b_brep[:, par:par + 1, :]
                            .broadcast_to([128, 2, CTB]), op=OP.mult))
                    else:
                        m0 = ms[0]
                        u2 = (gn * NDT + m0) % 2
                        k.op("vector", "cV", lambda e, m0=m0, u2=u2, par=par: e.tensor_tensor(
                            out=b_tmp[:, u2, :], in0=b_du[:, m0, :], in1=b_brep[:, par, :],
                            op=OP.mult))
                    for m in ms:
                        u = gn * NDT + m
                        u2 = u % 2
                        up = u % 4
                        k.wait("vector", "cS", dA_s[m])
                        k.op("vector", "cV", lambda e, m=m, n=n, up=up, u2=u2: e.tensor_tensor_scan(
                            out=b_h[:, u2, :], data0=b_dA[:, up, :], data1=b_tmp[:, u2, :],
                            initial=s_carry[:, m, n:n + 1],
                            op0=OP.mult, op1=OP.add))
                        scan_done[(gn, "scan", m)] = k.cnt["cV"]
                    if len(ms) == 2:
                        k.op("vector", "cV", lambda e, mp=mp, n=n: e.tensor_copy(
                            s_carry[:, mp:mp + 2, n:n + 1],
                            b_h[:, 0:2, CTB - 1:CTB]))
                    else:
                        m0 = ms[0]
                        u2 = (gn * NDT + m0) % 2
                        k.op("vector", "cV", lambda e, m0=m0, n=n, u2=u2: e.tensor_copy(
                            s_carry[:, m0, n:n + 1], b_h[:, u2, CTB - 1:CTB]))
                    if len(ms) == 2:
                        k.op("vector", "cV", lambda e, par=par: e.tensor_tensor(
                            out=b_tmp[:, 0:2, :], in0=b_h[:, 0:2, :],
                            in1=b_crep[:, par:par + 1, :]
                            .broadcast_to([128, 2, CTB]), op=OP.mult))
                        for m in ms:
                            scan_done[(gn, "mul", m)] = k.cnt["cV"]
                    else:
                        m0 = ms[0]
                        u2 = (gn * NDT + m0) % 2
                        k.op("vector", "cV", lambda e, u2=u2, par=par: e.tensor_tensor(
                            out=b_tmp[:, u2, :], in0=b_h[:, u2, :], in1=b_crep[:, par, :],
                            op=OP.mult))
                        scan_done[(gn, "mul", m0)] = k.cnt["cV"]
                    if ms[0] in OFFM:
                        for m in ms:
                            u = gn * NDT + m
                            u2 = u % 2
                            blo, bhi = OFFM[m]
                            if n == 0:
                                k.psum_guard(blo)
                                k.psum_guard(bhi)
                            k.wait("tensor", "cV", scan_done[(gn, "mul", m)])
                            k.op("tensor", "cT", lambda e, u2=u2, blo=blo, n=n: e.matmul(
                                psum[blo][:, 0:HTB], s_id[:], b_tmp[:, u2, 0:HTB],
                                start=(n == 0), stop=(n == NS - 1)))
                            k.op("tensor", "cT", lambda e, u2=u2, bhi=bhi, n=n: e.matmul(
                                psum[bhi][:, 0:HTB], s_id[:], b_tmp[:, u2, HTB:CTB],
                                start=(n == 0), stop=(n == NS - 1)))
                            idmm[u] = k.cnt["cT"]
                    elif len(ms) == 2:
                        k.op("vector", "cV", lambda e, mp=mp: e.tensor_tensor(
                            out=b_y[:, mp:mp + 2, :], in0=b_y[:, mp:mp + 2, :],
                            in1=b_tmp[:, 0:2, :], op=OP.add))
                    else:
                        m0 = ms[0]
                        u2 = (gn * NDT + m0) % 2
                        k.op("vector", "cV", lambda e, m0=m0, u2=u2: e.tensor_tensor(
                            out=b_y[:, m0, :], in0=b_y[:, m0, :], in1=b_tmp[:, u2, :],
                            op=OP.add))

            for m, (blo, bhi) in OFFM.items():
                k.wait("vector", "cT", idmm[(c * NS + NS - 1) * NDT + m])
                k.op("vector", "cV", lambda e, m=m, blo=blo: e.tensor_tensor(
                    out=b_y[:, m, 0:HTB], in0=b_y[:, m, 0:HTB],
                    in1=psum[blo][:, 0:HTB], op=OP.add))
                k.op("vector", "cV", lambda e, m=m, bhi=bhi: e.tensor_tensor(
                    out=b_y[:, m, HTB:CTB], in0=b_y[:, m, HTB:CTB],
                    in1=psum[bhi][:, 0:HTB], op=OP.add))
                k.psum_set_free(blo, ("cV", k.cnt["cV"]))
                k.psum_set_free(bhi, ("cV", k.cnt["cV"]))

            s = c // (NCHB // 2)
            p = c % (NCHB // 2)
            k.wait("gpsimd", "cV", k.cnt["cV"])
            k.op("gpsimd", "qYs", lambda e, s=s, p=p: e.dma_start(
                out=d_y_p[p][s].rearrange("(t p) l -> p t l", p=128),
                in_=b_y), 16)
            bst["prev_store"] = k.cnt["qYs"]
            if s == 1:
                k.wait("gpsimd", "qYs", k.cnt["qYs"])
                ag_cc[p] = k.op("gpsimd", "cc", lambda e, p=p: e.collective_compute(
                    "AllGather", OP.bypass, replica_groups=RG,
                    ins=[d_y_p[p]], outs=[d_yg_p[p]]))

        RAT = CTB // CTA
        WPC = -(-NWT // NCHB)
        for pp in range(min(2, NCHB)):
            emit_A(RAT * pp)
            emit_A(RAT * pp + 1)
            emit_AR(pp)
        for p in range(NCHB):
            if p + 2 < NCHB:
                emit_A(RAT * (p + 2))
                emit_A(RAT * (p + 2) + 1)
                emit_AR(p + 2)
            emit_B(p)
            emit_w_tiles(WPC)
        emit_w_tiles(NWT)

    k.barrier()
    AF32.reset(); AB16.reset(); AI32.reset()

    if True:
        c_sq = AB16.alloc([128, CTC])
        c_mean = AF32.alloc([1, CTC])
        c_msq = AF32.alloc([1, CTC])
        c_var = AF32.alloc([1, CTC])
        c_mr = AB16.alloc([128, CTC])
        c_rr = AB16.alloc([128, CTC])
        c_lnr0 = AB16.alloc([1, CTC])
        c_lnr1 = AB16.alloc([1, CTC])
        c_out = AF32.alloc([128, CTC])
        c_z = AB16.alloc([128, NXT, CTC])
        c2_f32_mark = AF32.off
        c2_b16_mark = AB16.off
        WSL1 = min(6, max(NDI, NXT)) * 128
        c_wsl = AB16.alloc([128, 4, WSL1])
        c_x = AF32.alloc([128, 2, NXT, CTC])
        c_xb = AB16.alloc([128, 2, NXT, CTC])
        c_y = AB16.alloc([128, NDI, CTC])
        c_yc = AB16.alloc([128, 4, NDT, CTC])
        c_res = AB16.alloc([128, NDI, CTC])
        c_x1 = AB16.alloc([128, NXT, CTC])
        xm_t = xTmine.rearrange("(t p) l -> p t l", p=128)
        outT_t = outT.rearrange("(t p) l -> p t l", p=128)
        dx1_t = d_x1.rearrange("(t p) l -> p t l", p=128)

        def grps(n):
            out, o = [], 0
            while o < n:
                sz = min(6, n - o)
                out.append((o, sz))
                o += sz
            return out

        def layernorm(src3, dst3, out_tile=None, mid=None, dst_wait=None,
                      ln_sync=False):
            k.psum_guard(6)
            k.psum_guard(7)
            k.wait("scalar", "cV", k.cnt["cV"])
            for i in range(NXT):
                k.wait("scalar", "cT", k.cnt["cT"])
                k.op("scalar", "cS", lambda e, i=i: e.activation(
                    out=c_sq, in_=src3[:, i, :], func=AF.Square))
                sq_s = k.cnt["cS"]
                k.wait("tensor", "cS", sq_s)
                k.op("tensor", "cT", lambda e, i=i: e.matmul(
                    psum[6][0:1, 0:CTC], s_ones[:], src3[:, i, :],
                    start=(i == 0), stop=(i == NXT - 1)))
                k.op("tensor", "cT", lambda e, i=i: e.matmul(
                    psum[7][0:1, 0:CTC], s_ones[:], c_sq[:],
                    start=(i == 0), stop=(i == NXT - 1)))
            st_t = k.cnt["cT"]
            k.wait("vector", "cT", st_t)
            k.op("vector", "cV", lambda e: e.tensor_scalar_mul(
                c_mean, psum[6][0:1, 0:CTC], 1.0 / D))
            k.op("vector", "cV", lambda e: e.tensor_scalar_mul(
                c_msq, psum[7][0:1, 0:CTC], 1.0 / D))
            k.psum_set_free(6, ("cV", k.cnt["cV"]))
            k.psum_set_free(7, ("cV", k.cnt["cV"]))
            k.op("vector", "cV", lambda e: e.tensor_tensor(
                out=c_var, in0=c_mean, in1=c_mean, op=OP.mult))
            k.op("vector", "cV", lambda e: e.tensor_tensor(
                out=c_var, in0=c_msq, in1=c_var, op=OP.subtract))
            var_v = k.cnt["cV"]
            k.wait("scalar", "cV", var_v)
            k.op("scalar", "cS", lambda e: e.activation(
                out=c_var, in_=c_var, func=AF.Ln, bias=s_eps[0:1, 0:1]))
            k.op("scalar", "cS", lambda e: e.activation(
                out=c_msq, in_=c_var, func=AF.Exp, scale=-0.5))
            rows_s = k.cnt["cS"]
            k.wait("vector", "cS", rows_s)
            k.op("vector", "cV", lambda e: e.tensor_copy(c_lnr0, c_mean))
            k.op("vector", "cV", lambda e: e.tensor_copy(c_lnr1, c_msq))
            rows_v = k.cnt["cV"]
            if ln_sync:
                k.wait("sync", "cV", rows_v)
                k.wait("sync", "qL", k.cnt["qL"])
                k.op("sync", "qL", lambda e: e.dma_start(
                    out=d_ln[0:1, :], in_=c_lnr0), 16)
                k.op("sync", "qL", lambda e: e.dma_start(
                    out=d_ln[1:2, :], in_=c_lnr1), 16)
                k.wait("sync", "qL", k.cnt["qL"])
            else:
                k.wait("gpsimd", "cV", rows_v)
                k.wait("gpsimd", "qM", k.cnt["qM"])
                k.op("gpsimd", "qG", lambda e: e.dma_start(
                    out=d_ln[0:1, :], in_=c_lnr0), 16)
                k.op("gpsimd", "qG", lambda e: e.dma_start(
                    out=d_ln[1:2, :], in_=c_lnr1), 16)
                k.wait("scalar", "qG", k.cnt["qG"])
            leng = "sync" if ln_sync else "scalar"
            lsem = "qL" if ln_sync else "qM"
            k.op(leng, lsem, lambda e: e.dma_start(
                out=c_mr, in_=d_ln[0:1, :].broadcast_to([128, CTC])), 16, nb=True)
            k.op(leng, lsem, lambda e: e.dma_start(
                out=c_rr, in_=d_ln[1:2, :].broadcast_to([128, CTC])), 16, nb=True)
            if mid is not None:
                mid()
            k.wait("vector", lsem, k.cnt[lsem])
            if dst_wait is not None:
                k.wait("vector", dst_wait[0], dst_wait[1])
            for i in range(NXT):
                k.op("vector", "cV", lambda e, i=i: e.tensor_tensor(
                    out=dst3[:, i, :], in0=src3[:, i, :], in1=c_mr,
                    op=OP.subtract))
                if out_tile is None:
                    k.op("vector", "cV", lambda e, i=i: e.tensor_tensor(
                        out=dst3[:, i, :], in0=dst3[:, i, :], in1=c_rr,
                        op=OP.mult))
                else:
                    k.wait("vector", "qY", k.cnt["qY"])
                    k.op("vector", "cV", lambda e, i=i: e.tensor_tensor(
                        out=c_out[:], in0=dst3[:, i, :], in1=c_rr,
                        op=OP.mult))
                    ov = k.cnt["cV"]
                    k.wait("sync", "cV", ov)
                    k.op("sync", "qY",
                         lambda e, i=i: e.dma_start(out=out_tile(i), in_=c_out), 16)

        NPC = 4
        pieces = [(c, j, s) for c in range(NCHC) for j in range(2)
                  for s in range(2)]
        st = dict(zadd={-1: 0, -2: 0}, wout={-1: 0}, res={-1: 0, -2: 0},
                  selv={}, ycq={}, cast={}, xq={}, x1st={-1: 0}, silu_gm={},
                  g_gm={})
        slab = dict(i=0, done={})

        def slab_load(src, W):
            i = slab["i"]
            par4 = i % 4
            qn = f"q{par4}"
            if i >= 4:
                k.wait("sync", "cT", slab["done"][i - 4])
            k.op("sync", qn, lambda e, s=src, par4=par4, W=W: e.dma_start(
                out=c_wsl[:, par4, 0:W], in_=s), 16)
            k.wait("tensor", qn, k.cnt[qn])
            return par4

        def slab_mark():
            slab["done"][slab["i"]] = k.cnt["cT"]
            slab["i"] += 1

        def x_load(c):
            if c >= NCHC:
                return
            par = c % 2
            k.wait("scalar", "cV", st["zadd"][c - 2])
            k.op("scalar", f"cx{par}", lambda e, lo=c * CTC, par=par: e.dma_start(
                out=c_x[:, par], in_=xm_t[:, :, lo:lo + CTC]), 16, nb=True)
            st["xq"][c] = k.cnt[f"cx{par}"]

        def x_cast(c):
            if c >= NCHC:
                return
            par = c % 2
            k.wait("vector", f"cx{par}", st["xq"][c])
            k.wait("vector", "cT", st["res"][c - 2])
            k.op("vector", "cV", lambda e, par=par: e.tensor_copy(
                c_xb[:, par], c_x[:, par]))
            st["cast"][c] = k.cnt["cV"]

        def piece_load(gi):
            if gi >= len(pieces):
                return
            c, j, s = pieces[gi]
            lo = c * CTC
            pidx = lo // CTB
            poff = lo % CTB
            slot = gi % 4
            sn = f"cy{slot}"
            k.wait("scalar", "cV", st["selv"].get(gi - 4, 0))
            k.wait("scalar", "cc", ag_cc[pidx])
            k.op("scalar", sn, lambda e, j=j, s=s, pidx=pidx, poff=poff, slot=slot:
                 e.dma_start(out=c_yc[:, slot],
                             in_=d_yg_p[pidx][j, s, :, poff:poff + CTC]
                             .rearrange("(t p) l -> p t l", p=128)), 16, nb=True)
            st["ycq"][gi] = (sn, k.cnt[sn])

        def emit_select(c):
            for i in range(NPC):
                gi = NPC * c + i
                _, j, s = pieces[gi]
                slot = gi % 4
                sn, sv = st["ycq"][gi]
                k.wait("vector", sn, sv)
                k.wait("vector", "cT", st["wout"][c - 1])
                for m in range(NDT):
                    if s == 0:
                        k.op("vector", "cV", lambda e, j=j, m=m, slot=slot:
                             e.tensor_scalar(
                                 out=c_y[:, j * NDT + m, :],
                                 in0=c_yc[:, slot, m, :],
                                 scalar1=s_msel[:, 0:1], scalar2=None,
                                 op0=OP.mult))
                    else:
                        k.op("vector", "cV", lambda e, j=j, m=m, slot=slot:
                             e.scalar_tensor_tensor(
                                 out=c_y[:, j * NDT + m, :],
                                 in0=c_yc[:, slot, m, :],
                                 scalar=s_msel[:, 1:2],
                                 in1=c_y[:, j * NDT + m, :],
                                 op0=OP.mult, op1=OP.add))
                st["selv"][gi] = k.cnt["cV"]
                piece_load(gi + 4)

        def emit_res(c):
            par = c % 2
            k.wait("tensor", "cV", st["cast"][c])
            for (gof, gsz) in grps(NDI):
                for pi in range(gsz):
                    k.psum_guard(pi)
                GW = gsz * 128
                for kk in range(NXT):
                    par4 = slab_load(
                        d_wres[kk * 128:(kk + 1) * 128,
                               gof * 128:gof * 128 + GW], GW)
                    for m in range(gsz):
                        k.op("tensor", "cT", lambda e, m=m, kk=kk, par4=par4,
                             par=par: e.matmul(
                                 psum[m][:, 0:CTC],
                                 c_wsl[:, par4, m * 128:(m + 1) * 128],
                                 c_xb[:, par, kk, :],
                                 start=(kk == 0), stop=(kk == NXT - 1)))
                    slab_mark()
                last_mm = k.cnt["cT"]
                for m in range(gsz):
                    gm = gof + m
                    k.wait("scalar", "cT", last_mm)
                    k.op("scalar", "cS", lambda e, m=m, gm=gm: e.activation(
                        out=c_res[:, gm, :], in_=psum[m][:, 0:CTC],
                        func=AF.Silu))
                    st["silu_gm"][gm] = k.cnt["cS"]
                    k.psum_set_free(m, ("cS", k.cnt["cS"]))
            st["res"][c] = k.cnt["cT"]

        def emit_gw(c):
            par = c % 2
            for gm in range(NDI):
                k.wait("vector", "cS", st["silu_gm"][gm])
                k.op("vector", "cV", lambda e, gm=gm: e.tensor_tensor(
                    out=c_y[:, gm, :], in0=c_y[:, gm, :], in1=c_res[:, gm, :],
                    op=OP.mult))
                st["g_gm"][gm] = k.cnt["cV"]
            for (mof, msz) in grps(NXT):
                for pi in range(msz):
                    k.psum_guard(pi)
                MW = msz * 128
                for kk in range(NDI):
                    par4 = slab_load(
                        d_wout[kk * 128:(kk + 1) * 128,
                               mof * 128:mof * 128 + MW], MW)
                    k.wait("tensor", "cV", st["g_gm"][kk])
                    for m in range(msz):
                        k.op("tensor", "cT", lambda e, m=m, kk=kk, par4=par4:
                             e.matmul(
                                 psum[m][:, 0:CTC],
                                 c_wsl[:, par4, m * 128:(m + 1) * 128],
                                 c_y[:, kk, :],
                                 start=(kk == 0), stop=(kk == NDI - 1)))
                    slab_mark()
                k.wait("vector", "cT", k.cnt["cT"])
                k.wait("vector", f"cx{par}", st["xq"][c])
                for m in range(msz):
                    k.op("vector", "cV", lambda e, m=m, mof=mof, par=par:
                         e.tensor_tensor(
                             out=c_z[:, mof + m, :],
                             in0=c_x[:, par, mof + m, :],
                             in1=psum[m][:, 0:CTC], op=OP.add))
                    k.psum_set_free(m, ("cV", k.cnt["cV"]))
            st["wout"][c] = k.cnt["cT"]
            st["zadd"][c] = k.cnt["cV"]

        for gi0 in range(4):
            piece_load(gi0)
        x_load(0)
        for c in range(NCHC):
            x_load(c + 1)
            if c == 0:
                x_cast(0)
                emit_select(0)
            x_cast(c + 1)
            emit_res(c)
            emit_gw(c)
            layernorm(c_z, c_x1,
                      mid=(lambda c=c: emit_select(c + 1))
                      if c + 1 < NCHC else None,
                      dst_wait=("qYs", st["x1st"][c - 1]))
            k.wait("gpsimd", "cV", k.cnt["cV"])
            k.op("gpsimd", "qYs", lambda e, lo=c * CTC: e.dma_start(
                out=dx1_t[:, :, lo:lo + CTC], in_=c_x1), 16)
            st["x1st"][c] = k.cnt["qYs"]

        k.barrier()
        AF32.off = c2_f32_mark
        AB16.off = c2_b16_mark
        x1a = AB16.alloc([128, NXT, MYT])
        hmb = AB16.alloc([128, 2, MYT])
        sgp = AB16.alloc([128, 2, CTC])
        c_wg = AB16.alloc([128, 4, NXT, 128])
        c_wu = AB16.alloc([128, 4, NXT, 128])
        c_wi = AI32.alloc([128, 2, NXT * 128])
        k.op("sync", "qW", lambda e: e.dma_start(out=x1a, in_=dx1_t), 16)
        k.wait("tensor", "qW", k.cnt["qW"])
        wgi = wg_t.rearrange("m p t c -> m p (t c)")
        wui = wu_t.rearrange("m p t c -> m p (t c)")
        NTC2 = MYT // CTC
        ffn_mm = {}
        hst = {}
        wc = {0: 0, 1: 0}
        wgu_done = {}

        def emit_wgu_cast(m):
            if m >= NH:
                return
            par = m % 4
            for wh in range(2):
                pari = (2 * m + wh) % 2
                qn = f"q{pari}"
                k.wait("sync", "cS", wc[pari])
                k.op("sync", qn, lambda e, m=m, pari=pari, wh=wh: e.dma_start(
                    out=c_wi[:, pari], in_=(wgi if wh == 0 else wui)[m]), 16)
                k.wait("scalar", qn, k.cnt[qn])
                k.wait("scalar", "cT", ffn_mm.get(m - 4, 0))
                dst = (c_wg if wh == 0 else c_wu)[:, par].rearrange(
                    "p t c -> p (t c)")
                k.op("scalar", "cS", lambda e, dst=dst, pari=pari: e.activation(
                    out=dst, in_=c_wi[:, pari], func=AF.Copy), nb=True)
                wc[pari] = k.cnt["cS"]
            wgu_done[m] = k.cnt["cS"]

        emit_wgu_cast(0)
        emit_wgu_cast(1)
        for m in range(NH):
            par = m % 4
            emit_wgu_cast(m + 2)
            k.wait("tensor", "cS", wgu_done[m])
            hb = m % 2
            if m >= 2:
                k.wait("vector", hst[m - 2][0], hst[m - 2][1])
            for tc in range(NTC2):
                sl = slice(tc * CTC, (tc + 1) * CTC)
                pg = 4 + 2 * (tc % 2)
                pu = pg + 1
                k.psum_guard(pg)
                k.psum_guard(pu)
                for kk in range(NXT):
                    k.op("tensor", "cT", lambda e, kk=kk, par=par, sl=sl, pg=pg: e.matmul(
                        psum[pg][:, 0:CTC], c_wg[:, par, kk, :], x1a[:, kk, sl],
                        start=(kk == 0), stop=(kk == NXT - 1)))
                for kk in range(NXT):
                    k.op("tensor", "cT", lambda e, kk=kk, par=par, sl=sl, pu=pu: e.matmul(
                        psum[pu][:, 0:CTC], c_wu[:, par, kk, :], x1a[:, kk, sl],
                        start=(kk == 0), stop=(kk == NXT - 1)))
                mm_t = k.cnt["cT"]
                k.wait("scalar", "cT", mm_t)
                sp = tc % 2
                k.op("scalar", "cS", lambda e, sp=sp, pg=pg: e.activation(
                    out=sgp[:, sp, :], in_=psum[pg][:, 0:CTC], func=AF.Silu, scale=QS))
                sg_s = k.cnt["cS"]
                k.psum_set_free(pg, ("cS", sg_s))
                k.wait("vector", "cS", sg_s)
                k.op("vector", "cV", lambda e, hb=hb, sl=sl, sp=sp, pu=pu: e.scalar_tensor_tensor(
                    out=hmb[:, hb, sl], in0=psum[pu][:, 0:CTC], scalar=QS,
                    in1=sgp[:, sp, :], op0=OP.mult, op1=OP.mult))
                k.psum_set_free(pu, ("cV", k.cnt["cV"]))
            ffn_mm[m] = k.cnt["cT"]
            k.wait("gpsimd", "cV", k.cnt["cV"])
            hq = "qU" if hb == 0 else "qX"
            k.op("gpsimd", hq, lambda e, m=m, hb=hb: e.dma_start(
                out=d_h[m], in_=hmb[:, hb, :]), 16)
            hst[m] = (hq, k.cnt[hq])

        k.barrier()
        AF32.off = c2_f32_mark
        AB16.off = c2_b16_mark
        c_z2 = AB16.alloc([128, NXT, CTC])
        c_wd = AB16.alloc([128, 4, WSL1])
        htc = AB16.alloc([128, 2, NH, CTC])
        x1c = AB16.alloc([128, 2, NXT, CTC])
        d_h_t = d_h.rearrange("m p l -> p m l")
        st2 = dict(dn={-1: 0, -2: 0}, z2={-1: 0, -2: 0})
        slab2 = dict(i=0, done={})

        def slab2_load(src, W):
            i = slab2["i"]
            par4 = i % 4
            gn = f"g{par4}"
            if i >= 4:
                k.wait("gpsimd", "cT", slab2["done"][i - 4])
            dst = c_wd[:, par4, 0:W]
            if len(src.shape) == 3:
                dst = dst.rearrange("p (a c) -> p a c", a=src.shape[1])
            k.op("gpsimd", gn, lambda e, s=src, d=dst: e.dma_start(
                out=d, in_=s), 16)
            k.wait("tensor", gn, k.cnt[gn])
            return par4

        def slab2_mark():
            slab2["done"][slab2["i"]] = k.cnt["cT"]
            slab2["i"] += 1

        for tc in range(NTC2):
            lo = tc * CTC
            hb = tc % 2
            hq = "qU" if hb == 0 else "qX"
            k.wait("gpsimd", "cT", st2["dn"][tc - 2])
            k.wait("gpsimd", "cV", st2["z2"][tc - 2])
            k.op("gpsimd", hq, lambda e, lo=lo, hb=hb: e.dma_start(
                out=htc[:, hb], in_=d_h_t[:, :, lo:lo + CTC]), 16)
            k.op("gpsimd", hq, lambda e, lo=lo, hb=hb: e.dma_start(
                out=x1c[:, hb], in_=dx1_t[:, :, lo:lo + CTC]), 16)
            lq = k.cnt[hq]
            k.wait("tensor", hq, lq)
            for (mof, msz) in grps(NXT):
                for pi in range(msz):
                    k.psum_guard(pi)
                MW = msz * 128
                KG = 2 if (2 * MW <= WSL1 and NH % 2 == 0) else 1
                for kk0 in range(0, NH, KG):
                    par4 = slab2_load(
                        d_wd[kk0 * 128:(kk0 + KG) * 128,
                             mof * 128:mof * 128 + MW]
                        .rearrange("(a p) c -> p a c", p=128), KG * MW)
                    for kl in range(KG):
                        kk = kk0 + kl
                        for m in range(msz):
                            k.op("tensor", "cT", lambda e, m=m, kk=kk, kl=kl,
                                 par4=par4, hb=hb, MW=MW: e.matmul(
                                     psum[m][:, 0:CTC],
                                     c_wd[:, par4,
                                          kl * MW + m * 128:kl * MW + (m + 1) * 128],
                                     htc[:, hb, kk, :],
                                     start=(kk == 0), stop=(kk == NH - 1)))
                    slab2_mark()
                k.wait("vector", "cT", k.cnt["cT"])
                k.wait("vector", hq, lq)
                for m in range(msz):
                    k.op("vector", "cV", lambda e, m=m, mof=mof, hb=hb:
                         e.scalar_tensor_tensor(
                             out=c_z2[:, mof + m, :], in0=psum[m][:, 0:CTC],
                             scalar=QS, in1=x1c[:, hb, mof + m, :],
                             op0=OP.mult, op1=OP.add))
                    k.psum_set_free(m, ("cV", k.cnt["cV"]))
            st2["dn"][tc] = k.cnt["cT"]
            st2["z2"][tc] = k.cnt["cV"]
            layernorm(c_z2, c_z,
                      out_tile=lambda i, lo=lo: outT_t[:, i, lo:lo + CTC],
                      ln_sync=True)

    for q in ("qG", "qS", "q0", "q1", "g0", "g1", "g2", "g3", "q2", "q3", "qW", "qL", "qY", "qYs", "qU", "qX", "qA", "cx0", "cx1", "cy0", "cy1", "qi0", "qi1", "qf0", "qf1", "qM", "cy2", "cy3"):
        k.wait("gpsimd", q, k.cnt[q])
    k.play()
    k.stack.close()
    return nc


_NC_CACHE = {}


def _get_nc(key, cfg):
    if key not in _NC_CACHE:
        _NC_CACHE[key] = build_nc(cfg)
    return _NC_CACHE[key]


def make_inputs(cfg, x, W_in, conv_w, conv_b, W_x, W_dt, b_dt, A_log, Dv, W_out,
                wg, wu, wd):
    g = _dv(cfg)
    DI, NS, DR, H = (g[k] for k in ("DI", "NS", "DR", "H"))
    HDD, MYT, NH, NXT = g["HDD"], g["MYT"], g["NH"], g["NXT"]
    NDT, XD, KC = g["NDT"], g["XD"], cfg["KC"]
    C = np.ascontiguousarray
    wgT = wg.T
    wuT = wu.T
    wdT = C(wd.T)
    wg_t = C(wgT.reshape(NXT, 128, NH, 128).transpose(2, 1, 0, 3))
    wu_t = C(wuT.reshape(NXT, 128, NH, 128).transpose(2, 1, 0, 3))
    w_res_c = C(W_in[:, DI:])
    w_out_c = C(W_out)
    ins = []
    for c in range(8):
        b, h = c >> 1, c & 1
        dd = slice(h * HDD, (h + 1) * HDD)
        ins.append({
            "xT": C(x[b].T),
            "xTmine": C(x[b, h * MYT:(h + 1) * MYT, :].T),
            "w_xin": C(W_in[:, dd]),
            "w_res": w_res_c,
            "w_x": C(W_x[dd, :].reshape(NDT, 128, XD).transpose(1, 0, 2)),
            "w_dt": C(W_dt[:, dd]),
            "b_dt": C(b_dt[dd].reshape(NDT, 128).T),
            "a_log": C(A_log[dd, :].reshape(NDT, 128, NS).transpose(1, 0, 2)),
            "d_vec": C(Dv[dd].reshape(NDT, 128).T),
            "conv_w": C(conv_w[dd, :].reshape(NDT, 128, KC).transpose(1, 0, 2)),
            "conv_b": C(conv_b[dd].reshape(NDT, 128).T),
            "w_out": w_out_c,
            "m_sel": C(np.tile(np.array([[1.0 - h, float(h)]], np.float32), (128, 1))),
            "ident": np.eye(128, dtype=np.float32),
            "wg_t": wg_t,
            "wu_t": wu_t,
            "wd_t": wdT,
        })
    return ins


def run(cfg, key, x, W_in, conv_w, conv_b, W_x, W_dt, b_dt, A_log, Dv, W_out,
        wg, wu, wd, runner=None):
    g = _dv(cfg)
    nc = _get_nc(key, cfg)
    ins = make_inputs(cfg, x, W_in, conv_w, conv_b, W_x, W_dt, b_dt, A_log,
                      Dv, W_out, wg, wu, wd)
    if runner is None:
        res = run_bass_kernel_spmd(nc, ins, core_ids=list(range(8)))
        outs = [np.asarray(r["outT"]) for r in res.results]
    else:
        outs = runner(nc, ins)
    B, L, D, MYT = x.shape[0], cfg["L"], cfg["D"], g["MYT"]
    out = np.empty((B, L, D), np.float32)
    for c in range(8):
        b, h = c >> 1, c & 1
        out[b, h * MYT:(h + 1) * MYT, :] = outs[c].T
    return out


def kernel(L0_attn_x, L0_attn_W_in, L0_attn_conv_w, L0_attn_conv_b, L0_attn_W_x,
           L0_attn_W_dt, L0_attn_b_dt, L0_attn_A_log, L0_attn_D, L0_attn_W_out,
           L0_ffn_w_gate, L0_ffn_w_up, L0_ffn_w_down):
    return run(FULL_CFG, "full", L0_attn_x, L0_attn_W_in, L0_attn_conv_w,
               L0_attn_conv_b, L0_attn_W_x, L0_attn_W_dt, L0_attn_b_dt,
               L0_attn_A_log, L0_attn_D, L0_attn_W_out,
               L0_ffn_w_gate, L0_ffn_w_up, L0_ffn_w_down)
